# version 06
# baseline (speedup 1.0000x reference)
import numpy as np
import ml_dtypes
import concourse.bass as bass
import concourse.mybir as mybir
from concourse.bass_utils import run_bass_kernel_spmd
from contextlib import ExitStack

"""Builders for the mk_seq_postln_d2 megakernel (dev module; inlined into kernel.py later)."""

F32 = mybir.dt.float32
BF16 = mybir.dt.bfloat16
AF = mybir.ActivationFunctionType
ALU = mybir.AluOpType
AX = mybir.AxisListType
BF = ml_dtypes.bfloat16
ENGINES = ("tensor", "vector", "scalar", "gpsimd", "sync")


class Prog:

    def __init__(self, auto_drain=False, inc_all=False):
        self.auto_drain = auto_drain
        self.inc_all = inc_all
        self.items = []
        self.ecnt = dict.fromkeys(ENGINES, 0)
        self.dcnt = {}

    def op(self, eng, fn, inc=False, drain=False, waits=()):
        if self.inc_all:
            inc = True
        self.items.append(("op", eng, fn, inc, drain, tuple(waits)))
        if inc:
            self.ecnt[eng] += 1
        return ("e", eng, self.ecnt[eng])

    def dma(self, eng, name, fn, waits=()):
        self.dcnt[name] = self.dcnt.get(name, 0) + 16
        self.items.append(("dma", eng, fn, name, tuple(waits)))
        return ("d", name, self.dcnt[name])

    def arm_dma(self, eng, name, fn, ndma, waits=()):
        self.dcnt[name] = self.dcnt.get(name, 0) + 16 * ndma
        self.items.append(("arm", eng, fn, name, tuple(waits), ndma))
        return ("d", name, self.dcnt[name])

    def now(self, eng):
        return ("e", eng, self.ecnt[eng])

    def dnow(self, name):
        return ("d", name, self.dcnt.get(name, 0))

    def barrier(self, engs):
        snaps = [self.now(e) for e in engs]
        out = []
        for e in engs:
            ws = [s for s in snaps if s[1] != e]
            self.items.append(("waits", e, ws))
        return snaps

    def wait(self, eng, *tokens):
        self.items.append(("waits", eng, tuple(tokens)))

    def build(self, nc, stack, verbose=False):
        esem = {e: stack.enter_context(nc.semaphore(f"es_{e}")) for e in ENGINES}
        dsem = {n: stack.enter_context(nc.semaphore(f"ds_{n}")) for n in self.dcnt}
        block = stack.enter_context(nc.Block())
        per = {e: [] for e in ENGINES}
        for it in self.items:
            per[it[1]].append(it)

        def run_engine(e, eng_items):
            seen = {}

            def handle(engine):
                cnt = 0
                dc = {}
                for it in eng_items:
                    if it[0] == "waits":
                        for tk in it[2]:
                            _wait(engine, tk)
                        continue
                    if it[0] == "op":
                        _, _, fn, inc, drain, waits = it
                        for tk in waits:
                            _wait(engine, tk)
                        instr = fn(engine)
                        if drain or (self.auto_drain and e in ("vector", "scalar", "gpsimd")):
                            instr = engine.drain()
                        if inc:
                            cnt += 1
                            (instr if instr is not None else engine.nop()).then_inc(esem[e], 1)
                    elif it[0] == "arm":
                        _, _, fn, name, waits, ndma = it
                        for tk in waits:
                            _wait(engine, tk)
                        for arm in engine.switch_core_id(n=128):
                            cid = arm.logical % 8
                            fn(engine, cid, dsem[name])
                    else:
                        _, _, fn, name, waits = it
                        for tk in waits:
                            _wait(engine, tk)
                        d = fn(engine)
                        d.then_inc(dsem[name], 16)

            def _wait(engine, tk):
                kind, key, val = tk
                if val <= 0:
                    return
                if kind == "e" and key == e:
                    return
                prev = seen.get((kind, key), -1)
                if val <= prev:
                    return
                seen[(kind, key)] = val
                sem = esem[key] if kind == "e" else dsem[key]
                engine.wait_ge(sem, val)

            return handle

        for e in ENGINES:
            getattr(block, e if e != "sync" else "sync")(run_engine(e, per[e]))
        return esem, dsem


def ceil_div(a, b):
    return (a + b - 1) // b


def build_mamba_fused(nc, p, stack, tens, psum):
    L, DM, DI, NS, DTR, Tc = 4096, 1024, 2048, 16, 64, 512
    NPOS = 9
    DTt = DI // 128
    KT = DM // 128
    MT = Tc // 128
    NP = DTR + 2 * NS

    sb = lambda n, s, d=F32: stack.enter_context(nc.sbuf_tensor(n, s, d))
    dram = lambda n, s, d=BF16: nc.dram_tensor(n, s, d, kind="Internal").ap()
    bank = lambda b, par=128: psum[0:par, b * 512:(b + 1) * 512]

    xbf = dram("xbf", [NPOS, 4, Tc, DM])
    inp_bf = {s: dram(f"inp_bf_{s}", [2 * DI, DM]) for s in "fr"}
    outp_bf = {s: dram(f"outp_bf_{s}", [DM, DI]) for s in "fr"}
    xp_bf = {s: dram(f"xp_bf_{s}", [128, DI]) for s in "fr"}
    dtw_bf = {s: dram(f"dtw_bf_{s}", [DI, 128]) for s in "fr"}
    inpT = {s: dram(f"inpT_{s}", [128, KT, 2 * DI]) for s in "fr"}
    bfb = {s: dram(f"bfb_{s}", [128, 4096]) for s in "fr"}
    outT = {s: dram(f"outT_{s}", [DI, DM]) for s in "fr"}
    sched_x = dram("sched_x", [7, 128, KT, DI])
    sched_bfb = dram("sched_bfb", [7, 128, 4096])
    sched_smF = nc.dram_tensor("sched_smF", [7, 128, 368], F32, kind="Internal").ap()
    scr = dram("scr", [2, 4, Tc, DM])

    slab_q = sb("slab_q", [128, 2, KT, 512], BF16)
    smallsB = sb("smallsB", [128, 4096], BF16)
    smallsF_sb = sb("smallsF_sb", [128, 368])
    A_sb = sb("A_sb", [128, DTt, NS])
    sel_sb = sb("sel_sb", [NS, NS * 128], BF16)
    masks_sb = sb("masks_sb", [128, 35])
    x0cT = sb("x0cT", [128, KT, Tc], BF16)
    conv_in = sb("conv_in", [128, 4, Tc + 3], BF16)
    halo = sb("halo", [128, DTt, 3])
    hF = sb("hF", [128, DTt, 3])
    hR = sb("hR", [128, DTt, 3])
    x_silu = sb("x_silu", [128, DTt, Tc], BF16)
    delta = sb("delta", [128, DTt, Tc], BF16)
    u_bf = sb("u_bf", [128, DTt, Tc], BF16)
    y_acc = sb("y_acc", [128, DTt, Tc])
    carry = sb("carry", [128, DTt * NS])
    cF = sb("cF", [128, DTt * NS])
    cR = sb("cR", [128, DTt * NS])
    dtr_bf = sb("dtr_bf", [DTR, Tc], BF16)
    Btile = sb("Btile", [NS, Tc], BF16)
    Ctile = sb("Ctile", [NS, Tc], BF16)
    dA_b = sb("dA_b", [128, 4, Tc])
    dBu_b = sb("dBu_b", [128, 8, Tc], BF16)
    h_b = sb("h_b", [128, DTt, Tc], BF16)
    m_b = sb("m_b", [128, 4, Tc], BF16)
    Bb_sb = sb("Bb_sb", [128, 2, Tc], BF16)
    A_pos = sb("A_pos", [128, DTt, NS])
    biasb = sb("biasb", [128, DTt, NS])
    P3 = sb("P3", [128, DTt, NS])
    hloc = sb("hloc", [128, DTt, NS])
    ones512 = sb("ones512", [128, Tc], BF16)
    zc = sb("zc", [128, 1])
    dce_sb = sb("dce_sb", [128, DTt])
    Bb16 = h_b
    z_bf = m_b
    tmp_c = dBu_b
    yq = delta
    attn_sb = sb("attn_sb", [128, DM], BF16)
    sp_t = dA_b
    sp_r = sb("sp_r", [128, 4, Tc], BF16)
    outT_t = sp_r.rearrange("p (a b) c -> p a (b c)", a=2)
    stage_t = slab_q[:, 0].rearrange("p a b -> p (a b)")
    halo_fl = halo.rearrange("p a b -> p (a b)")
    hF_fl = hF.rearrange("p a b -> p (a b)")
    hR_fl = hR.rearrange("p a b -> p (a b)")
    x_projT = smallsB[:, 0:2048].rearrange("p (a b) -> p a b", b=128)
    dt_wT = smallsB[:, 2048:4096]
    dtb_sb = smallsF_sb[:, 256:272]
    convb_sb = smallsF_sb[:, 272:288]
    D_sb = smallsF_sb[:, 288:304]
    convw_sb = smallsF_sb[:, 304:368].rearrange("p (a b) -> p a b", b=4)
    Alog_v = smallsF_sb[:, 0:256].rearrange("p (a b) -> p a b", b=NS)

    T, V, S, G, Y = "tensor", "vector", "scalar", "gpsimd", "sync"

    p.dma(G, "cast", lambda e: e.dma_start(out=xbf, in_=tens["x0sched"]))
    for s in "fr":
        p.dma(G, "cast", lambda e, s=s: e.dma_start(
            out=inp_bf[s], in_=tens[f"in_proj_{s}"]))
        p.dma(G, "cast", lambda e, s=s: e.dma_start(
            out=outp_bf[s], in_=tens[f"out_proj_{s}"]))
        p.dma(G, "cast", lambda e, s=s: e.dma_start(
            out=xp_bf[s][0:NP, :], in_=tens[f"x_proj_{s}"]))
        p.dma(G, "cast", lambda e, s=s: e.dma_start(
            out=dtw_bf[s][:, 0:DTR], in_=tens[f"dt_w_{s}"]))
    cast_done = p.dnow("cast")
    p.dma(Y, "ld", lambda e: e.dma_start(out=sel_sb[:], in_=tens["sel16"]))
    p.dma(Y, "ld", lambda e: e.dma_start(out=masks_sb[:], in_=tens["masks"]))
    ld_done = p.dnow("ld")
    for s in "fr":
        for k in range(KT):
            tpt = p.dma(Y, "tp", lambda e, s=s, k=k: e.dma_start(
                out=stage_t[:], in_=inp_bf[s][:, k * 128:(k + 1) * 128],
                transpose=True), waits=(cast_done, p.dnow("stg")))
            p.dma(G, "stg", lambda e, s=s, k=k: e.dma_start(
                out=inpT[s][:, k, :], in_=stage_t[:]), waits=(tpt,))
        for dt in range(DTt):
            tpt = p.dma(Y, "tp", lambda e, s=s, dt=dt: e.dma_start(
                out=stage_t[:, 0:128], in_=xp_bf[s][:, dt * 128:(dt + 1) * 128],
                transpose=True), waits=(cast_done, p.dnow("stg")))
            p.dma(G, "stg", lambda e, s=s, dt=dt: e.dma_start(
                out=bfb[s][:, dt * 128:(dt + 1) * 128], in_=stage_t[:, 0:128]),
                waits=(tpt,))
        tpt = p.dma(Y, "tp", lambda e, s=s: e.dma_start(
            out=stage_t[:, 0:2048], in_=dtw_bf[s], transpose=True),
            waits=(cast_done, p.dnow("stg")))
        p.dma(G, "stg", lambda e, s=s: e.dma_start(
            out=bfb[s][:, 2048:4096], in_=stage_t[:, 0:2048]), waits=(tpt,))
        for kt in range(DTt):
            tpt = p.dma(Y, "tp", lambda e, s=s, kt=kt: e.dma_start(
                out=stage_t[:, 0:DM], in_=outp_bf[s][:, kt * 128:(kt + 1) * 128],
                transpose=True), waits=(cast_done, p.dnow("stg")))
            p.dma(G, "stg", lambda e, s=s, kt=kt: e.dma_start(
                out=outT[s][kt * 128:(kt + 1) * 128, :], in_=stage_t[:, 0:DM]),
                waits=(tpt,))
    prep_done = (p.dnow("stg"), p.dnow("tp"), cast_done)
    for pos in range(7):
        def arm_fn(e, cid, sem, pos=pos):
            s = "f" if pos < cid else "r"
            e.dma_start(out=sched_x[pos], in_=inpT[s][:, :, 0:DI]).then_inc(sem, 16)
            e.dma_start(out=sched_bfb[pos], in_=bfb[s]).then_inc(sem, 16)
            return e.dma_start(out=sched_smF[pos],
                               in_=tens[f"smallsF_{s}"]).then_inc(sem, 16)
        p.arm_dma(Y, "sch", arm_fn, 3, waits=prep_done if pos == 0 else ())
    sch_done = p.dnow("sch")
    p.op(V, lambda e: e.memset(ones512[:], 1.0), inc=True)
    p.op(V, lambda e: e.memset(zc[:], 0.0), inc=True, drain=True)
    p.wait(T, ld_done, p.now(V), p.now(S))
    p.wait(V, ld_done)
    p.wait(S, ld_done)

    XB = (0, 1, 5, 6)
    for b in range(4):
      zs0 = p.op(V, lambda e: e.memset(carry[:], 0.0), inc=True)
      p.op(V, lambda e: e.memset(cF[:], 0.0), inc=True)
      p.op(V, lambda e: e.memset(cR[:], 0.0), inc=True)
      p.op(V, lambda e: e.memset(halo[:], 0.0), inc=True)
      p.op(V, lambda e: e.memset(hF[:], 0.0), inc=True)
      p.op(V, lambda e: e.memset(hR[:], 0.0), inc=True, drain=True)
      for pos in range(NPOS):
        own = pos >= 7
        dirx = 0 if pos == 7 else 1
        if pos < 7:
            smB_src, smF_src = sched_bfb[pos], sched_smF[pos]
        elif pos == 7:
            smB_src, smF_src = bfb["f"], tens["smallsF_f"]
        else:
            smB_src, smF_src = bfb["r"], tens["smallsF_r"]
        lw = (sch_done, p.now(T), p.now(V), p.now(S))
        p.dma(Y, "sml", lambda e, smB_src=smB_src: e.dma_start(
            out=smallsB[:], in_=smB_src), waits=lw)
        p.dma(Y, "sml", lambda e, smF_src=smF_src: e.dma_start(
            out=smallsF_sb[:], in_=smF_src))
        sml_ok = p.dnow("sml")
        e1 = p.op(S, lambda e: e.activation(A_pos[:], Alog_v, AF.Exp), inc=True,
                  waits=(sml_ok,))
        ae = p.op(V, lambda e: e.tensor_scalar(A_sb[:], A_pos[:], -1.0, None,
                                               ALU.mult), inc=True, waits=(e1,),
                  drain=True)
        if pos < 7:
            km = masks_sb[:, pos:pos + 1]
            p.op(V, lambda e, km=km: e.tensor_scalar(carry[:], carry[:], km, None,
                                                     ALU.mult), inc=True, drain=True)
            p.op(V, lambda e, km=km: e.tensor_scalar(halo_fl[:], halo_fl[:], km,
                                                     None, ALU.mult), inc=True,
                 drain=True)
        elif pos == 7:
            p.op(V, lambda e: e.tensor_copy(carry[:], cF[:]), inc=True, drain=True)
            p.op(V, lambda e: e.tensor_copy(halo[:], hF[:]), inc=True, drain=True)
        else:
            p.op(V, lambda e: e.tensor_copy(carry[:], cR[:]), inc=True, drain=True)
            p.op(V, lambda e: e.tensor_copy(halo[:], hR[:]), inc=True, drain=True)
        pro_v = p.now(V)

        xw = []
        for k in range(KT):
            xw.append(p.dma(Y, "x0c", lambda e, k=k, pos=pos, b=b: e.dma_start(
                out=x0cT[:, k, :], in_=xbf[pos, b, :, k * 128:(k + 1) * 128],
                transpose=True), waits=(cast_done,) if k == 0 else ()))
        x0c_ok = xw[-1]

        silu_snap = {}
        slab_war = {0: p.now(T), 1: p.now(T)}
        xslab_src = sched_x[pos] if pos < 7 else inpT["f" if pos == 7 else "r"]
        for g in range(0, DTt, 4):
            quad = (g, g + 1, g + 2, g + 3)
            buf = (g // 4) % 2
            sl = p.dma(Y, "slb", lambda e, buf=buf, g=g, xslab_src=xslab_src:
                       e.dma_start(out=slab_q[:, buf],
                                   in_=xslab_src[:, :, g * 128:g * 128 + 512]),
                       waits=(slab_war[buf], sch_done))
            mm_snap = {}
            for dt in quad:
                def mm_xz(e, dt=dt, buf=buf):
                    ins = None
                    for k in range(KT):
                        ins = e.matmul(bank(XB[dt % 4]),
                                       slab_q[:, buf, k, (dt % 4) * 128:(dt % 4 + 1) * 128],
                                       x0cT[:, k, :], start=(k == 0), stop=(k == KT - 1))
                    return ins
                mm_snap[dt] = p.op(T, mm_xz, inc=True,
                                   waits=(x0c_ok, sl, pro_v,
                                          silu_snap.get(g - 4, ("e", S, 0))))
            slab_war[buf] = mm_snap[quad[-1]]
            ev_snap = {}
            for dt in quad:
                ev_snap[dt] = p.op(S, lambda e, dt=dt: e.copy(
                    out=conv_in[:, dt % 4, 3:], in_=bank(XB[dt % 4])),
                    inc=True, waits=(mm_snap[dt],))
            for dt in quad:
                p.op(V, lambda e, dt=dt: e.tensor_copy(conv_in[:, dt % 4, 0:3], halo[:, dt, :]),
                     waits=(ev_snap[dt], silu_snap.get(g - 4, ("e", S, 0))))
            for j in range(4):
                for dt in quad:
                    if j == 0:
                        p.op(V, lambda e, dt=dt: e.tensor_scalar(
                            sp_t[:, dt % 4, :], conv_in[:, dt % 4, 0:Tc],
                            convw_sb[:, dt, 0:1], None, ALU.mult))
                    else:
                        p.op(V, lambda e, dt=dt, j=j: e.scalar_tensor_tensor(
                            sp_t[:, dt % 4, :], conv_in[:, dt % 4, j:Tc + j],
                            convw_sb[:, dt, j:j + 1], sp_t[:, dt % 4, :], ALU.mult, ALU.add))
            conv_v = {}
            for dt in quad:
                conv_v[dt] = p.op(V, lambda e, dt=dt: e.tensor_copy(
                    halo[:, dt, :], conv_in[:, dt % 4, Tc:Tc + 3]), inc=True)
            for dt in quad:
                silu_snap[g] = p.op(S, lambda e, dt=dt: e.activation(
                    x_silu[:, dt, :], sp_t[:, dt % 4, :], AF.Silu, bias=convb_sb[:, dt:dt + 1]),
                    inc=True, waits=(conv_v[dt],))
            for dt in quad:
                def mm_xdbl(e, dt=dt):
                    st = (dt == 0)
                    sp = (dt == DTt - 1)
                    e.matmul(bank(2, DTR), x_projT[:, dt, 0:DTR], x_silu[:, dt, :],
                             start=st, stop=sp)
                    e.matmul(bank(3, NS), x_projT[:, dt, DTR:DTR + NS], x_silu[:, dt, :],
                             start=st, stop=sp)
                    return e.matmul(bank(4, NS), x_projT[:, dt, DTR + NS:DTR + 2 * NS],
                                    x_silu[:, dt, :], start=st, stop=sp)
                p.op(T, mm_xdbl, inc=True, waits=(silu_snap[g],))
        xdbl_T = p.now(T)
        ev_dtr = p.op(S, lambda e: e.copy(out=dtr_bf[:], in_=bank(2, DTR)), inc=True,
                      waits=(xdbl_T,))
        ev_B = p.op(S, lambda e: e.copy(out=Btile[:], in_=bank(3, NS)), inc=True)
        ev_C = p.op(S, lambda e: e.copy(out=Ctile[:], in_=bank(4, NS)), inc=True)

        dmm = {}
        dv_hist = {}
        for g in range(0, DTt, 4):
            quad = (g, g + 1, g + 2, g + 3)
            for dt in quad:
                dmm[dt] = p.op(T, lambda e, dt=dt: e.matmul(
                    bank(XB[dt % 4]), dt_wT[0:DTR, dt * 128:(dt + 1) * 128], dtr_bf[:],
                    start=True, stop=True), inc=True,
                    waits=(ev_dtr, p.now(S)))
            sp_done = {}
            for dt in quad:
                p.op(S, lambda e, dt=dt: e.activation(
                    sp_t[:, dt % 4, 0:Tc], bank(XB[dt % 4]), AF.Abs, bias=dtb_sb[:, dt:dt + 1]),
                    waits=(dmm[dt], dv_hist.get(g - 4, ("e", V, 0))))
            for dt in quad:
                p.op(S, lambda e, dt=dt: e.activation(
                    sp_r[:, dt % 4, 0:Tc], bank(XB[dt % 4]),
                    AF.Relu, bias=dtb_sb[:, dt:dt + 1]))
            for dt in quad:
                p.op(S, lambda e, dt=dt: e.activation(
                    sp_t[:, dt % 4, 0:Tc], sp_t[:, dt % 4, 0:Tc], AF.Exp, scale=-1.0))
            for dt in quad:
                sp_done[dt] = p.op(S, lambda e, dt=dt: e.activation(
                    sp_t[:, dt % 4, 0:Tc], sp_t[:, dt % 4, 0:Tc], AF.Ln, bias=1.0), inc=True)
            dv = {}
            for dt in quad:
                dv[dt] = p.op(V, lambda e, dt=dt: e.tensor_tensor(
                    delta[:, dt, :], sp_r[:, dt % 4, 0:Tc],
                    sp_t[:, dt % 4, 0:Tc], ALU.add),
                    inc=True, waits=(sp_done[dt],))
            for dt in quad:
                p.op(V, lambda e, dt=dt: e.tensor_tensor(
                    u_bf[:, dt, :], delta[:, dt, :], x_silu[:, dt, :], ALU.mult), inc=True)
            dv_hist[g] = p.now(V)
        if own:
            yz = p.op(V, lambda e: e.memset(y_acc[:], 0.0), inc=True)
        p.barrier((T, V, S, G))

        carry3 = carry.rearrange("p (a b) -> p a b", b=NS)
        if own:
            scan_v_snap = {}
            dbu_tok = {}
            ebB_hist = {}
            n_last_v = {}
            pend = []

            def emit_carry(nq):
                cn, cg = nq
                p.op(V, lambda e, cn=cn, cg=cg: e.tensor_copy(
                    carry3[:, cg:cg + 4, cn:cn + 1],
                    h_b[:, cg:cg + 4, Tc - 1:Tc]), inc=True)

            for n in range(NS):
                bB = 2 if n % 2 == 0 else 3
                bC = 7 if n % 2 == 0 else 4
                war = n_last_v.get(n - 2, ("e", V, 0))
                war2 = ebB_hist.get(n - 2, ("e", S, 0))
                selL = sel_sb[:, n * 128:(n + 1) * 128]
                bc1 = p.op(T, lambda e, bB=bB, selL=selL: e.matmul(
                    bank(bB), selL, Btile[:], start=True, stop=True),
                    inc=True, waits=(ev_B, ev_C, war, war2))
                bc2 = p.op(T, lambda e, bC=bC, selL=selL: e.matmul(
                    bank(bC), selL, Ctile[:], start=True, stop=True), inc=True)
                dbu_war = dbu_tok.get((n - 2, DTt - 1), ("e", G, 0))
                ebB = p.op(S, lambda e, n=n, bB=bB: e.copy(
                    out=Bb_sb[:, n % 2, :], in_=bank(bB)), inc=True,
                    waits=(bc1, dbu_war))
                ebB_hist[n] = ebB
                for g in range(0, DTt, 4):
                    quad = (g, g + 1, g + 2, g + 3)
                    da = {}
                    for dt in quad:
                        prev = (scan_v_snap.get((n, dt - 4))
                                or scan_v_snap.get((n - 1, DTt - 4 + (dt % 4)))
                                or ("e", V, 0))
                        da[dt] = p.op(S, lambda e, dt=dt, n=n: e.activation(
                            dA_b[:, dt % 4, :], delta[:, dt, :], AF.Exp,
                            scale=A_sb[:, dt, n:n + 1]), inc=True, waits=(prev,))
                    for dt in quad:
                        wprev = (scan_v_snap.get((n, dt - 8))
                                 or scan_v_snap.get((n - 1, (dt + 8) % DTt))
                                 or ("e", V, 0))
                        dbu_tok[(n, dt)] = p.op(G, lambda e, dt=dt, n=n: e.tensor_tensor(
                            dBu_b[:, dt % 8, :], u_bf[:, dt, :], Bb_sb[:, n % 2, :],
                            ALU.mult), inc=True, waits=(ebB, wprev))
                    for dt in quad:
                        scan_v_snap[(n, dt)] = p.op(V, lambda e, dt=dt, n=n: e.tensor_tensor_scan(
                            h_b[:, dt, :], dA_b[:, dt % 4, :], dBu_b[:, dt % 8, :],
                            carry3[:, dt, n:n + 1], ALU.mult, ALU.add), inc=True,
                            waits=(da[dt], dbu_tok[(n, dt)]))
                    for dt in quad:
                        p.op(V, lambda e, dt=dt, bC=bC: e.tensor_tensor(
                            m_b[:, dt % 4, :], h_b[:, dt, :], bank(bC), ALU.mult),
                            waits=(bc2,) if dt == g else ())
                    if pend:
                        emit_carry(pend.pop(0))
                    pend.append((n, g))
                    for dt in quad:
                        p.op(V, lambda e, dt=dt: e.tensor_tensor(
                            y_acc[:, dt, :], y_acc[:, dt, :], m_b[:, dt % 4, :], ALU.add),
                            inc=(dt == DTt - 1))
                n_last_v[n] = p.now(V)
            for _ in range(2):
                p.op(V, lambda e: e.memset(m_b[0:1, 0, 0:1], 0.0), inc=True)
            emit_carry(pend.pop(0))
        else:
            ebB_tok = {}
            last_dbu = ("e", V, 0)
            for n in range(NS):
                bB = 2 if n % 2 == 0 else 3
                selL = sel_sb[:, n * 128:(n + 1) * 128]
                war2 = ebB_tok.get(n - 2, ("e", S, 0))
                bc1 = p.op(T, lambda e, bB=bB, selL=selL: e.matmul(
                    bank(bB), selL, Btile[:], start=True, stop=True),
                    inc=True, waits=(ev_B, ev_C, war2) if n < 2 else (war2,))
                ebB_tok[n] = p.op(S, lambda e, n=n, bB=bB: e.copy(
                    out=Bb16[:, n, :], in_=bank(bB)), inc=True, waits=(bc1,))
            ebB_all = p.now(S)
            for dt in range(DTt):
                p.op(V, lambda e, dt=dt: e.tensor_reduce(
                    dce_sb[:, dt:dt + 1], delta[:, dt, :], AX.X, ALU.add), inc=True)
            for dt in range(DTt):
                p.op(V, lambda e, dt=dt: e.tensor_scalar(
                    biasb[:, dt, :], A_sb[:, dt, :], dce_sb[:, dt:dt + 1], None,
                    ALU.mult), inc=True)
            bias_done = p.op(S, lambda e: e.activation(
                P3[:], biasb[:], AF.Exp), inc=True, waits=(p.now(V),))
            cum_tok = {}
            wS_last = {}
            wu_last = {}
            for dt in range(DTt):
                dcv = dA_b[:, dt % 2, :]
                cum_tok[dt] = p.op(V, lambda e, dt=dt, dcv=dcv: e.tensor_tensor_scan(
                    dcv, ones512[:], delta[:, dt, :], zc[:], ALU.mult, ALU.add),
                    inc=True, waits=(wS_last.get(dt - 2, ("e", S, 0)),))
                for n4 in range(0, NS, 4):
                    wq = {}
                    for n in range(n4, n4 + 4):
                        ws = ()
                        if n == n4:
                            ws = (cum_tok[dt], bias_done,
                                  wu_last.get((dt, n4 - 4))
                                  or wu_last.get((dt - 1, NS - 4))
                                  or ("e", V, 0))
                        wq[n] = p.op(S, lambda e, dt=dt, n=n, dcv=dcv: e.activation(
                            sp_r[:, n % 4, :], dcv, AF.Exp,
                            scale=A_pos[:, dt, n:n + 1],
                            bias=biasb[:, dt, n:n + 1]), inc=True, waits=ws)
                    for n in range(n4, n4 + 4):
                        tok = p.op(V, lambda e, dt=dt, n=n: e.tensor_tensor(
                            m_b[:, n % 4, :], sp_r[:, n % 4, :], u_bf[:, dt, :],
                            ALU.mult), inc=True, waits=(wq[n], ebB_all))
                        if n == n4 + 3:
                            wu_last[(dt, n4)] = tok
                    for n in range(n4, n4 + 4):
                        p.op(V, lambda e, dt=dt, n=n: e.tensor_tensor(
                            m_b[:, n % 4, :], m_b[:, n % 4, :], Bb16[:, n, :],
                            ALU.mult), inc=True)
                    for n in range(n4, n4 + 4):
                        p.op(V, lambda e, dt=dt, n=n: e.tensor_reduce(
                            hloc[:, dt, n:n + 1], m_b[:, n % 4, :], AX.X, ALU.add),
                            inc=True)
                    wS_last[dt] = wq[n4 + 3]
            for dt in range(DTt):
                p.op(V, lambda e, dt=dt: e.tensor_tensor(
                    carry3[:, dt, :], carry3[:, dt, :], P3[:, dt, :], ALU.mult),
                    inc=True, waits=(bias_done,) if dt == 0 else ())
            for dt in range(DTt):
                p.op(V, lambda e, dt=dt: e.tensor_tensor(
                    carry3[:, dt, :], carry3[:, dt, :], hloc[:, dt, :], ALU.add),
                    inc=True)
        p.barrier((T, V, S, G))

        if own:
            zset = inpT["f" if pos == 7 else "r"]
            z_war = {0: p.now(T), 1: p.now(T)}
            for g in range(0, DTt, 4):
                quad = (g, g + 1, g + 2, g + 3)
                buf = (g // 4) % 2
                zl = p.dma(Y, "slb", lambda e, buf=buf, g=g, zset=zset: e.dma_start(
                    out=slab_q[:, buf],
                    in_=zset[:, :, DI + g * 128:DI + g * 128 + 512]),
                    waits=(z_war[buf],))
                zmm = {}
                for dt in quad:
                    def mm_z(e, dt=dt, buf=buf):
                        ins = None
                        for k in range(KT):
                            ins = e.matmul(bank(XB[dt % 4]),
                                           slab_q[:, buf, k, (dt % 4) * 128:(dt % 4 + 1) * 128],
                                           x0cT[:, k, :], start=(k == 0), stop=(k == KT - 1))
                        return ins
                    zmm[dt] = p.op(T, mm_z, inc=True, waits=(zl, p.now(V), p.now(S)))
                z_war[buf] = zmm[quad[-1]]
                zs = {}
                for dt in quad:
                    zs[dt] = p.op(S, lambda e, dt=dt: e.activation(
                        z_bf[:, dt % 4, :], bank(XB[dt % 4]), AF.Silu), inc=True, waits=(zmm[dt],))
                for dt in quad:
                    p.op(V, lambda e, dt=dt: e.tensor_scalar(
                        tmp_c[:, dt % 4, :], x_silu[:, dt, :], D_sb[:, dt:dt + 1], None, ALU.mult),
                        waits=(zs[dt],))
                for dt in quad:
                    p.op(V, lambda e, dt=dt: e.tensor_tensor(
                        tmp_c[:, dt % 4, :], y_acc[:, dt, :], tmp_c[:, dt % 4, :], ALU.add))
                for dt in quad:
                    p.op(V, lambda e, dt=dt: e.tensor_tensor(
                        yq[:, dt, :], tmp_c[:, dt % 4, :], z_bf[:, dt % 4, :], ALU.mult), inc=True)
            p.barrier((T, V, S))

            outT_src = outT["f" if pos == 7 else "r"]
            dmm2 = {}
            for dt in range(DTt):
                lw = (dmm2[dt - 2],) if dt - 2 in dmm2 else (p.now(V),)
                otw = p.dma(Y, f"outT{dt % 2}", lambda e, dt=dt, outT_src=outT_src: e.dma_start(
                    out=outT_t[:, dt % 2, :], in_=outT_src[dt * 128:(dt + 1) * 128, :]),
                    waits=lw)

                def mm_out(e, dt=dt):
                    st = (dt == 0)
                    sp = (dt == DTt - 1)
                    ins = None
                    for m in range(MT):
                        for hh in range(2):
                            ins = e.matmul(bank(m * 2 + hh), yq[:, dt, m * 128:(m + 1) * 128],
                                           outT_t[:, dt % 2, hh * 512:(hh + 1) * 512],
                                           start=st, stop=sp)
                    return ins
                dmm2[dt] = p.op(T, mm_out, inc=True, waits=(otw,))
            out_T = p.now(T)
            for m in range(MT):
                evm = p.op(S, lambda e, m=m: e.copy(out=attn_sb[:], in_=psum[:, m * 1024:(m + 1) * 1024]),
                           inc=True, waits=(out_T, p.dnow("st")))
                p.dma(G, "st", lambda e, m=m, dirx=dirx, b=b: e.dma_start(
                    out=scr[dirx, b, m * 128:(m + 1) * 128, :], in_=attn_sb[:]),
                    waits=(evm,))
        else:
            sfm = masks_sb[:, 7 + pos:8 + pos]
            nsf = masks_sb[:, 14 + pos:15 + pos]
            srm = masks_sb[:, 21 + pos:22 + pos]
            nsr = masks_sb[:, 28 + pos:29 + pos]
            p.op(V, lambda e: e.tensor_scalar(cF[:], cF[:], nsf, None, ALU.mult),
                 inc=True, drain=True)
            p.op(V, lambda e: e.scalar_tensor_tensor(cF[:], carry[:], sfm, cF[:],
                                                     ALU.mult, ALU.add), inc=True,
                 drain=True)
            p.op(V, lambda e: e.tensor_scalar(cR[:], cR[:], nsr, None, ALU.mult),
                 inc=True, drain=True)
            p.op(V, lambda e: e.scalar_tensor_tensor(cR[:], carry[:], srm, cR[:],
                                                     ALU.mult, ALU.add), inc=True,
                 drain=True)
            p.op(V, lambda e: e.tensor_scalar(hF_fl[:], hF_fl[:], nsf, None, ALU.mult),
                 inc=True, drain=True)
            p.op(V, lambda e: e.scalar_tensor_tensor(hF_fl[:], halo_fl[:], sfm, hF_fl[:],
                                                     ALU.mult, ALU.add), inc=True,
                 drain=True)
            p.op(V, lambda e: e.tensor_scalar(hR_fl[:], hR_fl[:], nsr, None, ALU.mult),
                 inc=True, drain=True)
            p.op(V, lambda e: e.scalar_tensor_tensor(hR_fl[:], halo_fl[:], srm, hR_fl[:],
                                                     ALU.mult, ALU.add), inc=True,
                 drain=True)
        p.barrier((T, V, S, G, Y))
    p.wait(G, p.dnow("st"))
    p.op(G, lambda e: e.nop())
    return scr



"""Phase-B builders: x1=LN(x0+pf+pr); L0_ffn; L1_attn (GLA block); L1_ffn; final LN.

Token layout per core: 2048 tokens = 4 batches x 512 seq positions, b-major
(row = b*512 + l_local). 16 token-tiles of 128. GLA "time" = batch axis (4
steps); sub s (s=0..3) covers tiles {s, 4+s, 8+s, 12+s} = same l-range, b=0..3.
Canonical activation layout: (token-part 128, feature-free) f32 residual.
Matmul inputs are transposed on demand via TensorE into xT (feature-part,
token-free) bf16.

PSUM rule: one accumulation group per bank (first_mm clears the whole bank).
Same-engine dependent V/S ops carry drain=True (v1; modest op counts here).
"""

F32 = mybir.dt.float32
BF16 = mybir.dt.bfloat16
AF = mybir.ActivationFunctionType
ALU = mybir.AluOpType
AX = mybir.AxisListType
BF = ml_dtypes.bfloat16

T, V, S, G, Y = "tensor", "vector", "scalar", "gpsimd", "sync"

D = 1024
TT = 16
KT = 8
N = 2048
NH = 16
DK = 64
DHID = 4096
E0, R0 = 6, 16
E1, F1, FS = 8, 512, 1024
GLA_TEMP = 16.0
DBG_SUB = 0


class Ctx:
    pass


def alloc_common(nc, p, stack, tens, psum):
    c = Ctx()
    c.nc, c.p, c.stack, c.tens = nc, p, stack, tens
    sb = lambda n, s, d=F32: stack.enter_context(nc.sbuf_tensor(n, s, d))
    dram = lambda n, s, d=BF16: nc.dram_tensor(n, s, d, kind="Internal").ap()
    c.dram = dram

    c.resid = sb("resid", [128, TT, D])
    c.dacc = sb("dacc", [128, TT, D], BF16)
    c.xT = sb("xT", [128, KT, N], BF16)
    c.dump = sb("dump", [128, 4, D])
    c.cols = sb("cols", [128, 64])
    c.arena = sb("arena", [128, 24, 512], BF16)
    c.qkv = c.arena[:, 0:12, :]
    c.qb = c.qkv[:, 0:4, :]
    c.kb = c.qkv[:, 4:8, :]
    c.vb = c.qkv[:, 8:12, :]
    c.ogT = c.arena[:, 0:8, :]
    c.og = c.arena[:, 12:20, :].rearrange("p (a b) c -> p a (b c)", b=2)
    c.slab_g = c.arena[:, 0:8, :]
    c.slab_u = c.arena[:, 8:16, :]
    c.slab_d = c.arena[:, 16:24, :].rearrange("p (a b) c -> p a (b c)", b=2)
    c.Pb = sb("Pb", [128, 4, 512], BF16)
    c.Pib = sb("Pib", [128, 4, 512], BF16)
    c.oac = None
    c.sc = sb("sc", [128, 12, 8])
    c.u1 = sb("u1", [128, 4, 512], BF16)
    c.gh_f = c.u1[:, 0:2, :]
    c.uh_f = c.u1[:, 2:4, :]
    c.ghT = c.u1[0:16].rearrange("p a b -> p (a b)")
    c.hT = sb("hT", [128, 512], BF16)
    c.hTs = sb("hTs", [16, 128], BF16)
    c.hTr = sb("hTr", [96, 128], BF16)
    c.w8 = sb("w8", [128, TT, 8])
    c.wbufA = sb("wbufA", [128, 2, 1024], BF16)
    c.wg1T = sb("wg1T", [128, KT, 16], BF16)
    c.wg2T = sb("wg2T", [128, D], BF16)
    c.sharedAT = sb("sharedAT", [128, KT, 16], BF16)
    c.sharedBT = sb("sharedBT", [128, D], BF16)
    c.routedAT = sb("routedAT", [128, KT, E0 * R0], BF16)
    c.routedBcat = sb("routedBcat", [96, D], BF16)
    c.routerWT = sb("routerWT", [128, KT, 16], BF16)
    c.router1 = sb("router1", [128, KT, 8], BF16)
    c.bg1c = sb("bg1c", [16, 1])
    c.rbc = sb("rbc", [128, 8])
    c.bcast = {}
    for nm in ("base_b", "bg2", "bog", "scale1", "scale2"):
        c.bcast[nm] = sb(f"bc_{nm}", [128, D], BF16)
    c.ones1 = sb("ones1", [1, 128])
    c.rowbuf = None
    c.identF = sb("identF_sb", [128, 128])
    c.epsc = sb("epsc", [128, 1])
    c.identB = sb("identB_sb", [128, 128], BF16)
    c.identJ = sb("identJ_sb", [128, 128], BF16)
    c.psum = psum
    c.bank = lambda b, par=128: c.psum[0:par, b * 512:(b + 1) * 512]
    _psB = c.psum.bitcast(BF16)
    c.bankB = lambda b, par=128: _psB[0:par, b * 1024:(b + 1) * 1024]
    c.nstat = 0
    c.rowbuf = c.dump[0:1, 0, :]
    c.oac = c.dump[:, 0:2, :].rearrange("p a (b f) -> p (a b) f", f=512)

    p.dma(Y, "wld", lambda e: e.dma_start(out=c.identF[:], in_=tens["identF"]))
    p.dma(Y, "wld", lambda e: e.dma_start(out=c.identB[:], in_=tens["identB"]))
    p.dma(Y, "wld", lambda e: e.dma_start(out=c.identJ[:], in_=tens["identJ"]))

    c.wdram = {}

    def cast(nm, src, shape):
        dd = dram(f"cst_{nm}", shape)
        p.dma(G, "cast", lambda e, dd=dd, src=src: e.dma_start(out=dd, in_=src))
        c.wdram[nm] = dd
        return dd

    c.trans_specs = []

    def prep_transposed(nm, src, rows, cols):
        raw = cast(nm + "_raw", src, [rows, cols])
        out = dram(nm + "T", [cols, rows])
        c.trans_specs.append((nm, raw, out, rows, cols))
        return out

    tn = tens
    c.baseWT = prep_transposed("baseW", tn["L0_ffn_base_W"], D, D)
    c.wqT = prep_transposed("wq", tn["L1_attn_wq"], D, D)
    c.wkT = prep_transposed("wk", tn["L1_attn_wk"], D, D)
    c.wvT = prep_transposed("wv", tn["L1_attn_wv"], D, D)
    c.wogT = prep_transposed("wog", tn["L1_attn_wog"], D, D)
    c.woT = prep_transposed("wo", tn["L1_attn_wo"], D, D)
    c.wgateT = prep_transposed("wgate", tn["L1_attn_w_gate"], DHID, D)
    c.wupT = prep_transposed("wup", tn["L1_attn_w_up"], DHID, D)
    c.wdownT = prep_transposed("wdown", tn["L1_attn_w_down"], D, DHID)
    c.f1gate = cast("f1gate", tn["L1_ffn_w_gate"], [E1, D, F1])
    c.f1up = cast("f1up", tn["L1_ffn_w_up"], [E1, D, F1])
    c.f1down = cast("f1down", tn["L1_ffn_w_down"], [E1, F1, D])
    c.wsgate = cast("wsgate", tn["L1_ffn_ws_gate"], [D, FS])
    c.wsup = cast("wsup", tn["L1_ffn_ws_up"], [D, FS])
    c.wsdown = cast("wsdown", tn["L1_ffn_ws_down"], [FS, D])
    c.sA_raw = cast("sA_raw", tn["L0_ffn_shared_A"], [R0, D])
    c.wg1_raw = cast("wg1_raw", tn["L1_attn_wg1"], [16, D])
    c.rW_raw = dram("rW_raw", [16, D])
    p.dma(G, "cast", lambda e: e.dma_start(out=c.rW_raw[0:E0, :],
                                           in_=tn["L0_ffn_router_W"]))
    c.wg2_pad = dram("wg2_pad", [D, 128])
    p.dma(G, "cast", lambda e: e.dma_start(out=c.wg2_pad[:, 0:16],
                                           in_=tn["L1_attn_wg2"]))
    c.sB_pad = dram("sB_pad", [D, 128])
    p.dma(G, "cast", lambda e: e.dma_start(out=c.sB_pad[:, 0:R0],
                                           in_=tn["L0_ffn_shared_B"]))
    c.rA_raw = []
    c.rB_pad = []
    for ei in range(E0):
        ra = dram(f"rA_raw{ei}", [R0, D])
        p.dma(G, "cast", lambda e, ra=ra, ei=ei: e.dma_start(
            out=ra, in_=tn["L0_ffn_routed_A"][ei]))
        c.rA_raw.append(ra)
        rb = dram(f"rB_pad{ei}", [D, 128])
        p.dma(G, "cast", lambda e, rb=rb, ei=ei: e.dma_start(
            out=rb[:, 0:R0], in_=tn["L0_ffn_routed_B"][ei]))
        c.rB_pad.append(rb)
    c.r1_raw = cast("r1_raw", tn["L1_ffn_router_w"], [D, 8])
    cast_done = p.dnow("cast")

    stg = c.wbufA[:, 0, :]
    for nm, raw, out, rows, cols in c.trans_specs:
        for kk in range(cols // 128):
            for r0 in range(0, rows, 1024):
                rc = min(1024, rows - r0)
                tpt = p.dma(Y, "tps", lambda e, raw=raw, kk=kk, r0=r0, rc=rc: e.dma_start(
                    out=stg[:, 0:rc], in_=raw[r0:r0 + rc, kk * 128:(kk + 1) * 128],
                    transpose=True), waits=(cast_done, p.dnow("stg")))
                p.dma(G, "stg", lambda e, out=out, kk=kk, r0=r0, rc=rc: e.dma_start(
                    out=out[kk * 128:(kk + 1) * 128, r0:r0 + rc], in_=stg[:, 0:rc]),
                    waits=(tpt,))
    for kk in range(KT):
        p.dma(Y, "tp", lambda e, kk=kk: e.dma_start(
            out=c.sharedAT[:, kk, :], in_=c.sA_raw[:, kk * 128:(kk + 1) * 128],
            transpose=True), waits=(cast_done,))
        p.dma(Y, "tp", lambda e, kk=kk: e.dma_start(
            out=c.wg1T[:, kk, :], in_=c.wg1_raw[:, kk * 128:(kk + 1) * 128],
            transpose=True), waits=(cast_done,))
        p.dma(Y, "tp", lambda e, kk=kk: e.dma_start(
            out=c.routerWT[:, kk, :], in_=c.rW_raw[:, kk * 128:(kk + 1) * 128],
            transpose=True), waits=(cast_done,))
        for ei in range(E0):
            p.dma(Y, "tp", lambda e, kk=kk, ei=ei: e.dma_start(
                out=c.routedAT[:, kk, ei * R0:(ei + 1) * R0],
                in_=c.rA_raw[ei][:, kk * 128:(kk + 1) * 128], transpose=True),
                waits=(cast_done,))
        p.dma(Y, "tp", lambda e, kk=kk: e.dma_start(
            out=c.wg2T[:, kk * 128:(kk + 1) * 128],
            in_=c.wg2_pad[kk * 128:(kk + 1) * 128, :], transpose=True),
            waits=(cast_done,))
        p.dma(Y, "tp", lambda e, kk=kk: e.dma_start(
            out=c.sharedBT[:, kk * 128:(kk + 1) * 128],
            in_=c.sB_pad[kk * 128:(kk + 1) * 128, :], transpose=True),
            waits=(cast_done,))
        p.dma(Y, "wld", lambda e, kk=kk: e.dma_start(
            out=c.router1[:, kk, :], in_=c.r1_raw[kk * 128:(kk + 1) * 128, :]),
            waits=(cast_done,))
    rb_stage = c.dacc[:, 0, :]
    for ei in range(E0):
        for kk in range(KT):
            tpt = p.dma(Y, "tpr", lambda e, ei=ei, kk=kk: e.dma_start(
                out=rb_stage[:, 0:128], in_=c.rB_pad[ei][kk * 128:(kk + 1) * 128, :],
                transpose=True), waits=(cast_done, p.dnow("rbmv")))
            p.dma(G, "rbmv", lambda e, ei=ei, kk=kk: e.dma_start(
                out=c.routedBcat[ei * R0:(ei + 1) * R0, kk * 128:(kk + 1) * 128],
                in_=rb_stage[0:R0, 0:128]), waits=(tpt,))

    mset = p.op(V, lambda e: e.memset(c.ones1[:], 1.0), inc=True)
    p.op(V, lambda e: e.memset(c.epsc[:], 1e-5), inc=True, drain=True)
    for nm, src in (("base_b", tn["L0_ffn_base_b"]), ("bg2", tn["L1_attn_bg2"]),
                    ("bog", tn["L1_attn_bog"]), ("scale1", tn["L1_attn_scale1"]),
                    ("scale2", tn["L1_attn_scale2"])):
        bl = p.dma(Y, "rbl", lambda e, src=src: e.dma_start(
            out=c.rowbuf[:], in_=src.unsqueeze(0)), waits=(p.now(T),))

        def bc_mm(e):
            e.matmul(c.bank(0), c.ones1[:], c.rowbuf[:, 0:512], start=True, stop=True)
            return e.matmul(c.bank(1), c.ones1[:], c.rowbuf[:, 512:1024], start=True,
                            stop=True)
        mm = p.op(T, bc_mm, inc=True, waits=(bl, mset, p.now(S)))
        p.op(S, lambda e, nm=nm: e.copy(out=c.bcast[nm][:, 0:512], in_=c.bank(0)),
             waits=(mm,))
        p.op(S, lambda e, nm=nm: e.copy(out=c.bcast[nm][:, 512:1024], in_=c.bank(1)),
             inc=True)
    p.dma(Y, "wld", lambda e: e.dma_start(out=c.bg1c[:],
                                          in_=tn["L1_attn_bg1"].unsqueeze(1)))
    rb_ld = p.dma(Y, "rbl", lambda e: e.dma_start(
        out=c.rowbuf[:, 0:E0], in_=tn["L0_ffn_router_b"].unsqueeze(0)),
        waits=(p.now(T),))
    mm = p.op(T, lambda e: e.matmul(c.psum[0:128, 0:8], c.ones1[:], c.rowbuf[:, 0:8],
                                    start=True, stop=True), inc=True,
              waits=(rb_ld, p.now(S)))
    p.op(S, lambda e: e.copy(out=c.rbc[:], in_=c.psum[0:128, 0:8]), inc=True,
         waits=(mm,))
    p.barrier((T, V, S))
    p.wait(T, p.dnow("tp"), p.dnow("wld"), p.dnow("stg"), p.dnow("rbmv"), p.dnow("tps"), p.dnow("tpr"))
    p.wait(V, p.dnow("tp"), p.dnow("wld"))
    p.wait(S, p.dnow("tp"), p.dnow("wld"), p.dnow("rbmv"))
    p.wait(G, p.dnow("tp"))
    return c



def ln_apply(c, v_ap, out_ap, kind="ln", scale_bc=None, eps=1e-5, rms_eps=1e-6, vwait=()):
    p = c.p
    i = c.nstat % 4
    c.nstat += 1
    sq = c.cols[:, i * 8:i * 8 + 1]
    sm = c.cols[:, i * 8 + 1:i * 8 + 2]
    v1 = c.cols[:, i * 8 + 2:i * 8 + 3]
    v2 = c.cols[:, i * 8 + 3:i * 8 + 4]
    v3 = c.cols[:, i * 8 + 4:i * 8 + 5]
    d0 = c.dump[:, 2, :]
    d1 = c.dump[:, 3, :]
    s1 = p.op(S, lambda e: e.activation(d0, v_ap, AF.Square, accum_out=sq), inc=True,
               waits=tuple(vwait) + (p.now(V),))
    if kind == "ln":
        s2 = p.op(S, lambda e: e.activation(d1, v_ap, AF.Copy, accum_out=sm), inc=True)
        p.op(V, lambda e: e.tensor_scalar(sm, sm, 1.0 / D, None, ALU.mult), drain=True,
             waits=(s2,))
        p.op(V, lambda e: e.tensor_tensor(v1, sm, sm, ALU.mult), drain=True)
        vv2 = p.op(V, lambda e: e.scalar_tensor_tensor(v2, sq, 1.0 / D, v1, ALU.mult,
                                                       ALU.subtract), drain=True,
                   inc=True)
        s3 = p.op(S, lambda e: e.activation(v2, v2, AF.Sqrt, bias=c.epsc[:]), inc=True,
                  waits=(vv2,))
        p.op(V, lambda e: e.reciprocal(v3, v2), drain=True, waits=(s3,))
        r2 = p.op(V, lambda e: e.scalar_tensor_tensor(v1, sm, -1.0, v3, ALU.mult,
                                                      ALU.mult), drain=True, inc=True)
        return p.op(V, lambda e: e.tensor_scalar(out_ap, v_ap, v3, v1, ALU.mult,
                                                 ALU.add), inc=True, drain=True,
                    waits=(r2,))
    else:
        s3 = p.op(S, lambda e: e.activation(v2, sq, AF.Sqrt), inc=True,
                  waits=(s1, p.now(V)))
        p.op(V, lambda e: e.tensor_scalar(v2, v2, 1.0 / np.sqrt(D), rms_eps, ALU.mult,
                                          ALU.add), drain=True, waits=(s3,))
        r1 = p.op(V, lambda e: e.reciprocal(v3, v2), drain=True, inc=True)
        sa = p.op(V, lambda e: e.tensor_scalar(d1, v_ap, v3, None, ALU.mult), inc=True,
                  drain=True, waits=(r1,))
        return p.op(V, lambda e: e.tensor_tensor(out_ap, d1, scale_bc, ALU.mult),
                    inc=True, waits=(sa,), drain=True)


def transpose_act(c, src_fn, dst):
    p = c.p
    prev = {}
    for tt in range(TT):
        for k in range(KT):
            bnk = (tt * KT + k) % 4
            w = (prev[bnk],) if bnk in prev else ()
            mm = p.op(T, lambda e, tt=tt, k=k, bnk=bnk: e.transpose(
                c.bank(bnk)[:, 0:128], src_fn(tt)[:, k * 128:(k + 1) * 128],
                c.identF[:]), inc=True, waits=w)
            prev[bnk] = p.op(S, lambda e, tt=tt, k=k, bnk=bnk: e.copy(
                out=dst[:, k, tt * 128:(tt + 1) * 128], in_=c.bank(bnk)[:, 0:128]),
                inc=True, waits=(mm,))
    p.barrier((T, S, V))


def softmax_topk(c, psum_ap, ncand, keepk, wout_ap, renorm, bias_bc=None, waits=()):
    p = c.p
    i = c.nstat % 4
    c.nstat += 1
    mx = c.cols[:, i * 8 + 5:i * 8 + 6]
    sm = c.cols[:, i * 8 + 6:i * 8 + 7]
    s2 = c.cols[:, i * 8 + 7:i * 8 + 8]
    lg = c.dump[:, 2, 0:8]
    cnt = c.dump[:, 3, 0:8]
    if bias_bc is not None:
        p.op(V, lambda e: e.tensor_tensor(lg[:, 0:ncand], psum_ap,
                                          bias_bc[:, 0:ncand], ALU.add), drain=True,
             waits=tuple(waits))
    else:
        p.op(V, lambda e: e.tensor_copy(lg[:, 0:ncand], psum_ap), drain=True,
             waits=tuple(waits))
    p.op(V, lambda e: e.tensor_reduce(mx, lg[:, 0:ncand], AX.X, ALU.max, negate=True),
         drain=True, inc=True)
    sx = p.op(S, lambda e: e.activation(lg[:, 0:ncand], lg[:, 0:ncand], AF.Exp,
                                        bias=mx, accum_out=sm), inc=True,
              waits=(p.now(V),))
    p.op(V, lambda e: e.reciprocal(sm, sm), drain=True, waits=(sx,))
    p.op(V, lambda e: e.tensor_scalar(lg[:, 0:ncand], lg[:, 0:ncand], sm, None,
                                      ALU.mult), drain=True)
    p.op(V, lambda e: e.memset(cnt[:, 0:ncand], 0.0), drain=True)
    for j in range(ncand):
        p.op(V, lambda e, j=j: e.scalar_tensor_tensor(
            cnt[:, 0:ncand], lg[:, 0:ncand], lg[:, j:j + 1], cnt[:, 0:ncand],
            ALU.is_lt, ALU.add), drain=True)
    p.op(V, lambda e: e.tensor_scalar(cnt[:, 0:ncand], cnt[:, 0:ncand],
                                      float(keepk) - 0.5, None, ALU.is_lt), drain=True)
    if ncand < 8:
        p.op(V, lambda e: e.memset(wout_ap[:, ncand:8], 0.0), drain=True)
    last = p.op(V, lambda e: e.tensor_tensor(wout_ap[:, 0:ncand], lg[:, 0:ncand],
                                             cnt[:, 0:ncand], ALU.mult), inc=True,
                drain=True)
    if renorm:
        p.op(V, lambda e: e.tensor_reduce(s2, wout_ap[:, 0:ncand], AX.X, ALU.add),
             drain=True)
        p.op(V, lambda e: e.tensor_scalar(s2, s2, 1e-6, None, ALU.add), drain=True)
        p.op(V, lambda e: e.reciprocal(s2, s2), drain=True)
        last = p.op(V, lambda e: e.tensor_scalar(wout_ap[:, 0:ncand],
                                                 wout_ap[:, 0:ncand], s2, None,
                                                 ALU.mult), inc=True, drain=True)
    return last



def step1_fused(c, x0_ap, scr):
    p = c.p
    for tt in range(TT):
        b, q = tt // 4, tt % 4
        lw = (p.now(V), p.now(S), p.now(T))
        if tt == 0:
            lw = lw + (p.dnow("rbmv"), p.dnow("tpr"))
        l3 = p.dma(Y, "ld1", lambda e, tt=tt: e.dma_start(
            out=c.dump[:, 2, :], in_=x0_ap[tt * 128:(tt + 1) * 128, :]), waits=lw)
        pf_sb = c.dacc[:, tt, :]
        l1 = p.dma(Y, "ld1", lambda e, pf_sb=pf_sb, b=b, q=q: e.dma_start(
            out=pf_sb, in_=scr[0, b, q * 128:(q + 1) * 128, :]))
        prb = c.Pb[:, 0:2, :].rearrange("p a b -> p (a b)")
        l2 = p.dma(Y, "ld1", lambda e, prb=prb, b=b, q=q: e.dma_start(
            out=prb, in_=scr[1, b, (3 - q) * 128:(4 - q) * 128, :]))

        def mm_j(e, prb=prb):
            e.matmul(c.bank(0), c.identJ[:], prb[:, 0:512], start=True, stop=True)
            return e.matmul(c.bank(1), c.identJ[:], prb[:, 512:1024], start=True,
                            stop=True)
        mmj = p.op(T, mm_j, inc=True, waits=(l2, p.now(V), p.now(S)))
        v = c.dump[:, 0, :]
        p.op(V, lambda e, pf_sb=pf_sb: e.tensor_tensor(
            v, c.dump[:, 2, :], pf_sb, ALU.add), waits=(l1, l3), drain=True)
        p.op(V, lambda e: e.tensor_tensor(v[:, 0:512], v[:, 0:512], c.bank(0),
                                          ALU.add), waits=(mmj,), drain=True)
        vv = p.op(V, lambda e: e.tensor_tensor(v[:, 512:1024], v[:, 512:1024],
                                               c.bank(1), ALU.add),
                  inc=True, drain=True)
        ln_apply(c, v, c.resid[:, tt, :], kind="ln", vwait=(vv,))
    p.barrier((T, V, S))



def step2(c):
    p = c.p
    transpose_act(c, lambda tt: c.resid[:, tt, :], c.xT)
    for tt in range(TT):
        ts = tt * 128

        def mm_small(e, ts=ts):
            ins = None
            for k in range(KT):
                st, sp = (k == 0), (k == KT - 1)
                e.matmul(c.bank(4)[:, 0:16], c.xT[:, k, ts:ts + 128],
                         c.sharedAT[:, k, :], start=st, stop=sp)
                e.matmul(c.bank(6)[:, 0:96], c.xT[:, k, ts:ts + 128],
                         c.routedAT[:, k, :], start=st, stop=sp)
                ins = e.matmul(c.bank(5)[:, 0:8], c.xT[:, k, ts:ts + 128],
                               c.routerWT[:, k, 0:8], start=st, stop=sp)
            return ins
        mm = p.op(T, mm_small, inc=True, waits=(p.now(S), p.now(V)))
        softmax_topk(c, c.bank(5)[:, 0:E0], E0, 2, c.w8[:, tt, :], True, bias_bc=c.rbc,
                     waits=(mm,))
        hsb = c.gh_f.rearrange("p a b -> p (a b)")[:, 0:112]
        ev1 = p.op(S, lambda e, hsb=hsb: e.copy(out=hsb[:, 0:16], in_=c.bank(4)[:, 0:16]),
                   waits=(mm, p.now(V)))
        ev2 = p.op(S, lambda e, hsb=hsb: e.copy(out=hsb[:, 16:112],
                                                in_=c.bank(6)[:, 0:96]), inc=True)
        for ei in range(E0):
            hv = p.op(V, lambda e, ei=ei, hsb=hsb, tt=tt: e.tensor_scalar(
                hsb[:, 16 + ei * R0:16 + (ei + 1) * R0],
                hsb[:, 16 + ei * R0:16 + (ei + 1) * R0],
                c.w8[:, tt, ei:ei + 1], None, ALU.mult), drain=True, waits=(ev2,),
                inc=(ei == E0 - 1))
        mm2 = p.op(T, lambda e, hsb=hsb: e.transpose(c.bankB(7)[0:16, 0:128],
                                                     hsb[:, 0:16], c.identB[:]),
                   waits=(hv, p.now(S)))
        mm2b = p.op(T, lambda e, hsb=hsb: e.transpose(c.bankB(6)[0:96, 0:128],
                                                      hsb[:, 16:112], c.identB[:]),
                    inc=True)
        p.op(S, lambda e: e.copy(out=c.hTs[:], in_=c.bankB(7)[0:16, 0:128]),
             waits=(mm2b,))
        evT = p.op(S, lambda e: e.copy(out=c.hTr[:], in_=c.bankB(6)[0:96, 0:128]),
                   inc=True)

        def mm_main(e, ts=ts):
            ins = None
            for half in range(2):
                f0 = half * 512
                for k in range(KT):
                    e.matmul(c.bank(half), c.xT[:, k, ts:ts + 128],
                             c.wbufA[:, (half * KT + k) % 2, 0:512],
                             start=(k == 0), stop=False)
                e.matmul(c.bank(half), c.hT[0:16, 0:128], c.sharedBT[0:16, f0:f0 + 512],
                         start=False, stop=False)
                ins = e.matmul(c.bank(half), c.hT[16:112, 0:128],
                               c.routedBcat[:, f0:f0 + 512], start=False, stop=True)
            return ins
        slot_consumer = {0: p.now(T), 1: p.now(T)}
        for half in range(2):
            f0 = half * 512
            for k in range(KT):
                slot = (half * KT + k) % 2
                wld = p.dma(Y, f"ws{slot}", lambda e, k=k, f0=f0, slot=slot: e.dma_start(
                    out=c.wbufA[:, slot, 0:512],
                    in_=c.baseWT[k * 128:(k + 1) * 128, f0:f0 + 512]),
                    waits=(slot_consumer[slot],))
                slot_consumer[slot] = p.op(T, lambda e, k=k, half=half, ts=ts, slot=slot: e.matmul(
                    c.bank(half), c.xT[:, k, ts:ts + 128], c.wbufA[:, slot, 0:512],
                    start=(k == 0), stop=False), inc=True,
                    waits=(wld,) + ((evT,) if (half == 0 and k == 0) else ()))
            p.op(T, lambda e, half=half, f0=f0: e.matmul(
                c.bank(half), c.hTs[:], c.sharedBT[0:16, f0:f0 + 512],
                start=False, stop=False))
            p.op(T, lambda e, half=half, f0=f0: e.matmul(
                c.bank(half), c.hTr[:], c.routedBcat[:, f0:f0 + 512],
                start=False, stop=True), inc=True)
        mm3 = p.now(T)
        v = c.dump[:, 0, :]
        p.op(V, lambda e: e.tensor_tensor(v[:, 0:512], c.bank(0),
                                          c.bcast["base_b"][:, 0:512], ALU.add),
             waits=(mm3,))
        p.op(V, lambda e: e.tensor_tensor(v[:, 512:1024], c.bank(1),
                                          c.bcast["base_b"][:, 512:1024], ALU.add),
             drain=True)
        vv = p.op(V, lambda e, tt=tt: e.tensor_tensor(v, v, c.resid[:, tt, :], ALU.add),
                  inc=True, drain=True)
        ln_apply(c, v, c.resid[:, tt, :], kind="ln", vwait=(vv,))
    p.barrier((T, V, S))



def step3(c, upto3='all'):
    p = c.p
    if upto3 == 'rms1':
        for tt in range(TT):
            ln_apply(c, c.resid[:, tt, :], c.dacc[:, tt, :], kind="rms",
                     scale_bc=c.bcast["scale1"][:])
        p.barrier((T, V, S))
        return
    for tt in range(TT):
        ln_apply(c, c.resid[:, tt, :], c.dump[:, 1, :], kind="rms",
                 scale_bc=c.bcast["scale1"][:])
        for k in range(KT):
            bnk = k % 4
            mm = p.op(T, lambda e, k=k, bnk=bnk: e.transpose(
                c.bank(bnk)[:, 0:128], c.dump[:, 1, k * 128:(k + 1) * 128],
                c.identF[:]), inc=True, waits=(p.now(V), p.now(S)))
            p.op(S, lambda e, k=k, bnk=bnk, tt=tt: e.copy(
                out=c.xT[:, k, tt * 128:(tt + 1) * 128], in_=c.bank(bnk)[:, 0:128]),
                inc=True, waits=(mm,))
        p.barrier((T, V, S))
    for fc in range(4):
        f0 = fc * 512

        def mm_gh(e, f0=f0):
            ins = None
            for k in range(KT):
                ins = e.matmul(c.bank(4, 16), c.wg1T[:, k, :], c.xT[:, k, f0:f0 + 512],
                               start=(k == 0), stop=(k == KT - 1))
            return ins
        mm = p.op(T, mm_gh, inc=True, waits=(p.now(S),))
        p.op(S, lambda e, f0=f0: e.activation(c.ghT[:, f0:f0 + 512], c.bank(4, 16),
                                              AF.Silu, bias=c.bg1c[:]), inc=True,
             waits=(mm,))
    p.barrier((T, V, S))
    if upto3 == 'xt':
        p.op(V, lambda e: e.memset(c.dacc[:], 0.0), inc=True, drain=True)
        for k in range(KT):
            p.op(V, lambda e, k=k: e.tensor_copy(c.dacc[:, k, :],
                                                 c.xT[:, k, 0:1024]), inc=True,
                 drain=True)
        p.barrier((T, V, S))
        return
    p.op(V, lambda e: e.memset(c.dacc[:], 0.0), inc=True, drain=True)
    pairs = [(t, s) for t in range(4) for s in range(t + 1)]
    for sub in range(4):
        tiles = [sub + 4 * b for b in range(4)]
        for hh in range(2):
            h0 = hh * 512
            for bi in range(4):
                ts = tiles[bi] * 128
                mm = p.op(T, lambda e, bi=bi, ts=ts, h0=h0: e.matmul(
                    c.bank(bi), c.ghT[:, ts:ts + 128], c.wg2T[0:16, h0:h0 + 512],
                    start=True, stop=True), inc=True, waits=(p.now(S), p.now(V)))
                zf = c.dump[:, bi % 2, 0:512]
                zv = p.op(V, lambda e, bi=bi, zf=zf, h0=h0: e.tensor_tensor(
                    zf, c.bank(bi), c.bcast["bg2"][:, h0:h0 + 512], ALU.add),
                    inc=True, drain=True, waits=(mm,))
                p.op(S, lambda e, bi=bi, zf=zf: e.activation(
                    c.Pb[:, bi, :], zf, AF.Sigmoid, scale=1.0 / GLA_TEMP), inc=True,
                    waits=(zv,))
                se = p.op(S, lambda e, bi=bi, zf=zf: e.activation(
                    c.dump[:, 2 + bi % 2, 0:512], zf, AF.Exp, scale=-1.0 / GLA_TEMP),
                    inc=True)
                p.op(V, lambda e, bi=bi: e.tensor_scalar(
                    c.Pib[:, bi, :], c.dump[:, 2 + bi % 2, 0:512], 1.0, None, ALU.add),
                    inc=True, drain=True, waits=(se,))
            for bi in range(1, 4):
                p.op(V, lambda e, bi=bi: e.tensor_tensor(
                    c.Pb[:, bi, :], c.Pb[:, bi, :], c.Pb[:, bi - 1, :], ALU.mult),
                    drain=True)
            for bi in range(1, 4):
                p.op(V, lambda e, bi=bi: e.tensor_tensor(
                    c.Pib[:, bi, :], c.Pib[:, bi, :], c.Pib[:, bi - 1, :], ALU.mult),
                    drain=True, inc=(bi == 3))
            if upto3 == 'al1' and sub == DBG_SUB and hh == 0:
                p.barrier((T, V, S))
                return
            slot_consumer = {0: p.now(T), 1: p.now(T)}
            for wd, kind in ((c.wqT, "q"), (c.wkT, "k"), (c.wvT, "v"), (c.wogT, "og")):
                for k in range(KT):
                    slot = k % 2
                    wld = p.dma(Y, f"ws{slot}", lambda e, wd=wd, k=k, h0=h0,
                                slot=slot: e.dma_start(
                        out=c.wbufA[:, slot, 0:512],
                        in_=wd[k * 128:(k + 1) * 128, h0:h0 + 512]),
                        waits=(slot_consumer[slot],))

                    def mm_w(e, k=k, tiles=tiles):
                        ins = None
                        for bi in range(4):
                            ts = tiles[bi] * 128
                            ins = e.matmul(c.bank(4 + bi), c.xT[:, k, ts:ts + 128],
                                           c.wbufA[:, k % 2, 0:512], start=(k == 0),
                                           stop=(k == KT - 1))
                        return ins
                    last_mm = p.op(T, mm_w, inc=True,
                                   waits=(wld,) + ((p.now(S), p.now(V)) if k == 0 else ()))
                    slot_consumer[slot] = last_mm
                sil = {}
                for bi in range(4):
                    if kind == "q":
                        p.op(S, lambda e, bi=bi: e.copy(out=c.qb[:, bi, :],
                                                        in_=c.bank(4 + bi)), inc=True,
                             waits=(last_mm,))
                    elif kind == "k":
                        p.op(S, lambda e, bi=bi: e.copy(out=c.kb[:, bi, :],
                                                        in_=c.bank(4 + bi)), inc=True,
                             waits=(last_mm,))
                    elif kind == "v":
                        p.op(S, lambda e, bi=bi: e.copy(out=c.vb[:, bi, :],
                                                        in_=c.bank(4 + bi)), inc=True,
                             waits=(last_mm,))
                    else:
                        ov = p.op(V, lambda e, bi=bi, h0=h0: e.tensor_tensor(
                            c.dump[:, bi % 2, 0:512], c.bank(4 + bi),
                            c.bcast["bog"][:, h0:h0 + 512], ALU.add), inc=True,
                            drain=True,
                            waits=(last_mm,) + ((sil[bi - 2],) if bi - 2 in sil else ()))
                        sil[bi] = p.op(S, lambda e, bi=bi, h0=h0: e.activation(
                            c.og[:, bi, h0:h0 + 512], c.dump[:, bi % 2, 0:512],
                            AF.Silu), inc=True, waits=(ov,))
            p.barrier((T, V, S))
            if upto3 in ('q', 'k', 'v', 'og', 'alpha') and sub == DBG_SUB and hh == 0:
                src = {'q': c.qb, 'k': c.kb, 'v': c.vb, 'og': c.og, 'alpha': c.Pb}[upto3]
                wid = 1024 if upto3 == 'og' else 512
                for bi in range(4):
                    p.op(V, lambda e, bi=bi, src=src, wid=wid: e.tensor_copy(
                        c.dacc[:, bi, 0:wid], src[:, bi, 0:wid]), inc=True, drain=True)
                p.barrier((T, V, S))
                return
            for bi in range(4):
                pi = pairs.index((bi, bi))
                tm = c.dump[:, 2 + bi % 2, 0:512] if bi < 2 else c.dump[:, 2 + bi % 2, 512:1024]
                p.op(V, lambda e, bi=bi, tm=tm: e.tensor_tensor(
                    tm, c.qb[:, bi, :], c.kb[:, bi, :], ALU.mult), drain=True)
                p.op(V, lambda e, pi=pi, tm=tm: e.tensor_reduce(
                    c.sc[:, pi, :], tm.rearrange("p (h k) -> p h k", k=DK), AX.X,
                    ALU.add), drain=True, inc=True)
            for bi in range(4):
                p.op(V, lambda e, bi=bi: e.tensor_tensor(
                    c.qb[:, bi, :], c.qb[:, bi, :], c.Pb[:, bi, :], ALU.mult))
            for bi in range(4):
                p.op(V, lambda e, bi=bi: e.tensor_tensor(
                    c.kb[:, bi, :], c.kb[:, bi, :], c.Pib[:, bi, :], ALU.mult),
                    inc=(bi == 3))
            for pi, (t_, s_) in enumerate(pairs):
                if t_ == s_:
                    continue
                tm = c.dump[:, 2 + pi % 2, 0:512] if pi % 4 < 2 else c.dump[:, 2 + pi % 2, 512:1024]
                p.op(V, lambda e, t_=t_, s_=s_, tm=tm: e.tensor_tensor(
                    tm, c.qb[:, t_, :], c.kb[:, s_, :], ALU.mult), drain=True)
                p.op(V, lambda e, pi=pi, tm=tm: e.tensor_reduce(
                    c.sc[:, pi, :], tm.rearrange("p (h k) -> p h k", k=DK), AX.X,
                    ALU.add), drain=True, inc=True)
            p.op(V, lambda e: e.memset(c.oac[:], 0.0), drain=True)
            for pi, (t_, s_) in enumerate(pairs):
                tm = c.dump[:, 2 + pi % 2, 0:512] if pi % 4 < 2 else c.dump[:, 2 + pi % 2, 512:1024]
                scbc = c.sc[:, pi, :].unsqueeze(2).to_broadcast([128, 8, DK])
                p.op(V, lambda e, s_=s_, tm=tm, scbc=scbc: e.tensor_tensor(
                    tm.rearrange("p (h k) -> p h k", k=DK), scbc,
                    c.vb[:, s_, :].rearrange("p (h k) -> p h k", k=DK), ALU.mult),
                    drain=True)
                p.op(V, lambda e, t_=t_, tm=tm: e.tensor_tensor(
                    c.oac[:, t_, :], c.oac[:, t_, :], tm, ALU.add), drain=True)
            if upto3 == 'o' and sub == DBG_SUB and hh == 0:
                for bi in range(4):
                    p.op(V, lambda e, bi=bi: e.tensor_copy(
                        c.dacc[:, bi, 0:512], c.oac[:, bi, :]), inc=True, drain=True)
                p.barrier((T, V, S))
                return
            for bi in range(4):
                i8 = c.sc[:, 10, :]
                r8 = c.sc[:, 11, :]
                t8 = c.cols[:, 40:48]
                ov = c.oac[:, bi, :]
                o3 = ov.rearrange("p (h k) -> p h k", k=DK)
                p.op(V, lambda e, o3=o3, i8=i8: e.tensor_reduce(i8, o3, AX.X, ALU.add),
                     drain=True, inc=True)
                sq = c.dump[:, 2, 0:512]
                sqo = p.op(S, lambda e, ov=ov, sq=sq: e.activation(sq, ov, AF.Square),
                           inc=True, waits=(p.now(V),))
                p.op(V, lambda e, sq=sq, r8=r8: e.tensor_reduce(
                    r8, sq.rearrange("p (h k) -> p h k", k=DK), AX.X, ALU.add),
                    drain=True, waits=(sqo,))
                p.op(V, lambda e, i8=i8: e.tensor_scalar(i8, i8, 1.0 / DK, None,
                                                         ALU.mult), drain=True)
                p.op(V, lambda e, i8=i8, t8=t8: e.tensor_tensor(t8, i8, i8, ALU.mult),
                     drain=True)
                p.op(V, lambda e, r8=r8, t8=t8: e.scalar_tensor_tensor(
                    r8, r8, 1.0 / DK, t8, ALU.mult, ALU.subtract), drain=True)
                sr = p.op(S, lambda e, r8=r8: e.activation(r8, r8, AF.Sqrt, bias=c.epsc[:]),
                          inc=True, waits=(p.now(V),))
                p.op(V, lambda e, r8=r8: e.reciprocal(r8, r8), drain=True, waits=(sr,))
                p.op(V, lambda e, o3=o3, i8=i8: e.tensor_tensor(
                    o3, o3, i8.unsqueeze(2).to_broadcast([128, 8, DK]), ALU.subtract),
                    drain=True)
                p.op(V, lambda e, o3=o3, r8=r8: e.tensor_tensor(
                    o3, o3, r8.unsqueeze(2).to_broadcast([128, 8, DK]), ALU.mult),
                    drain=True)
                p.op(V, lambda e, bi=bi, ov=ov, h0=h0: e.tensor_tensor(
                    c.og[:, bi, h0:h0 + 512], ov, c.og[:, bi, h0:h0 + 512], ALU.mult),
                    inc=True, drain=True)
            p.barrier((V, S, T))
            if upto3 == 'oln' and sub == DBG_SUB and hh == 0:
                for bi in range(4):
                    p.op(V, lambda e, bi=bi: e.tensor_copy(
                        c.dacc[:, bi, 0:512], c.oac[:, bi, :]), inc=True, drain=True)
                p.barrier((T, V, S))
                return
        if upto3 == 'og2' and sub == DBG_SUB:
            for bi in range(4):
                p.op(V, lambda e, bi=bi: e.tensor_copy(
                    c.dacc[:, bi, :], c.og[:, bi, :]), inc=True, drain=True)
            p.barrier((T, V, S))
            return
        if upto3 == 'hh1' and sub == DBG_SUB:
            p.barrier((T, V, S))
            return
        for bi in range(4):
            for k in range(KT):
                bnk = k % 4
                mm = p.op(T, lambda e, bi=bi, k=k, bnk=bnk: e.transpose(
                    c.bankB(bnk)[:, 0:128], c.og[:, bi, k * 128:(k + 1) * 128],
                    c.identB[:]), inc=True, waits=(p.now(S), p.now(V)))
                p.op(S, lambda e, bi=bi, k=k, bnk=bnk: e.copy(
                    out=c.ogT[:, k, bi * 128:(bi + 1) * 128],
                    in_=c.bankB(bnk)[:, 0:128]), inc=True, waits=(mm,))
        p.barrier((T, S, V))
        slot_consumer = {0: p.now(T), 1: p.now(T)}
        for bi in range(4):
            tt = tiles[bi]
            for k in range(KT):
                slot = k % 2
                wld = p.dma(Y, f"ws{slot}", lambda e, k=k, slot=slot: e.dma_start(
                    out=c.wbufA[:, slot, :], in_=c.woT[k * 128:(k + 1) * 128, :]),
                    waits=(slot_consumer[slot],))

                def mm_wo(e, bi=bi, k=k, slot=slot):
                    e.matmul(c.bank(4), c.ogT[:, k, bi * 128:(bi + 1) * 128],
                             c.wbufA[:, slot, 0:512], start=(k == 0),
                             stop=(k == KT - 1))
                    return e.matmul(c.bank(5), c.ogT[:, k, bi * 128:(bi + 1) * 128],
                                    c.wbufA[:, slot, 512:1024], start=(k == 0),
                                    stop=(k == KT - 1))
                slot_consumer[slot] = p.op(T, mm_wo, inc=True,
                     waits=(wld,) + ((p.now(V), p.now(S)) if k == 0 else ()))
            ev = p.op(V, lambda e, tt=tt: e.tensor_copy(c.dacc[:, tt, 0:512],
                                                        c.bank(4)), drain=True,
                      waits=(p.now(T),))
            p.op(V, lambda e, tt=tt: e.tensor_copy(c.dacc[:, tt, 512:1024],
                                                   c.bank(5)), inc=True, drain=True)
        if upto3 == 'wo1' and sub == DBG_SUB:
            p.barrier((T, V, S))
            return
    p.barrier((T, V, S))
    if upto3 == 'gla':
        return
    for tt in range(TT):
        v = c.dump[:, 1, :]
        vv = p.op(V, lambda e, tt=tt: e.tensor_tensor(v, c.resid[:, tt, :],
                                                      c.dacc[:, tt, :], ALU.add),
                  inc=True, drain=True, waits=(p.now(S),))
        ln_apply(c, v, c.dump[:, 0, :], kind="rms", scale_bc=c.bcast["scale2"][:],
                 vwait=(vv,))
        for k in range(KT):
            bnk = k % 4
            mm = p.op(T, lambda e, k=k, bnk=bnk: e.transpose(
                c.bank(bnk)[:, 0:128], c.dump[:, 0, k * 128:(k + 1) * 128],
                c.identF[:]), inc=True, waits=(p.now(V), p.now(S)))
            p.op(S, lambda e, k=k, bnk=bnk, tt=tt: e.copy(
                out=c.xT[:, k, tt * 128:(tt + 1) * 128], in_=c.bank(bnk)[:, 0:128]),
                inc=True, waits=(mm,))
        p.barrier((T, V, S))
    if upto3 == 'rms2':
        return
    ffn_slab(c, [("g", c.wgateT, c.wupT, c.wdownT, None, DHID // 512)], c.dacc, step=3)
    p.barrier((T, V, S))
    for tt in range(TT):
        v = c.dump[:, 1, :]
        vv = p.op(V, lambda e, tt=tt: e.scalar_tensor_tensor(
            v, c.resid[:, tt, :], 2.0, c.dacc[:, tt, :], ALU.mult, ALU.add),
            inc=True, drain=True, waits=(p.now(S),))
        ln_apply(c, v, c.resid[:, tt, :], kind="ln", vwait=(vv,))
    p.barrier((T, V, S))



def step4(c, out_ap):
    p = c.p
    transpose_act(c, lambda tt: c.resid[:, tt, :], c.xT)
    for tt in range(TT):
        ts = tt * 128

        def mm_r(e, ts=ts):
            ins = None
            for k in range(KT):
                ins = e.matmul(c.bank(6)[:, 0:8], c.xT[:, k, ts:ts + 128],
                               c.router1[:, k, :], start=(k == 0), stop=(k == KT - 1))
            return ins
        mm = p.op(T, mm_r, inc=True, waits=(p.now(S), p.now(V)))
        p.wait(V, mm)
        softmax_topk(c, c.bank(6)[:, 0:E1], E1, 6, c.w8[:, tt, :], False)
    p.barrier((T, V, S))
    p.op(V, lambda e: e.memset(c.dacc[:], 0.0), inc=True, drain=True)
    specs = []
    for ei in range(E1):
        specs.append(("exp", ei, c.f1gate[ei], c.f1up[ei], c.f1down[ei]))
    specs.append(("shr", 0, c.wsgate[:, 0:512], c.wsup[:, 0:512], c.wsdown[0:512, :]))
    specs.append(("shr", 1, c.wsgate[:, 512:1024], c.wsup[:, 512:1024],
                  c.wsdown[512:1024, :]))
    ffn_slab(c, [(kind, wg, wu, wd, ei, 1) for kind, ei, wg, wu, wd in specs],
             c.dacc, step=4)
    p.barrier((T, V, S))
    for tt in range(TT):
        v = c.dump[:, 1, :]
        vv = p.op(V, lambda e, tt=tt: e.tensor_tensor(v, c.resid[:, tt, :],
                                                      c.dacc[:, tt, :], ALU.add),
                  inc=True, drain=True, waits=(p.now(S),))
        lo = ln_apply(c, v, c.dump[:, 0, :], kind="ln", vwait=(vv,))
        p.dma(G, "st", lambda e, tt=tt: e.dma_start(
            out=out_ap[tt * 128:(tt + 1) * 128, :], in_=c.dump[:, 0, :]),
            waits=(lo,))
        p.wait(V, p.dnow("st"))
        p.wait(S, p.dnow("st"))
    p.wait(G, p.dnow("st"))
    p.op(G, lambda e: e.nop())


def ffn_slab(c, units, dacc, step):
    p = c.p
    slab_last_T = p.now(T)
    for kind, wg, wu, wd, ei, nfc in units:
        for fc in range(nfc):
            f0 = fc * 512
            lds = []
            for k in range(KT):
                if step == 3:
                    ga = wg[k * 128:(k + 1) * 128, f0:f0 + 512]
                    ua = wu[k * 128:(k + 1) * 128, f0:f0 + 512]
                else:
                    ga = wg[k * 128:(k + 1) * 128, 0:512]
                    ua = wu[k * 128:(k + 1) * 128, 0:512]
                lds.append(p.dma(Y, "wsl", lambda e, k=k, ga=ga: e.dma_start(
                    out=c.slab_g[:, k, :], in_=ga), waits=(slab_last_T,)))
                lds.append(p.dma(G, "wsl2", lambda e, k=k, ua=ua: e.dma_start(
                    out=c.slab_u[:, k, :], in_=ua), waits=(slab_last_T,)))
            for k4 in range(4):
                if step == 3:
                    da = wd[f0 + k4 * 128:f0 + (k4 + 1) * 128, :]
                else:
                    da = wd[k4 * 128:(k4 + 1) * 128, :]
                lds.append(p.dma(Y, "wsl", lambda e, k4=k4, da=da: e.dma_start(
                    out=c.slab_d[:, k4, :], in_=da), waits=(slab_last_T,)))
            slab_ok = (p.dnow("wsl"), p.dnow("wsl2"))
            for tt in range(TT):
                ts = tt * 128

                def mm_gu(e, ts=ts):
                    ins = None
                    for k in range(KT):
                        e.matmul(c.bank(0), c.xT[:, k, ts:ts + 128],
                                 c.slab_g[:, k, :], start=(k == 0), stop=(k == KT - 1))
                        ins = e.matmul(c.bank(1), c.xT[:, k, ts:ts + 128],
                                       c.slab_u[:, k, :], start=(k == 0),
                                       stop=(k == KT - 1))
                    return ins
                gu = p.op(T, mm_gu, inc=True,
                          waits=slab_ok + (p.now(S), p.now(V)))
                sg = p.op(S, lambda e: e.activation(c.gh_f[:, 0, :], c.bank(0),
                                                    AF.Silu), inc=True, waits=(gu,))
                su = p.op(S, lambda e: e.copy(out=c.uh_f[:, 0, :], in_=c.bank(1)),
                          inc=True)
                hv = p.op(V, lambda e: e.tensor_tensor(
                    c.gh_f[:, 1, :], c.gh_f[:, 0, :], c.uh_f[:, 0, :], ALU.mult),
                    inc=True, drain=True, waits=(su,))
                if kind == "exp":
                    hv = p.op(V, lambda e, ei=ei, tt=tt: e.tensor_scalar(
                        c.gh_f[:, 1, :], c.gh_f[:, 1, :], c.w8[:, tt, ei:ei + 1],
                        None, ALU.mult), inc=True, drain=True)
                for k4 in range(4):
                    bnk = 2 + k4 % 2
                    mm = p.op(T, lambda e, k4=k4, bnk=bnk: e.transpose(
                        c.bankB(bnk)[:, 0:128], c.gh_f[:, 1, k4 * 128:(k4 + 1) * 128],
                        c.identB[:]), inc=True, waits=(hv, p.now(S)))
                    p.op(S, lambda e, k4=k4, bnk=bnk: e.copy(
                        out=c.hT[:, k4 * 128:(k4 + 1) * 128],
                        in_=c.bankB(bnk)[:, 0:128]), inc=True, waits=(mm,))
                hTready = p.now(S)

                def mm_dn(e):
                    ins = None
                    for k4 in range(4):
                        e.matmul(c.bank(4), c.hT[:, k4 * 128:(k4 + 1) * 128],
                                 c.slab_d[:, k4, 0:512], start=(k4 == 0),
                                 stop=(k4 == 3))
                        ins = e.matmul(c.bank(5), c.hT[:, k4 * 128:(k4 + 1) * 128],
                                       c.slab_d[:, k4, 512:1024], start=(k4 == 0),
                                       stop=(k4 == 3))
                    return ins
                dn = p.op(T, mm_dn, inc=True, waits=(hTready, p.now(V)))
                slab_last_T = dn
                p.op(V, lambda e, tt=tt: e.tensor_tensor(
                    dacc[:, tt, 0:512], dacc[:, tt, 0:512], c.bank(4), ALU.add),
                    drain=True, waits=(dn,))
                p.op(V, lambda e, tt=tt: e.tensor_tensor(
                    dacc[:, tt, 512:1024], dacc[:, tt, 512:1024], c.bank(5), ALU.add),
                    inc=True, drain=True)



WNAMES = [
    "L0_ffn_base_W", "L0_ffn_base_b", "L0_ffn_shared_A", "L0_ffn_shared_B",
    "L0_ffn_routed_A", "L0_ffn_routed_B", "L0_ffn_router_W", "L0_ffn_router_b",
    "L1_attn_scale1", "L1_attn_scale2", "L1_attn_wq", "L1_attn_wk", "L1_attn_wv",
    "L1_attn_wg1", "L1_attn_bg1", "L1_attn_wg2", "L1_attn_bg2", "L1_attn_wog",
    "L1_attn_bog", "L1_attn_wo", "L1_attn_w_gate", "L1_attn_w_up", "L1_attn_w_down",
    "L1_ffn_router_w", "L1_ffn_w_gate", "L1_ffn_w_up", "L1_ffn_w_down",
    "L1_ffn_ws_gate", "L1_ffn_ws_up", "L1_ffn_ws_down",
]

_WSHAPES = {
    "L0_ffn_base_W": (1024, 1024), "L0_ffn_base_b": (1024,),
    "L0_ffn_shared_A": (16, 1024), "L0_ffn_shared_B": (1024, 16),
    "L0_ffn_routed_A": (6, 16, 1024), "L0_ffn_routed_B": (6, 1024, 16),
    "L0_ffn_router_W": (6, 1024), "L0_ffn_router_b": (6,),
    "L1_attn_scale1": (1024,), "L1_attn_scale2": (1024,),
    "L1_attn_wq": (1024, 1024), "L1_attn_wk": (1024, 1024),
    "L1_attn_wv": (1024, 1024), "L1_attn_wg1": (16, 1024),
    "L1_attn_bg1": (16,), "L1_attn_wg2": (1024, 16), "L1_attn_bg2": (1024,),
    "L1_attn_wog": (1024, 1024), "L1_attn_bog": (1024,),
    "L1_attn_wo": (1024, 1024), "L1_attn_w_gate": (4096, 1024),
    "L1_attn_w_up": (4096, 1024), "L1_attn_w_down": (1024, 4096),
    "L1_ffn_router_w": (1024, 8), "L1_ffn_w_gate": (8, 1024, 512),
    "L1_ffn_w_up": (8, 1024, 512), "L1_ffn_w_down": (8, 512, 1024),
    "L1_ffn_ws_gate": (1024, 1024), "L1_ffn_ws_up": (1024, 1024),
    "L1_ffn_ws_down": (1024, 1024),
}


def _build_fused():
    nc = bass.Bass("TRN2", target_bir_lowering=False)
    tens = {}
    ein = lambda nm, shp, dt=F32: nc.dram_tensor(nm, list(shp), dt,
                                                 kind="ExternalInput").ap()
    tens["x0sched"] = ein("x0sched", [9, 4, 512, 1024])
    for s in "fr":
        tens[f"in_proj_{s}"] = ein(f"in_proj_{s}", [4096, 1024])
        tens[f"out_proj_{s}"] = ein(f"out_proj_{s}", [1024, 2048])
        tens[f"x_proj_{s}"] = ein(f"x_proj_{s}", [96, 2048])
        tens[f"dt_w_{s}"] = ein(f"dt_w_{s}", [2048, 64])
        tens[f"smallsF_{s}"] = ein(f"smallsF_{s}", [128, 368])
    tens["sel16"] = ein("sel16", [16, 2048], BF16)
    tens["masks"] = ein("masks", [128, 35])
    for nm in WNAMES:
        tens[nm] = ein(nm, _WSHAPES[nm])
    tens["x0s"] = ein("x0s", [2048, 1024])
    tens["identF"] = ein("identF", [128, 128])
    tens["identB"] = ein("identB", [128, 128], BF16)
    tens["identJ"] = ein("identJ", [128, 128], BF16)
    out = nc.dram_tensor("out", [2048, 1024], F32, kind="ExternalOutput").ap()

    p = Prog(inc_all=True)
    stack = ExitStack()
    with stack:
        psum = stack.enter_context(nc.psum_tensor("psum", [128, 4096], F32))
        st1 = ExitStack()
        scr = build_mamba_fused(nc, p, st1, tens, psum)
        p.barrier(tuple(ENGINES))
        chans = list(p.dcnt)
        for eng in ENGINES:
            p.wait(eng, *[p.dnow(ch) for ch in chans])
        st1.close()
        c = alloc_common(nc, p, stack, tens, psum)
        step1_fused(c, tens["x0s"], scr)
        step2(c)
        step3(c)
        step4(c, out)
        p.build(nc, stack)
    return nc


def kernel(**inputs):
    inputs = {k: np.asarray(v, np.float32) for k, v in inputs.items()}
    x_full = inputs["L0_attn_hidden_states"]
    flip = x_full[:, ::-1, :]

    nc = _build_fused()

    sel = np.zeros((16, 2048), BF)
    for n in range(16):
        sel[n, n * 128:(n + 1) * 128] = 1
    identF = np.eye(128, dtype=np.float32)
    identB = np.eye(128, dtype=BF)
    identJ = np.eye(128, dtype=np.float32)[::-1].astype(BF)
    smallsF = {}
    for s in "fr":
        A = np.asarray(inputs[f"L0_attn_A_log_{s}"]).reshape(16, 128, 16)
        A = A.transpose(1, 0, 2).reshape(128, 256)
        dtb = inputs[f"L0_attn_dt_b_{s}"].reshape(16, 128).T
        convb = inputs[f"L0_attn_conv_b_{s}"].reshape(16, 128).T
        Dp = inputs[f"L0_attn_D_{s}"].reshape(16, 128).T
        convw = inputs[f"L0_attn_conv_w_{s}"].reshape(2048, 4)
        convw = convw.reshape(16, 128, 4).transpose(1, 0, 2).reshape(128, 64)
        smallsF[s] = np.ascontiguousarray(
            np.concatenate([A, dtb, convb, Dp, convw], axis=1), np.float32)

    in_maps = []
    for cid in range(8):
        keep = np.ones(7, np.float32)
        keep[0] = 0.0
        if cid < 7:
            keep[cid] = 0.0
        sf = np.zeros(7, np.float32)
        if cid >= 1:
            sf[cid - 1] = 1.0
        sr = np.zeros(7, np.float32)
        if cid < 7:
            sr[6] = 1.0
        mrow = np.concatenate([keep, sf, 1.0 - sf, sr, 1.0 - sr])
        masks = np.ascontiguousarray(np.tile(mrow, (128, 1)), np.float32)

        x0sched = np.zeros((9, 4, 512, 1024), np.float32)
        for b in range(4):
            for pos in range(7):
                if pos < cid:
                    x0sched[pos, b] = x_full[b, 512 * pos:512 * (pos + 1)]
                else:
                    k = pos - cid
                    x0sched[pos, b] = flip[b, 512 * k:512 * (k + 1)]
            x0sched[7, b] = x_full[b, 512 * cid:512 * (cid + 1)]
            x0sched[8, b] = flip[b, 512 * (7 - cid):512 * (8 - cid)]
        x0s2 = np.ascontiguousarray(np.concatenate(
            [x_full[b, 512 * cid:512 * (cid + 1)] for b in range(4)], 0))

        im = {nm: inputs[nm] for nm in WNAMES}
        for s in "fr":
            im[f"in_proj_{s}"] = inputs[f"L0_attn_in_proj_{s}"]
            im[f"out_proj_{s}"] = inputs[f"L0_attn_out_proj_{s}"]
            im[f"x_proj_{s}"] = inputs[f"L0_attn_x_proj_{s}"]
            im[f"dt_w_{s}"] = inputs[f"L0_attn_dt_w_{s}"]
            im[f"smallsF_{s}"] = smallsF[s]
        im["x0sched"] = np.ascontiguousarray(x0sched)
        im["sel16"] = sel
        im["masks"] = masks
        im["x0s"] = x0s2
        im["identF"] = identF
        im["identB"] = identB
        im["identJ"] = np.ascontiguousarray(identJ)
        in_maps.append(im)

    res = run_bass_kernel_spmd(nc, in_maps, core_ids=list(range(8)))

    out = np.zeros((4, 4096, 1024), np.float32)
    for cid in range(8):
        o = np.asarray(res.results[cid]["out"]).reshape(4, 512, 1024)
        out[:, 512 * cid:512 * (cid + 1), :] = o
    return out


# version 07
# speedup vs baseline: 1.1997x; 1.1997x over previous
import numpy as np
import ml_dtypes
import concourse.bass as bass
import concourse.mybir as mybir
from concourse.bass_utils import run_bass_kernel_spmd
from contextlib import ExitStack

"""Builders for the mk_seq_postln_d2 megakernel (dev module; inlined into kernel.py later)."""

F32 = mybir.dt.float32
BF16 = mybir.dt.bfloat16
AF = mybir.ActivationFunctionType
ALU = mybir.AluOpType
AX = mybir.AxisListType
BF = ml_dtypes.bfloat16
ENGINES = ("tensor", "vector", "scalar", "gpsimd", "sync")


class Prog:

    def __init__(self, auto_drain=False, inc_all=False):
        self.auto_drain = auto_drain
        self.inc_all = inc_all
        self.items = []
        self.ecnt = dict.fromkeys(ENGINES, 0)
        self.dcnt = {}

    def op(self, eng, fn, inc=False, drain=False, waits=()):
        if self.inc_all:
            inc = True
        self.items.append(("op", eng, fn, inc, drain, tuple(waits)))
        if inc:
            self.ecnt[eng] += 1
        return ("e", eng, self.ecnt[eng])

    def dma(self, eng, name, fn, waits=()):
        self.dcnt[name] = self.dcnt.get(name, 0) + 16
        self.items.append(("dma", eng, fn, name, tuple(waits)))
        return ("d", name, self.dcnt[name])

    def arm_dma(self, eng, name, fn, ndma, waits=()):
        self.dcnt[name] = self.dcnt.get(name, 0) + 16 * ndma
        self.items.append(("arm", eng, fn, name, tuple(waits), ndma))
        return ("d", name, self.dcnt[name])

    def now(self, eng):
        return ("e", eng, self.ecnt[eng])

    def dnow(self, name):
        return ("d", name, self.dcnt.get(name, 0))

    def barrier(self, engs):
        snaps = [self.now(e) for e in engs]
        out = []
        for e in engs:
            ws = [s for s in snaps if s[1] != e]
            self.items.append(("waits", e, ws))
        return snaps

    def wait(self, eng, *tokens):
        self.items.append(("waits", eng, tuple(tokens)))

    def build(self, nc, stack, verbose=False):
        esem = {e: stack.enter_context(nc.semaphore(f"es_{e}")) for e in ENGINES}
        dsem = {n: stack.enter_context(nc.semaphore(f"ds_{n}")) for n in self.dcnt}
        block = stack.enter_context(nc.Block())
        per = {e: [] for e in ENGINES}
        for it in self.items:
            per[it[1]].append(it)

        def run_engine(e, eng_items):
            seen = {}

            def handle(engine):
                cnt = 0
                dc = {}
                for it in eng_items:
                    if it[0] == "waits":
                        for tk in it[2]:
                            _wait(engine, tk)
                        continue
                    if it[0] == "op":
                        _, _, fn, inc, drain, waits = it
                        for tk in waits:
                            _wait(engine, tk)
                        instr = fn(engine)
                        if drain or (self.auto_drain and e in ("vector", "scalar", "gpsimd")):
                            instr = engine.drain()
                        if inc:
                            cnt += 1
                            (instr if instr is not None else engine.nop()).then_inc(esem[e], 1)
                    elif it[0] == "arm":
                        _, _, fn, name, waits, ndma = it
                        for tk in waits:
                            _wait(engine, tk)
                        for arm in engine.switch_core_id(n=128):
                            cid = arm.logical % 8
                            fn(engine, cid, dsem[name])
                    else:
                        _, _, fn, name, waits = it
                        for tk in waits:
                            _wait(engine, tk)
                        d = fn(engine)
                        d.then_inc(dsem[name], 16)

            def _wait(engine, tk):
                kind, key, val = tk
                if val <= 0:
                    return
                if kind == "e" and key == e:
                    return
                prev = seen.get((kind, key), -1)
                if val <= prev:
                    return
                seen[(kind, key)] = val
                sem = esem[key] if kind == "e" else dsem[key]
                engine.wait_ge(sem, val)

            return handle

        for e in ENGINES:
            getattr(block, e if e != "sync" else "sync")(run_engine(e, per[e]))
        return esem, dsem


def ceil_div(a, b):
    return (a + b - 1) // b


def build_mamba_fused(nc, p, stack, tens, psum):
    L, DM, DI, NS, DTR, Tc = 4096, 1024, 2048, 16, 64, 512
    NPOS = 9
    DTt = DI // 128
    KT = DM // 128
    MT = Tc // 128
    NP = DTR + 2 * NS

    sb = lambda n, s, d=F32: stack.enter_context(nc.sbuf_tensor(n, s, d))
    dram = lambda n, s, d=BF16: nc.dram_tensor(n, s, d, kind="Internal").ap()
    bank = lambda b, par=128: psum[0:par, b * 512:(b + 1) * 512]

    xbf = dram("xbf", [NPOS, 4, Tc, DM])
    inp_bf = {s: dram(f"inp_bf_{s}", [2 * DI, DM]) for s in "fr"}
    outp_bf = {s: dram(f"outp_bf_{s}", [DM, DI]) for s in "fr"}
    xp_bf = {s: dram(f"xp_bf_{s}", [128, DI]) for s in "fr"}
    dtw_bf = {s: dram(f"dtw_bf_{s}", [DI, 128]) for s in "fr"}
    inpT = {s: dram(f"inpT_{s}", [128, KT, 2 * DI]) for s in "fr"}
    bfb = {s: dram(f"bfb_{s}", [128, 4096]) for s in "fr"}
    outT = {s: dram(f"outT_{s}", [DI, DM]) for s in "fr"}
    sched_x = dram("sched_x", [7, 128, KT, DI])
    sched_bfb = dram("sched_bfb", [7, 128, 4096])
    sched_smF = nc.dram_tensor("sched_smF", [7, 128, 368], F32, kind="Internal").ap()
    scr = dram("scr", [2, 4, Tc, DM])

    slab_q = sb("slab_q", [128, 2, KT, 512], BF16)
    smallsB = sb("smallsB", [128, 4096], BF16)
    smallsF_sb = sb("smallsF_sb", [128, 368])
    A_sb = sb("A_sb", [128, DTt, NS])
    sel_sb = sb("sel_sb", [NS, NS * 128], BF16)
    masks_sb = sb("masks_sb", [128, 35])
    x0cT = sb("x0cT", [128, KT, Tc], BF16)
    conv_in = sb("conv_in", [128, 4, Tc + 3], BF16)
    halo = sb("halo", [128, DTt, 3])
    hF = sb("hF", [128, DTt, 3])
    hR = sb("hR", [128, DTt, 3])
    x_silu = sb("x_silu", [128, DTt, Tc], BF16)
    delta = sb("delta", [128, DTt, Tc], BF16)
    u_bf = sb("u_bf", [128, DTt, Tc], BF16)
    y_acc = sb("y_acc", [128, DTt, Tc])
    carry = sb("carry", [128, DTt * NS])
    cF = sb("cF", [128, DTt * NS])
    cR = sb("cR", [128, DTt * NS])
    dtr_bf = sb("dtr_bf", [DTR, Tc], BF16)
    Btile = sb("Btile", [NS, Tc], BF16)
    Ctile = sb("Ctile", [NS, Tc], BF16)
    dA_b = sb("dA_b", [128, 4, Tc])
    dBu_b = sb("dBu_b", [128, 8, Tc], BF16)
    h_b = sb("h_b", [128, DTt, Tc], BF16)
    m_b = sb("m_b", [128, 4, Tc], BF16)
    Bb_sb = sb("Bb_sb", [128, 2, Tc], BF16)
    A_pos = sb("A_pos", [128, DTt, NS])
    biasb = sb("biasb", [128, DTt, NS])
    P3 = sb("P3", [128, DTt, NS])
    hloc = sb("hloc", [128, DTt, NS])
    ones512 = sb("ones512", [128, Tc], BF16)
    zc = sb("zc", [128, 1])
    dce_sb = sb("dce_sb", [128, DTt])
    Bb16 = h_b
    z_bf = m_b
    tmp_c = dBu_b
    yq = delta
    attn_sb = sb("attn_sb", [128, DM], BF16)
    sp_t = dA_b
    sp_r = sb("sp_r", [128, 4, Tc], BF16)
    outT_t = sp_r.rearrange("p (a b) c -> p a (b c)", a=2)
    stage_t = slab_q[:, 0].rearrange("p a b -> p (a b)")
    halo_fl = halo.rearrange("p a b -> p (a b)")
    hF_fl = hF.rearrange("p a b -> p (a b)")
    hR_fl = hR.rearrange("p a b -> p (a b)")
    x_projT = smallsB[:, 0:2048].rearrange("p (a b) -> p a b", b=128)
    dt_wT = smallsB[:, 2048:4096]
    dtb_sb = smallsF_sb[:, 256:272]
    convb_sb = smallsF_sb[:, 272:288]
    D_sb = smallsF_sb[:, 288:304]
    convw_sb = smallsF_sb[:, 304:368].rearrange("p (a b) -> p a b", b=4)
    Alog_v = smallsF_sb[:, 0:256].rearrange("p (a b) -> p a b", b=NS)

    T, V, S, G, Y = "tensor", "vector", "scalar", "gpsimd", "sync"

    p.dma(G, "cast", lambda e: e.dma_start(out=xbf, in_=tens["x0sched"]))
    for s in "fr":
        p.dma(G, "cast", lambda e, s=s: e.dma_start(
            out=inp_bf[s], in_=tens[f"in_proj_{s}"]))
        p.dma(G, "cast", lambda e, s=s: e.dma_start(
            out=outp_bf[s], in_=tens[f"out_proj_{s}"]))
        p.dma(G, "cast", lambda e, s=s: e.dma_start(
            out=xp_bf[s][0:NP, :], in_=tens[f"x_proj_{s}"]))
        p.dma(G, "cast", lambda e, s=s: e.dma_start(
            out=dtw_bf[s][:, 0:DTR], in_=tens[f"dt_w_{s}"]))
    cast_done = p.dnow("cast")
    p.dma(Y, "ld", lambda e: e.dma_start(out=sel_sb[:], in_=tens["sel16"]))
    p.dma(Y, "ld", lambda e: e.dma_start(out=masks_sb[:], in_=tens["masks"]))
    ld_done = p.dnow("ld")
    for s in "fr":
        for k in range(KT):
            tpt = p.dma(Y, "tp", lambda e, s=s, k=k: e.dma_start(
                out=stage_t[:], in_=inp_bf[s][:, k * 128:(k + 1) * 128],
                transpose=True), waits=(cast_done, p.dnow("stg")))
            p.dma(G, "stg", lambda e, s=s, k=k: e.dma_start(
                out=inpT[s][:, k, :], in_=stage_t[:]), waits=(tpt,))
        for dt in range(DTt):
            tpt = p.dma(Y, "tp", lambda e, s=s, dt=dt: e.dma_start(
                out=stage_t[:, 0:128], in_=xp_bf[s][:, dt * 128:(dt + 1) * 128],
                transpose=True), waits=(cast_done, p.dnow("stg")))
            p.dma(G, "stg", lambda e, s=s, dt=dt: e.dma_start(
                out=bfb[s][:, dt * 128:(dt + 1) * 128], in_=stage_t[:, 0:128]),
                waits=(tpt,))
        tpt = p.dma(Y, "tp", lambda e, s=s: e.dma_start(
            out=stage_t[:, 0:2048], in_=dtw_bf[s], transpose=True),
            waits=(cast_done, p.dnow("stg")))
        p.dma(G, "stg", lambda e, s=s: e.dma_start(
            out=bfb[s][:, 2048:4096], in_=stage_t[:, 0:2048]), waits=(tpt,))
        for kt in range(DTt):
            tpt = p.dma(Y, "tp", lambda e, s=s, kt=kt: e.dma_start(
                out=stage_t[:, 0:DM], in_=outp_bf[s][:, kt * 128:(kt + 1) * 128],
                transpose=True), waits=(cast_done, p.dnow("stg")))
            p.dma(G, "stg", lambda e, s=s, kt=kt: e.dma_start(
                out=outT[s][kt * 128:(kt + 1) * 128, :], in_=stage_t[:, 0:DM]),
                waits=(tpt,))
    prep_done = (p.dnow("stg"), p.dnow("tp"), cast_done)
    for pos in range(7):
        def arm_fn(e, cid, sem, pos=pos):
            s = "f" if pos < cid else "r"
            e.dma_start(out=sched_x[pos], in_=inpT[s][:, :, 0:DI]).then_inc(sem, 16)
            e.dma_start(out=sched_bfb[pos], in_=bfb[s]).then_inc(sem, 16)
            return e.dma_start(out=sched_smF[pos],
                               in_=tens[f"smallsF_{s}"]).then_inc(sem, 16)
        p.arm_dma(Y, "sch", arm_fn, 3, waits=prep_done if pos == 0 else ())
    sch_done = p.dnow("sch")
    p.op(V, lambda e: e.memset(ones512[:], 1.0), inc=True)
    p.op(V, lambda e: e.memset(zc[:], 0.0), inc=True, drain=True)
    p.wait(T, ld_done, p.now(V), p.now(S))
    p.wait(V, ld_done)
    p.wait(S, ld_done)

    XB = (0, 1, 5, 6)
    for b in range(4):
      zs0 = p.op(V, lambda e: e.memset(carry[:], 0.0), inc=True)
      p.op(V, lambda e: e.memset(cF[:], 0.0), inc=True)
      p.op(V, lambda e: e.memset(cR[:], 0.0), inc=True)
      p.op(V, lambda e: e.memset(halo[:], 0.0), inc=True)
      p.op(V, lambda e: e.memset(hF[:], 0.0), inc=True)
      p.op(V, lambda e: e.memset(hR[:], 0.0), inc=True, drain=True)
      for pos in range(NPOS):
        own = pos >= 7
        dirx = 0 if pos == 7 else 1
        if pos < 7:
            smB_src, smF_src = sched_bfb[pos], sched_smF[pos]
        elif pos == 7:
            smB_src, smF_src = bfb["f"], tens["smallsF_f"]
        else:
            smB_src, smF_src = bfb["r"], tens["smallsF_r"]
        lw = (sch_done, p.now(T), p.now(V), p.now(S))
        p.dma(Y, "sml", lambda e, smB_src=smB_src: e.dma_start(
            out=smallsB[:], in_=smB_src), waits=lw)
        p.dma(Y, "sml", lambda e, smF_src=smF_src: e.dma_start(
            out=smallsF_sb[:], in_=smF_src))
        sml_ok = p.dnow("sml")
        e1 = p.op(S, lambda e: e.activation(A_pos[:], Alog_v, AF.Exp), inc=True,
                  waits=(sml_ok,))
        ae = p.op(V, lambda e: e.tensor_scalar(A_sb[:], A_pos[:], -1.0, None,
                                               ALU.mult), inc=True, waits=(e1,),
                  drain=True)
        if pos < 7:
            km = masks_sb[:, pos:pos + 1]
            p.op(V, lambda e, km=km: e.tensor_scalar(carry[:], carry[:], km, None,
                                                     ALU.mult), inc=True, drain=True)
            p.op(V, lambda e, km=km: e.tensor_scalar(halo_fl[:], halo_fl[:], km,
                                                     None, ALU.mult), inc=True,
                 drain=True)
        elif pos == 7:
            p.op(V, lambda e: e.tensor_copy(carry[:], cF[:]), inc=True, drain=True)
            p.op(V, lambda e: e.tensor_copy(halo[:], hF[:]), inc=True, drain=True)
        else:
            p.op(V, lambda e: e.tensor_copy(carry[:], cR[:]), inc=True, drain=True)
            p.op(V, lambda e: e.tensor_copy(halo[:], hR[:]), inc=True, drain=True)
        pro_v = p.now(V)

        xw = []
        for k in range(KT):
            xw.append(p.dma(Y, "x0c", lambda e, k=k, pos=pos, b=b: e.dma_start(
                out=x0cT[:, k, :], in_=xbf[pos, b, :, k * 128:(k + 1) * 128],
                transpose=True), waits=(cast_done,) if k == 0 else ()))
        x0c_ok = xw[-1]

        silu_snap = {}
        slab_war = {0: p.now(T), 1: p.now(T)}
        xslab_src = sched_x[pos] if pos < 7 else inpT["f" if pos == 7 else "r"]
        for g in range(0, DTt, 4):
            quad = (g, g + 1, g + 2, g + 3)
            buf = (g // 4) % 2
            sl = p.dma(Y, "slb", lambda e, buf=buf, g=g, xslab_src=xslab_src:
                       e.dma_start(out=slab_q[:, buf],
                                   in_=xslab_src[:, :, g * 128:g * 128 + 512]),
                       waits=(slab_war[buf], sch_done))
            mm_snap = {}
            for dt in quad:
                def mm_xz(e, dt=dt, buf=buf):
                    ins = None
                    for k in range(KT):
                        ins = e.matmul(bank(XB[dt % 4]),
                                       slab_q[:, buf, k, (dt % 4) * 128:(dt % 4 + 1) * 128],
                                       x0cT[:, k, :], start=(k == 0), stop=(k == KT - 1))
                    return ins
                mm_snap[dt] = p.op(T, mm_xz, inc=True,
                                   waits=(x0c_ok, sl, pro_v,
                                          silu_snap.get(g - 4, ("e", S, 0))))
            slab_war[buf] = mm_snap[quad[-1]]
            ev_snap = {}
            for dt in quad:
                ev_snap[dt] = p.op(S, lambda e, dt=dt: e.copy(
                    out=conv_in[:, dt % 4, 3:], in_=bank(XB[dt % 4])),
                    inc=True, waits=(mm_snap[dt],))
            for dt in quad:
                p.op(V, lambda e, dt=dt: e.tensor_copy(conv_in[:, dt % 4, 0:3], halo[:, dt, :]),
                     waits=(ev_snap[dt], silu_snap.get(g - 4, ("e", S, 0))))
            for j in range(4):
                for dt in quad:
                    if j == 0:
                        p.op(V, lambda e, dt=dt: e.tensor_scalar(
                            sp_t[:, dt % 4, :], conv_in[:, dt % 4, 0:Tc],
                            convw_sb[:, dt, 0:1], None, ALU.mult))
                    else:
                        p.op(V, lambda e, dt=dt, j=j: e.scalar_tensor_tensor(
                            sp_t[:, dt % 4, :], conv_in[:, dt % 4, j:Tc + j],
                            convw_sb[:, dt, j:j + 1], sp_t[:, dt % 4, :], ALU.mult, ALU.add))
            conv_v = {}
            for dt in quad:
                conv_v[dt] = p.op(V, lambda e, dt=dt: e.tensor_copy(
                    halo[:, dt, :], conv_in[:, dt % 4, Tc:Tc + 3]), inc=True)
            for dt in quad:
                silu_snap[g] = p.op(S, lambda e, dt=dt: e.activation(
                    x_silu[:, dt, :], sp_t[:, dt % 4, :], AF.Silu, bias=convb_sb[:, dt:dt + 1]),
                    inc=True, waits=(conv_v[dt],))
            for dt in quad:
                def mm_xdbl(e, dt=dt):
                    st = (dt == 0)
                    sp = (dt == DTt - 1)
                    e.matmul(bank(2, DTR), x_projT[:, dt, 0:DTR], x_silu[:, dt, :],
                             start=st, stop=sp)
                    e.matmul(bank(3, NS), x_projT[:, dt, DTR:DTR + NS], x_silu[:, dt, :],
                             start=st, stop=sp)
                    return e.matmul(bank(4, NS), x_projT[:, dt, DTR + NS:DTR + 2 * NS],
                                    x_silu[:, dt, :], start=st, stop=sp)
                p.op(T, mm_xdbl, inc=True, waits=(silu_snap[g],))
        xdbl_T = p.now(T)
        ev_dtr = p.op(S, lambda e: e.copy(out=dtr_bf[:], in_=bank(2, DTR)), inc=True,
                      waits=(xdbl_T,))
        ev_B = p.op(S, lambda e: e.copy(out=Btile[:], in_=bank(3, NS)), inc=True)
        ev_C = p.op(S, lambda e: e.copy(out=Ctile[:], in_=bank(4, NS)), inc=True)

        dmm = {}
        dv_hist = {}
        for g in range(0, DTt, 4):
            quad = (g, g + 1, g + 2, g + 3)
            for dt in quad:
                dmm[dt] = p.op(T, lambda e, dt=dt: e.matmul(
                    bank(XB[dt % 4]), dt_wT[0:DTR, dt * 128:(dt + 1) * 128], dtr_bf[:],
                    start=True, stop=True), inc=True,
                    waits=(ev_dtr, p.now(S)))
            sp_done = {}
            for dt in quad:
                p.op(S, lambda e, dt=dt: e.activation(
                    sp_t[:, dt % 4, 0:Tc], bank(XB[dt % 4]), AF.Abs, bias=dtb_sb[:, dt:dt + 1]),
                    waits=(dmm[dt], dv_hist.get(g - 4, ("e", V, 0))))
            for dt in quad:
                p.op(S, lambda e, dt=dt: e.activation(
                    sp_r[:, dt % 4, 0:Tc], bank(XB[dt % 4]),
                    AF.Relu, bias=dtb_sb[:, dt:dt + 1]))
            for dt in quad:
                p.op(S, lambda e, dt=dt: e.activation(
                    sp_t[:, dt % 4, 0:Tc], sp_t[:, dt % 4, 0:Tc], AF.Exp, scale=-1.0))
            for dt in quad:
                sp_done[dt] = p.op(S, lambda e, dt=dt: e.activation(
                    sp_t[:, dt % 4, 0:Tc], sp_t[:, dt % 4, 0:Tc], AF.Ln, bias=1.0), inc=True)
            dv = {}
            for dt in quad:
                dv[dt] = p.op(V, lambda e, dt=dt: e.tensor_tensor(
                    delta[:, dt, :], sp_r[:, dt % 4, 0:Tc],
                    sp_t[:, dt % 4, 0:Tc], ALU.add),
                    inc=True, waits=(sp_done[dt],))
            for dt in quad:
                p.op(V, lambda e, dt=dt: e.tensor_tensor(
                    u_bf[:, dt, :], delta[:, dt, :], x_silu[:, dt, :], ALU.mult), inc=True)
            dv_hist[g] = p.now(V)
        if own:
            yz = p.op(V, lambda e: e.memset(y_acc[:], 0.0), inc=True)
        p.barrier((T, V, S, G))

        carry3 = carry.rearrange("p (a b) -> p a b", b=NS)
        if own:
            scan_v_snap = {}
            dbu_tok = {}
            ebB_hist = {}
            n_last_v = {}
            pend = []

            def emit_carry(nq):
                cn, cg = nq
                p.op(V, lambda e, cn=cn, cg=cg: e.tensor_copy(
                    carry3[:, cg:cg + 4, cn:cn + 1],
                    h_b[:, cg:cg + 4, Tc - 1:Tc]), inc=True)

            for n in range(NS):
                bB = 2 if n % 2 == 0 else 3
                bC = 7 if n % 2 == 0 else 4
                war = n_last_v.get(n - 2, ("e", V, 0))
                war2 = ebB_hist.get(n - 2, ("e", S, 0))
                selL = sel_sb[:, n * 128:(n + 1) * 128]
                bc1 = p.op(T, lambda e, bB=bB, selL=selL: e.matmul(
                    bank(bB), selL, Btile[:], start=True, stop=True),
                    inc=True, waits=(ev_B, ev_C, war, war2))
                bc2 = p.op(T, lambda e, bC=bC, selL=selL: e.matmul(
                    bank(bC), selL, Ctile[:], start=True, stop=True), inc=True)
                dbu_war = dbu_tok.get((n - 2, DTt - 1), ("e", G, 0))
                ebB = p.op(S, lambda e, n=n, bB=bB: e.copy(
                    out=Bb_sb[:, n % 2, :], in_=bank(bB)), inc=True,
                    waits=(bc1, dbu_war))
                ebB_hist[n] = ebB
                for g in range(0, DTt, 4):
                    quad = (g, g + 1, g + 2, g + 3)
                    da = {}
                    for dt in quad:
                        prev = (scan_v_snap.get((n, dt - 4))
                                or scan_v_snap.get((n - 1, DTt - 4 + (dt % 4)))
                                or ("e", V, 0))
                        da[dt] = p.op(S, lambda e, dt=dt, n=n: e.activation(
                            dA_b[:, dt % 4, :], delta[:, dt, :], AF.Exp,
                            scale=A_sb[:, dt, n:n + 1]), inc=True, waits=(prev,))
                    for dt in quad:
                        wprev = (scan_v_snap.get((n, dt - 8))
                                 or scan_v_snap.get((n - 1, (dt + 8) % DTt))
                                 or ("e", V, 0))
                        dbu_tok[(n, dt)] = p.op(G, lambda e, dt=dt, n=n: e.tensor_tensor(
                            dBu_b[:, dt % 8, :], u_bf[:, dt, :], Bb_sb[:, n % 2, :],
                            ALU.mult), inc=True, waits=(ebB, wprev))
                    for dt in quad:
                        scan_v_snap[(n, dt)] = p.op(V, lambda e, dt=dt, n=n: e.tensor_tensor_scan(
                            h_b[:, dt, :], dA_b[:, dt % 4, :], dBu_b[:, dt % 8, :],
                            carry3[:, dt, n:n + 1], ALU.mult, ALU.add), inc=True,
                            waits=(da[dt], dbu_tok[(n, dt)]))
                    for dt in quad:
                        p.op(V, lambda e, dt=dt, bC=bC: e.tensor_tensor(
                            m_b[:, dt % 4, :], h_b[:, dt, :], bank(bC), ALU.mult),
                            waits=(bc2,) if dt == g else ())
                    if pend:
                        emit_carry(pend.pop(0))
                    pend.append((n, g))
                    for dt in quad:
                        p.op(V, lambda e, dt=dt: e.tensor_tensor(
                            y_acc[:, dt, :], y_acc[:, dt, :], m_b[:, dt % 4, :], ALU.add),
                            inc=(dt == DTt - 1))
                n_last_v[n] = p.now(V)
            for _ in range(2):
                p.op(V, lambda e: e.memset(m_b[0:1, 0, 0:1], 0.0), inc=True)
            emit_carry(pend.pop(0))
        else:
            ebB_tok = {}
            last_dbu = ("e", V, 0)
            for n in range(NS):
                bB = 2 if n % 2 == 0 else 3
                selL = sel_sb[:, n * 128:(n + 1) * 128]
                war2 = ebB_tok.get(n - 2, ("e", S, 0))
                bc1 = p.op(T, lambda e, bB=bB, selL=selL: e.matmul(
                    bank(bB), selL, Btile[:], start=True, stop=True),
                    inc=True, waits=(ev_B, ev_C, war2) if n < 2 else (war2,))
                ebB_tok[n] = p.op(S, lambda e, n=n, bB=bB: e.copy(
                    out=Bb16[:, n, :], in_=bank(bB)), inc=True, waits=(bc1,))
            ebB_all = p.now(S)
            for dt in range(DTt):
                p.op(V, lambda e, dt=dt: e.tensor_reduce(
                    dce_sb[:, dt:dt + 1], delta[:, dt, :], AX.X, ALU.add), inc=True)
            for dt in range(DTt):
                p.op(V, lambda e, dt=dt: e.tensor_scalar(
                    biasb[:, dt, :], A_sb[:, dt, :], dce_sb[:, dt:dt + 1], None,
                    ALU.mult), inc=True)
            bias_done = p.op(S, lambda e: e.activation(
                P3[:], biasb[:], AF.Exp), inc=True, waits=(p.now(V),))
            cum_tok = {}
            wS_last = {}
            wu_last = {}
            pend_red = []
            qidx = 0
            for dt in range(DTt):
                dcv = dA_b[:, dt % 2, :]
                cum_tok[dt] = p.op(V, lambda e, dt=dt, dcv=dcv: e.tensor_tensor_scan(
                    dcv, ones512[:], delta[:, dt, :], zc[:], ALU.mult, ALU.add),
                    inc=True, waits=(wS_last.get(dt - 2, ("e", S, 0)),))
                for n4 in range(0, NS, 4):
                    mbuf = m_b if qidx % 2 == 0 else dBu_b
                    qidx += 1
                    wq = {}
                    for n in range(n4, n4 + 4):
                        ws = ()
                        if n == n4:
                            ws = (cum_tok[dt], bias_done,
                                  wu_last.get((dt, n4 - 4))
                                  or wu_last.get((dt - 1, NS - 4))
                                  or ("e", V, 0))
                        wq[n] = p.op(S, lambda e, dt=dt, n=n, dcv=dcv: e.activation(
                            sp_r[:, n % 4, :], dcv, AF.Exp,
                            scale=A_pos[:, dt, n:n + 1],
                            bias=biasb[:, dt, n:n + 1]), inc=True, waits=ws)
                    for n in range(n4, n4 + 4):
                        tok = p.op(V, lambda e, dt=dt, n=n, mbuf=mbuf: e.tensor_tensor(
                            mbuf[:, n % 4, :], sp_r[:, n % 4, :], u_bf[:, dt, :],
                            ALU.mult), inc=True, waits=(wq[n], ebB_all))
                        if n == n4 + 3:
                            wu_last[(dt, n4)] = tok
                    if pend_red:
                        p.op(V, pend_red.pop(0), inc=True)
                    for n in range(n4, n4 + 4):
                        p.op(V, lambda e, dt=dt, n=n, mbuf=mbuf: e.tensor_tensor(
                            mbuf[:, n % 4, :], mbuf[:, n % 4, :], Bb16[:, n, :],
                            ALU.mult), inc=True)
                    pend_red.append(lambda e, dt=dt, n4=n4, mbuf=mbuf: e.tensor_reduce(
                        hloc[:, dt, n4:n4 + 4], mbuf[:, 0:4, :], AX.X, ALU.add))
                    wS_last[dt] = wq[n4 + 3]
            for _ in range(3):
                p.op(V, lambda e: e.memset(zc[:], 0.0), inc=True)
            p.op(V, pend_red.pop(0), inc=True)
            for dt in range(DTt):
                p.op(V, lambda e, dt=dt: e.tensor_tensor(
                    carry3[:, dt, :], carry3[:, dt, :], P3[:, dt, :], ALU.mult),
                    inc=True, waits=(bias_done,) if dt == 0 else ())
            for dt in range(DTt):
                p.op(V, lambda e, dt=dt: e.tensor_tensor(
                    carry3[:, dt, :], carry3[:, dt, :], hloc[:, dt, :], ALU.add),
                    inc=True)
        p.barrier((T, V, S, G))

        if own:
            zset = inpT["f" if pos == 7 else "r"]
            z_war = {0: p.now(T), 1: p.now(T)}
            for g in range(0, DTt, 4):
                quad = (g, g + 1, g + 2, g + 3)
                buf = (g // 4) % 2
                zl = p.dma(Y, "slb", lambda e, buf=buf, g=g, zset=zset: e.dma_start(
                    out=slab_q[:, buf],
                    in_=zset[:, :, DI + g * 128:DI + g * 128 + 512]),
                    waits=(z_war[buf],))
                zmm = {}
                for dt in quad:
                    def mm_z(e, dt=dt, buf=buf):
                        ins = None
                        for k in range(KT):
                            ins = e.matmul(bank(XB[dt % 4]),
                                           slab_q[:, buf, k, (dt % 4) * 128:(dt % 4 + 1) * 128],
                                           x0cT[:, k, :], start=(k == 0), stop=(k == KT - 1))
                        return ins
                    zmm[dt] = p.op(T, mm_z, inc=True, waits=(zl, p.now(V), p.now(S)))
                z_war[buf] = zmm[quad[-1]]
                zs = {}
                for dt in quad:
                    zs[dt] = p.op(S, lambda e, dt=dt: e.activation(
                        z_bf[:, dt % 4, :], bank(XB[dt % 4]), AF.Silu), inc=True, waits=(zmm[dt],))
                for dt in quad:
                    p.op(V, lambda e, dt=dt: e.tensor_scalar(
                        tmp_c[:, dt % 4, :], x_silu[:, dt, :], D_sb[:, dt:dt + 1], None, ALU.mult),
                        waits=(zs[dt],))
                for dt in quad:
                    p.op(V, lambda e, dt=dt: e.tensor_tensor(
                        tmp_c[:, dt % 4, :], y_acc[:, dt, :], tmp_c[:, dt % 4, :], ALU.add))
                for dt in quad:
                    p.op(V, lambda e, dt=dt: e.tensor_tensor(
                        yq[:, dt, :], tmp_c[:, dt % 4, :], z_bf[:, dt % 4, :], ALU.mult), inc=True)
            p.barrier((T, V, S))

            outT_src = outT["f" if pos == 7 else "r"]
            dmm2 = {}
            for dt in range(DTt):
                lw = (dmm2[dt - 2],) if dt - 2 in dmm2 else (p.now(V),)
                otw = p.dma(Y, f"outT{dt % 2}", lambda e, dt=dt, outT_src=outT_src: e.dma_start(
                    out=outT_t[:, dt % 2, :], in_=outT_src[dt * 128:(dt + 1) * 128, :]),
                    waits=lw)

                def mm_out(e, dt=dt):
                    st = (dt == 0)
                    sp = (dt == DTt - 1)
                    ins = None
                    for m in range(MT):
                        for hh in range(2):
                            ins = e.matmul(bank(m * 2 + hh), yq[:, dt, m * 128:(m + 1) * 128],
                                           outT_t[:, dt % 2, hh * 512:(hh + 1) * 512],
                                           start=st, stop=sp)
                    return ins
                dmm2[dt] = p.op(T, mm_out, inc=True, waits=(otw,))
            out_T = p.now(T)
            for m in range(MT):
                evm = p.op(S, lambda e, m=m: e.copy(out=attn_sb[:], in_=psum[:, m * 1024:(m + 1) * 1024]),
                           inc=True, waits=(out_T, p.dnow("st")))
                p.dma(G, "st", lambda e, m=m, dirx=dirx, b=b: e.dma_start(
                    out=scr[dirx, b, m * 128:(m + 1) * 128, :], in_=attn_sb[:]),
                    waits=(evm,))
        else:
            sfm = masks_sb[:, 7 + pos:8 + pos]
            nsf = masks_sb[:, 14 + pos:15 + pos]
            srm = masks_sb[:, 21 + pos:22 + pos]
            nsr = masks_sb[:, 28 + pos:29 + pos]
            p.op(V, lambda e: e.tensor_scalar(cF[:], cF[:], nsf, None, ALU.mult),
                 inc=True, drain=True)
            p.op(V, lambda e: e.scalar_tensor_tensor(cF[:], carry[:], sfm, cF[:],
                                                     ALU.mult, ALU.add), inc=True,
                 drain=True)
            p.op(V, lambda e: e.tensor_scalar(cR[:], cR[:], nsr, None, ALU.mult),
                 inc=True, drain=True)
            p.op(V, lambda e: e.scalar_tensor_tensor(cR[:], carry[:], srm, cR[:],
                                                     ALU.mult, ALU.add), inc=True,
                 drain=True)
            p.op(V, lambda e: e.tensor_scalar(hF_fl[:], hF_fl[:], nsf, None, ALU.mult),
                 inc=True, drain=True)
            p.op(V, lambda e: e.scalar_tensor_tensor(hF_fl[:], halo_fl[:], sfm, hF_fl[:],
                                                     ALU.mult, ALU.add), inc=True,
                 drain=True)
            p.op(V, lambda e: e.tensor_scalar(hR_fl[:], hR_fl[:], nsr, None, ALU.mult),
                 inc=True, drain=True)
            p.op(V, lambda e: e.scalar_tensor_tensor(hR_fl[:], halo_fl[:], srm, hR_fl[:],
                                                     ALU.mult, ALU.add), inc=True,
                 drain=True)
        p.barrier((T, V, S, G, Y))
    p.wait(G, p.dnow("st"))
    p.op(G, lambda e: e.nop())
    return scr



"""Phase-B builders: x1=LN(x0+pf+pr); L0_ffn; L1_attn (GLA block); L1_ffn; final LN.

Token layout per core: 2048 tokens = 4 batches x 512 seq positions, b-major
(row = b*512 + l_local). 16 token-tiles of 128. GLA "time" = batch axis (4
steps); sub s (s=0..3) covers tiles {s, 4+s, 8+s, 12+s} = same l-range, b=0..3.
Canonical activation layout: (token-part 128, feature-free) f32 residual.
Matmul inputs are transposed on demand via TensorE into xT (feature-part,
token-free) bf16.

PSUM rule: one accumulation group per bank (first_mm clears the whole bank).
Same-engine dependent V/S ops carry drain=True (v1; modest op counts here).
"""

F32 = mybir.dt.float32
BF16 = mybir.dt.bfloat16
AF = mybir.ActivationFunctionType
ALU = mybir.AluOpType
AX = mybir.AxisListType
BF = ml_dtypes.bfloat16

T, V, S, G, Y = "tensor", "vector", "scalar", "gpsimd", "sync"

D = 1024
TT = 16
KT = 8
N = 2048
NH = 16
DK = 64
DHID = 4096
E0, R0 = 6, 16
E1, F1, FS = 8, 512, 1024
GLA_TEMP = 16.0
DBG_SUB = 0


class Ctx:
    pass


def alloc_common(nc, p, stack, tens, psum):
    c = Ctx()
    c.nc, c.p, c.stack, c.tens = nc, p, stack, tens
    sb = lambda n, s, d=F32: stack.enter_context(nc.sbuf_tensor(n, s, d))
    dram = lambda n, s, d=BF16: nc.dram_tensor(n, s, d, kind="Internal").ap()
    c.dram = dram

    c.resid = sb("resid", [128, TT, D])
    c.dacc = sb("dacc", [128, TT, D], BF16)
    c.xT = sb("xT", [128, KT, N], BF16)
    c.dump = sb("dump", [128, 4, D])
    c.cols = sb("cols", [128, 64])
    c.arena = sb("arena", [128, 24, 512], BF16)
    c.qkv = c.arena[:, 0:12, :]
    c.qb = c.qkv[:, 0:4, :]
    c.kb = c.qkv[:, 4:8, :]
    c.vb = c.qkv[:, 8:12, :]
    c.ogT = c.arena[:, 0:8, :]
    c.og = c.arena[:, 12:20, :].rearrange("p (a b) c -> p a (b c)", b=2)
    c.slab_g = c.arena[:, 0:8, :]
    c.slab_u = c.arena[:, 8:16, :]
    c.slab_d = c.arena[:, 16:24, :].rearrange("p (a b) c -> p a (b c)", b=2)
    c.Pb = sb("Pb", [128, 4, 512], BF16)
    c.Pib = sb("Pib", [128, 4, 512], BF16)
    c.oac = None
    c.sc = sb("sc", [128, 12, 8])
    c.u1 = sb("u1", [128, 4, 512], BF16)
    c.gh_f = c.u1[:, 0:2, :]
    c.uh_f = c.u1[:, 2:4, :]
    c.ghT = c.u1[0:16].rearrange("p a b -> p (a b)")
    c.hT = sb("hT", [128, 512], BF16)
    c.hTs = sb("hTs", [16, 128], BF16)
    c.hTr = sb("hTr", [96, 128], BF16)
    c.w8 = sb("w8", [128, TT, 8])
    c.wbufA = sb("wbufA", [128, 2, 1024], BF16)
    c.wg1T = sb("wg1T", [128, KT, 16], BF16)
    c.wg2T = sb("wg2T", [128, D], BF16)
    c.sharedAT = sb("sharedAT", [128, KT, 16], BF16)
    c.sharedBT = sb("sharedBT", [128, D], BF16)
    c.routedAT = sb("routedAT", [128, KT, E0 * R0], BF16)
    c.routedBcat = sb("routedBcat", [96, D], BF16)
    c.routerWT = sb("routerWT", [128, KT, 16], BF16)
    c.router1 = sb("router1", [128, KT, 8], BF16)
    c.bg1c = sb("bg1c", [16, 1])
    c.rbc = sb("rbc", [128, 8])
    c.bcast = {}
    for nm in ("base_b", "bg2", "bog", "scale1", "scale2"):
        c.bcast[nm] = sb(f"bc_{nm}", [128, D], BF16)
    c.ones1 = sb("ones1", [1, 128])
    c.rowbuf = None
    c.identF = sb("identF_sb", [128, 128])
    c.epsc = sb("epsc", [128, 1])
    c.identB = sb("identB_sb", [128, 128], BF16)
    c.identJ = sb("identJ_sb", [128, 128], BF16)
    c.psum = psum
    c.bank = lambda b, par=128: c.psum[0:par, b * 512:(b + 1) * 512]
    _psB = c.psum.bitcast(BF16)
    c.bankB = lambda b, par=128: _psB[0:par, b * 1024:(b + 1) * 1024]
    c.nstat = 0
    c.rowbuf = c.dump[0:1, 0, :]
    c.oac = c.dump[:, 0:2, :].rearrange("p a (b f) -> p (a b) f", f=512)

    p.dma(Y, "wld", lambda e: e.dma_start(out=c.identF[:], in_=tens["identF"]))
    p.dma(Y, "wld", lambda e: e.dma_start(out=c.identB[:], in_=tens["identB"]))
    p.dma(Y, "wld", lambda e: e.dma_start(out=c.identJ[:], in_=tens["identJ"]))

    c.wdram = {}

    def cast(nm, src, shape):
        dd = dram(f"cst_{nm}", shape)
        p.dma(G, "cast", lambda e, dd=dd, src=src: e.dma_start(out=dd, in_=src))
        c.wdram[nm] = dd
        return dd

    c.trans_specs = []

    def prep_transposed(nm, src, rows, cols):
        raw = cast(nm + "_raw", src, [rows, cols])
        out = dram(nm + "T", [cols, rows])
        c.trans_specs.append((nm, raw, out, rows, cols))
        return out

    tn = tens
    c.baseWT = prep_transposed("baseW", tn["L0_ffn_base_W"], D, D)
    c.wqT = prep_transposed("wq", tn["L1_attn_wq"], D, D)
    c.wkT = prep_transposed("wk", tn["L1_attn_wk"], D, D)
    c.wvT = prep_transposed("wv", tn["L1_attn_wv"], D, D)
    c.wogT = prep_transposed("wog", tn["L1_attn_wog"], D, D)
    c.woT = prep_transposed("wo", tn["L1_attn_wo"], D, D)
    c.wgateT = prep_transposed("wgate", tn["L1_attn_w_gate"], DHID, D)
    c.wupT = prep_transposed("wup", tn["L1_attn_w_up"], DHID, D)
    c.wdownT = prep_transposed("wdown", tn["L1_attn_w_down"], D, DHID)
    c.f1gate = cast("f1gate", tn["L1_ffn_w_gate"], [E1, D, F1])
    c.f1up = cast("f1up", tn["L1_ffn_w_up"], [E1, D, F1])
    c.f1down = cast("f1down", tn["L1_ffn_w_down"], [E1, F1, D])
    c.wsgate = cast("wsgate", tn["L1_ffn_ws_gate"], [D, FS])
    c.wsup = cast("wsup", tn["L1_ffn_ws_up"], [D, FS])
    c.wsdown = cast("wsdown", tn["L1_ffn_ws_down"], [FS, D])
    c.sA_raw = cast("sA_raw", tn["L0_ffn_shared_A"], [R0, D])
    c.wg1_raw = cast("wg1_raw", tn["L1_attn_wg1"], [16, D])
    c.rW_raw = dram("rW_raw", [16, D])
    p.dma(G, "cast", lambda e: e.dma_start(out=c.rW_raw[0:E0, :],
                                           in_=tn["L0_ffn_router_W"]))
    c.wg2_pad = dram("wg2_pad", [D, 128])
    p.dma(G, "cast", lambda e: e.dma_start(out=c.wg2_pad[:, 0:16],
                                           in_=tn["L1_attn_wg2"]))
    c.sB_pad = dram("sB_pad", [D, 128])
    p.dma(G, "cast", lambda e: e.dma_start(out=c.sB_pad[:, 0:R0],
                                           in_=tn["L0_ffn_shared_B"]))
    c.rA_raw = []
    c.rB_pad = []
    for ei in range(E0):
        ra = dram(f"rA_raw{ei}", [R0, D])
        p.dma(G, "cast", lambda e, ra=ra, ei=ei: e.dma_start(
            out=ra, in_=tn["L0_ffn_routed_A"][ei]))
        c.rA_raw.append(ra)
        rb = dram(f"rB_pad{ei}", [D, 128])
        p.dma(G, "cast", lambda e, rb=rb, ei=ei: e.dma_start(
            out=rb[:, 0:R0], in_=tn["L0_ffn_routed_B"][ei]))
        c.rB_pad.append(rb)
    c.r1_raw = cast("r1_raw", tn["L1_ffn_router_w"], [D, 8])
    cast_done = p.dnow("cast")

    stg = c.wbufA[:, 0, :]
    for nm, raw, out, rows, cols in c.trans_specs:
        for kk in range(cols // 128):
            for r0 in range(0, rows, 1024):
                rc = min(1024, rows - r0)
                tpt = p.dma(Y, "tps", lambda e, raw=raw, kk=kk, r0=r0, rc=rc: e.dma_start(
                    out=stg[:, 0:rc], in_=raw[r0:r0 + rc, kk * 128:(kk + 1) * 128],
                    transpose=True), waits=(cast_done, p.dnow("stg")))
                p.dma(G, "stg", lambda e, out=out, kk=kk, r0=r0, rc=rc: e.dma_start(
                    out=out[kk * 128:(kk + 1) * 128, r0:r0 + rc], in_=stg[:, 0:rc]),
                    waits=(tpt,))
    for kk in range(KT):
        p.dma(Y, "tp", lambda e, kk=kk: e.dma_start(
            out=c.sharedAT[:, kk, :], in_=c.sA_raw[:, kk * 128:(kk + 1) * 128],
            transpose=True), waits=(cast_done,))
        p.dma(Y, "tp", lambda e, kk=kk: e.dma_start(
            out=c.wg1T[:, kk, :], in_=c.wg1_raw[:, kk * 128:(kk + 1) * 128],
            transpose=True), waits=(cast_done,))
        p.dma(Y, "tp", lambda e, kk=kk: e.dma_start(
            out=c.routerWT[:, kk, :], in_=c.rW_raw[:, kk * 128:(kk + 1) * 128],
            transpose=True), waits=(cast_done,))
        for ei in range(E0):
            p.dma(Y, "tp", lambda e, kk=kk, ei=ei: e.dma_start(
                out=c.routedAT[:, kk, ei * R0:(ei + 1) * R0],
                in_=c.rA_raw[ei][:, kk * 128:(kk + 1) * 128], transpose=True),
                waits=(cast_done,))
        p.dma(Y, "tp", lambda e, kk=kk: e.dma_start(
            out=c.wg2T[:, kk * 128:(kk + 1) * 128],
            in_=c.wg2_pad[kk * 128:(kk + 1) * 128, :], transpose=True),
            waits=(cast_done,))
        p.dma(Y, "tp", lambda e, kk=kk: e.dma_start(
            out=c.sharedBT[:, kk * 128:(kk + 1) * 128],
            in_=c.sB_pad[kk * 128:(kk + 1) * 128, :], transpose=True),
            waits=(cast_done,))
        p.dma(Y, "wld", lambda e, kk=kk: e.dma_start(
            out=c.router1[:, kk, :], in_=c.r1_raw[kk * 128:(kk + 1) * 128, :]),
            waits=(cast_done,))
    rb_stage = c.dacc[:, 0, :]
    for ei in range(E0):
        for kk in range(KT):
            tpt = p.dma(Y, "tpr", lambda e, ei=ei, kk=kk: e.dma_start(
                out=rb_stage[:, 0:128], in_=c.rB_pad[ei][kk * 128:(kk + 1) * 128, :],
                transpose=True), waits=(cast_done, p.dnow("rbmv")))
            p.dma(G, "rbmv", lambda e, ei=ei, kk=kk: e.dma_start(
                out=c.routedBcat[ei * R0:(ei + 1) * R0, kk * 128:(kk + 1) * 128],
                in_=rb_stage[0:R0, 0:128]), waits=(tpt,))

    mset = p.op(V, lambda e: e.memset(c.ones1[:], 1.0), inc=True)
    p.op(V, lambda e: e.memset(c.epsc[:], 1e-5), inc=True, drain=True)
    for nm, src in (("base_b", tn["L0_ffn_base_b"]), ("bg2", tn["L1_attn_bg2"]),
                    ("bog", tn["L1_attn_bog"]), ("scale1", tn["L1_attn_scale1"]),
                    ("scale2", tn["L1_attn_scale2"])):
        bl = p.dma(Y, "rbl", lambda e, src=src: e.dma_start(
            out=c.rowbuf[:], in_=src.unsqueeze(0)), waits=(p.now(T),))

        def bc_mm(e):
            e.matmul(c.bank(0), c.ones1[:], c.rowbuf[:, 0:512], start=True, stop=True)
            return e.matmul(c.bank(1), c.ones1[:], c.rowbuf[:, 512:1024], start=True,
                            stop=True)
        mm = p.op(T, bc_mm, inc=True, waits=(bl, mset, p.now(S)))
        p.op(S, lambda e, nm=nm: e.copy(out=c.bcast[nm][:, 0:512], in_=c.bank(0)),
             waits=(mm,))
        p.op(S, lambda e, nm=nm: e.copy(out=c.bcast[nm][:, 512:1024], in_=c.bank(1)),
             inc=True)
    p.dma(Y, "wld", lambda e: e.dma_start(out=c.bg1c[:],
                                          in_=tn["L1_attn_bg1"].unsqueeze(1)))
    rb_ld = p.dma(Y, "rbl", lambda e: e.dma_start(
        out=c.rowbuf[:, 0:E0], in_=tn["L0_ffn_router_b"].unsqueeze(0)),
        waits=(p.now(T),))
    mm = p.op(T, lambda e: e.matmul(c.psum[0:128, 0:8], c.ones1[:], c.rowbuf[:, 0:8],
                                    start=True, stop=True), inc=True,
              waits=(rb_ld, p.now(S)))
    p.op(S, lambda e: e.copy(out=c.rbc[:], in_=c.psum[0:128, 0:8]), inc=True,
         waits=(mm,))
    p.barrier((T, V, S))
    p.wait(T, p.dnow("tp"), p.dnow("wld"), p.dnow("stg"), p.dnow("rbmv"), p.dnow("tps"), p.dnow("tpr"))
    p.wait(V, p.dnow("tp"), p.dnow("wld"))
    p.wait(S, p.dnow("tp"), p.dnow("wld"), p.dnow("rbmv"))
    p.wait(G, p.dnow("tp"))
    return c



def ln_apply(c, v_ap, out_ap, kind="ln", scale_bc=None, eps=1e-5, rms_eps=1e-6, vwait=()):
    p = c.p
    i = c.nstat % 4
    c.nstat += 1
    sq = c.cols[:, i * 8:i * 8 + 1]
    sm = c.cols[:, i * 8 + 1:i * 8 + 2]
    v1 = c.cols[:, i * 8 + 2:i * 8 + 3]
    v2 = c.cols[:, i * 8 + 3:i * 8 + 4]
    v3 = c.cols[:, i * 8 + 4:i * 8 + 5]
    d0 = c.dump[:, 2, :]
    d1 = c.dump[:, 3, :]
    s1 = p.op(S, lambda e: e.activation(d0, v_ap, AF.Square, accum_out=sq), inc=True,
               waits=tuple(vwait) + (p.now(V),))
    if kind == "ln":
        s2 = p.op(S, lambda e: e.activation(d1, v_ap, AF.Copy, accum_out=sm), inc=True)
        p.op(V, lambda e: e.tensor_scalar(sm, sm, 1.0 / D, None, ALU.mult), drain=True,
             waits=(s2,))
        p.op(V, lambda e: e.tensor_tensor(v1, sm, sm, ALU.mult), drain=True)
        vv2 = p.op(V, lambda e: e.scalar_tensor_tensor(v2, sq, 1.0 / D, v1, ALU.mult,
                                                       ALU.subtract), drain=True,
                   inc=True)
        s3 = p.op(S, lambda e: e.activation(v2, v2, AF.Sqrt, bias=c.epsc[:]), inc=True,
                  waits=(vv2,))
        p.op(V, lambda e: e.reciprocal(v3, v2), drain=True, waits=(s3,))
        r2 = p.op(V, lambda e: e.scalar_tensor_tensor(v1, sm, -1.0, v3, ALU.mult,
                                                      ALU.mult), drain=True, inc=True)
        return p.op(V, lambda e: e.tensor_scalar(out_ap, v_ap, v3, v1, ALU.mult,
                                                 ALU.add), inc=True, drain=True,
                    waits=(r2,))
    else:
        s3 = p.op(S, lambda e: e.activation(v2, sq, AF.Sqrt), inc=True,
                  waits=(s1, p.now(V)))
        p.op(V, lambda e: e.tensor_scalar(v2, v2, 1.0 / np.sqrt(D), rms_eps, ALU.mult,
                                          ALU.add), drain=True, waits=(s3,))
        r1 = p.op(V, lambda e: e.reciprocal(v3, v2), drain=True, inc=True)
        sa = p.op(V, lambda e: e.tensor_scalar(d1, v_ap, v3, None, ALU.mult), inc=True,
                  drain=True, waits=(r1,))
        return p.op(V, lambda e: e.tensor_tensor(out_ap, d1, scale_bc, ALU.mult),
                    inc=True, waits=(sa,), drain=True)


def transpose_act(c, src_fn, dst):
    p = c.p
    prev = {}
    for tt in range(TT):
        for k in range(KT):
            bnk = (tt * KT + k) % 4
            w = (prev[bnk],) if bnk in prev else ()
            mm = p.op(T, lambda e, tt=tt, k=k, bnk=bnk: e.transpose(
                c.bank(bnk)[:, 0:128], src_fn(tt)[:, k * 128:(k + 1) * 128],
                c.identF[:]), inc=True, waits=w)
            prev[bnk] = p.op(S, lambda e, tt=tt, k=k, bnk=bnk: e.copy(
                out=dst[:, k, tt * 128:(tt + 1) * 128], in_=c.bank(bnk)[:, 0:128]),
                inc=True, waits=(mm,))
    p.barrier((T, S, V))


def softmax_topk(c, psum_ap, ncand, keepk, wout_ap, renorm, bias_bc=None, waits=()):
    p = c.p
    i = c.nstat % 4
    c.nstat += 1
    mx = c.cols[:, i * 8 + 5:i * 8 + 6]
    sm = c.cols[:, i * 8 + 6:i * 8 + 7]
    s2 = c.cols[:, i * 8 + 7:i * 8 + 8]
    lg = c.dump[:, 2, 0:8]
    cnt = c.dump[:, 3, 0:8]
    if bias_bc is not None:
        p.op(V, lambda e: e.tensor_tensor(lg[:, 0:ncand], psum_ap,
                                          bias_bc[:, 0:ncand], ALU.add), drain=True,
             waits=tuple(waits))
    else:
        p.op(V, lambda e: e.tensor_copy(lg[:, 0:ncand], psum_ap), drain=True,
             waits=tuple(waits))
    p.op(V, lambda e: e.tensor_reduce(mx, lg[:, 0:ncand], AX.X, ALU.max, negate=True),
         drain=True, inc=True)
    sx = p.op(S, lambda e: e.activation(lg[:, 0:ncand], lg[:, 0:ncand], AF.Exp,
                                        bias=mx, accum_out=sm), inc=True,
              waits=(p.now(V),))
    p.op(V, lambda e: e.reciprocal(sm, sm), drain=True, waits=(sx,))
    p.op(V, lambda e: e.tensor_scalar(lg[:, 0:ncand], lg[:, 0:ncand], sm, None,
                                      ALU.mult), drain=True)
    p.op(V, lambda e: e.memset(cnt[:, 0:ncand], 0.0), drain=True)
    for j in range(ncand):
        p.op(V, lambda e, j=j: e.scalar_tensor_tensor(
            cnt[:, 0:ncand], lg[:, 0:ncand], lg[:, j:j + 1], cnt[:, 0:ncand],
            ALU.is_lt, ALU.add), drain=True)
    p.op(V, lambda e: e.tensor_scalar(cnt[:, 0:ncand], cnt[:, 0:ncand],
                                      float(keepk) - 0.5, None, ALU.is_lt), drain=True)
    if ncand < 8:
        p.op(V, lambda e: e.memset(wout_ap[:, ncand:8], 0.0), drain=True)
    last = p.op(V, lambda e: e.tensor_tensor(wout_ap[:, 0:ncand], lg[:, 0:ncand],
                                             cnt[:, 0:ncand], ALU.mult), inc=True,
                drain=True)
    if renorm:
        p.op(V, lambda e: e.tensor_reduce(s2, wout_ap[:, 0:ncand], AX.X, ALU.add),
             drain=True)
        p.op(V, lambda e: e.tensor_scalar(s2, s2, 1e-6, None, ALU.add), drain=True)
        p.op(V, lambda e: e.reciprocal(s2, s2), drain=True)
        last = p.op(V, lambda e: e.tensor_scalar(wout_ap[:, 0:ncand],
                                                 wout_ap[:, 0:ncand], s2, None,
                                                 ALU.mult), inc=True, drain=True)
    return last



def step1_fused(c, x0_ap, scr):
    p = c.p
    for tt in range(TT):
        b, q = tt // 4, tt % 4
        lw = (p.now(V), p.now(S), p.now(T))
        if tt == 0:
            lw = lw + (p.dnow("rbmv"), p.dnow("tpr"))
        l3 = p.dma(Y, "ld1", lambda e, tt=tt: e.dma_start(
            out=c.dump[:, 2, :], in_=x0_ap[tt * 128:(tt + 1) * 128, :]), waits=lw)
        pf_sb = c.dacc[:, tt, :]
        l1 = p.dma(Y, "ld1", lambda e, pf_sb=pf_sb, b=b, q=q: e.dma_start(
            out=pf_sb, in_=scr[0, b, q * 128:(q + 1) * 128, :]))
        prb = c.Pb[:, 0:2, :].rearrange("p a b -> p (a b)")
        l2 = p.dma(Y, "ld1", lambda e, prb=prb, b=b, q=q: e.dma_start(
            out=prb, in_=scr[1, b, (3 - q) * 128:(4 - q) * 128, :]))

        def mm_j(e, prb=prb):
            e.matmul(c.bank(0), c.identJ[:], prb[:, 0:512], start=True, stop=True)
            return e.matmul(c.bank(1), c.identJ[:], prb[:, 512:1024], start=True,
                            stop=True)
        mmj = p.op(T, mm_j, inc=True, waits=(l2, p.now(V), p.now(S)))
        v = c.dump[:, 0, :]
        p.op(V, lambda e, pf_sb=pf_sb: e.tensor_tensor(
            v, c.dump[:, 2, :], pf_sb, ALU.add), waits=(l1, l3), drain=True)
        p.op(V, lambda e: e.tensor_tensor(v[:, 0:512], v[:, 0:512], c.bank(0),
                                          ALU.add), waits=(mmj,), drain=True)
        vv = p.op(V, lambda e: e.tensor_tensor(v[:, 512:1024], v[:, 512:1024],
                                               c.bank(1), ALU.add),
                  inc=True, drain=True)
        ln_apply(c, v, c.resid[:, tt, :], kind="ln", vwait=(vv,))
    p.barrier((T, V, S))



def step2(c):
    p = c.p
    transpose_act(c, lambda tt: c.resid[:, tt, :], c.xT)
    for tt in range(TT):
        ts = tt * 128

        def mm_small(e, ts=ts):
            ins = None
            for k in range(KT):
                st, sp = (k == 0), (k == KT - 1)
                e.matmul(c.bank(4)[:, 0:16], c.xT[:, k, ts:ts + 128],
                         c.sharedAT[:, k, :], start=st, stop=sp)
                e.matmul(c.bank(6)[:, 0:96], c.xT[:, k, ts:ts + 128],
                         c.routedAT[:, k, :], start=st, stop=sp)
                ins = e.matmul(c.bank(5)[:, 0:8], c.xT[:, k, ts:ts + 128],
                               c.routerWT[:, k, 0:8], start=st, stop=sp)
            return ins
        mm = p.op(T, mm_small, inc=True, waits=(p.now(S), p.now(V)))
        softmax_topk(c, c.bank(5)[:, 0:E0], E0, 2, c.w8[:, tt, :], True, bias_bc=c.rbc,
                     waits=(mm,))
        hsb = c.gh_f.rearrange("p a b -> p (a b)")[:, 0:112]
        ev1 = p.op(S, lambda e, hsb=hsb: e.copy(out=hsb[:, 0:16], in_=c.bank(4)[:, 0:16]),
                   waits=(mm, p.now(V)))
        ev2 = p.op(S, lambda e, hsb=hsb: e.copy(out=hsb[:, 16:112],
                                                in_=c.bank(6)[:, 0:96]), inc=True)
        for ei in range(E0):
            hv = p.op(V, lambda e, ei=ei, hsb=hsb, tt=tt: e.tensor_scalar(
                hsb[:, 16 + ei * R0:16 + (ei + 1) * R0],
                hsb[:, 16 + ei * R0:16 + (ei + 1) * R0],
                c.w8[:, tt, ei:ei + 1], None, ALU.mult), drain=True, waits=(ev2,),
                inc=(ei == E0 - 1))
        mm2 = p.op(T, lambda e, hsb=hsb: e.transpose(c.bankB(7)[0:16, 0:128],
                                                     hsb[:, 0:16], c.identB[:]),
                   waits=(hv, p.now(S)))
        mm2b = p.op(T, lambda e, hsb=hsb: e.transpose(c.bankB(6)[0:96, 0:128],
                                                      hsb[:, 16:112], c.identB[:]),
                    inc=True)
        p.op(S, lambda e: e.copy(out=c.hTs[:], in_=c.bankB(7)[0:16, 0:128]),
             waits=(mm2b,))
        evT = p.op(S, lambda e: e.copy(out=c.hTr[:], in_=c.bankB(6)[0:96, 0:128]),
                   inc=True)

        def mm_main(e, ts=ts):
            ins = None
            for half in range(2):
                f0 = half * 512
                for k in range(KT):
                    e.matmul(c.bank(half), c.xT[:, k, ts:ts + 128],
                             c.wbufA[:, (half * KT + k) % 2, 0:512],
                             start=(k == 0), stop=False)
                e.matmul(c.bank(half), c.hT[0:16, 0:128], c.sharedBT[0:16, f0:f0 + 512],
                         start=False, stop=False)
                ins = e.matmul(c.bank(half), c.hT[16:112, 0:128],
                               c.routedBcat[:, f0:f0 + 512], start=False, stop=True)
            return ins
        slot_consumer = {0: p.now(T), 1: p.now(T)}
        for half in range(2):
            f0 = half * 512
            for k in range(KT):
                slot = (half * KT + k) % 2
                wld = p.dma(Y, f"ws{slot}", lambda e, k=k, f0=f0, slot=slot: e.dma_start(
                    out=c.wbufA[:, slot, 0:512],
                    in_=c.baseWT[k * 128:(k + 1) * 128, f0:f0 + 512]),
                    waits=(slot_consumer[slot],))
                slot_consumer[slot] = p.op(T, lambda e, k=k, half=half, ts=ts, slot=slot: e.matmul(
                    c.bank(half), c.xT[:, k, ts:ts + 128], c.wbufA[:, slot, 0:512],
                    start=(k == 0), stop=False), inc=True,
                    waits=(wld,) + ((evT,) if (half == 0 and k == 0) else ()))
            p.op(T, lambda e, half=half, f0=f0: e.matmul(
                c.bank(half), c.hTs[:], c.sharedBT[0:16, f0:f0 + 512],
                start=False, stop=False))
            p.op(T, lambda e, half=half, f0=f0: e.matmul(
                c.bank(half), c.hTr[:], c.routedBcat[:, f0:f0 + 512],
                start=False, stop=True), inc=True)
        mm3 = p.now(T)
        v = c.dump[:, 0, :]
        p.op(V, lambda e: e.tensor_tensor(v[:, 0:512], c.bank(0),
                                          c.bcast["base_b"][:, 0:512], ALU.add),
             waits=(mm3,))
        p.op(V, lambda e: e.tensor_tensor(v[:, 512:1024], c.bank(1),
                                          c.bcast["base_b"][:, 512:1024], ALU.add),
             drain=True)
        vv = p.op(V, lambda e, tt=tt: e.tensor_tensor(v, v, c.resid[:, tt, :], ALU.add),
                  inc=True, drain=True)
        ln_apply(c, v, c.resid[:, tt, :], kind="ln", vwait=(vv,))
    p.barrier((T, V, S))



def step3(c, upto3='all'):
    p = c.p
    if upto3 == 'rms1':
        for tt in range(TT):
            ln_apply(c, c.resid[:, tt, :], c.dacc[:, tt, :], kind="rms",
                     scale_bc=c.bcast["scale1"][:])
        p.barrier((T, V, S))
        return
    for tt in range(TT):
        ln_apply(c, c.resid[:, tt, :], c.dump[:, 1, :], kind="rms",
                 scale_bc=c.bcast["scale1"][:])
        for k in range(KT):
            bnk = k % 4
            mm = p.op(T, lambda e, k=k, bnk=bnk: e.transpose(
                c.bank(bnk)[:, 0:128], c.dump[:, 1, k * 128:(k + 1) * 128],
                c.identF[:]), inc=True, waits=(p.now(V), p.now(S)))
            p.op(S, lambda e, k=k, bnk=bnk, tt=tt: e.copy(
                out=c.xT[:, k, tt * 128:(tt + 1) * 128], in_=c.bank(bnk)[:, 0:128]),
                inc=True, waits=(mm,))
        p.barrier((T, V, S))
    for fc in range(4):
        f0 = fc * 512

        def mm_gh(e, f0=f0):
            ins = None
            for k in range(KT):
                ins = e.matmul(c.bank(4, 16), c.wg1T[:, k, :], c.xT[:, k, f0:f0 + 512],
                               start=(k == 0), stop=(k == KT - 1))
            return ins
        mm = p.op(T, mm_gh, inc=True, waits=(p.now(S),))
        p.op(S, lambda e, f0=f0: e.activation(c.ghT[:, f0:f0 + 512], c.bank(4, 16),
                                              AF.Silu, bias=c.bg1c[:]), inc=True,
             waits=(mm,))
    p.barrier((T, V, S))
    if upto3 == 'xt':
        p.op(V, lambda e: e.memset(c.dacc[:], 0.0), inc=True, drain=True)
        for k in range(KT):
            p.op(V, lambda e, k=k: e.tensor_copy(c.dacc[:, k, :],
                                                 c.xT[:, k, 0:1024]), inc=True,
                 drain=True)
        p.barrier((T, V, S))
        return
    p.op(V, lambda e: e.memset(c.dacc[:], 0.0), inc=True, drain=True)
    pairs = [(t, s) for t in range(4) for s in range(t + 1)]
    for sub in range(4):
        tiles = [sub + 4 * b for b in range(4)]
        for hh in range(2):
            h0 = hh * 512
            for bi in range(4):
                ts = tiles[bi] * 128
                mm = p.op(T, lambda e, bi=bi, ts=ts, h0=h0: e.matmul(
                    c.bank(bi), c.ghT[:, ts:ts + 128], c.wg2T[0:16, h0:h0 + 512],
                    start=True, stop=True), inc=True, waits=(p.now(S), p.now(V)))
                zf = c.dump[:, bi % 2, 0:512]
                zv = p.op(V, lambda e, bi=bi, zf=zf, h0=h0: e.tensor_tensor(
                    zf, c.bank(bi), c.bcast["bg2"][:, h0:h0 + 512], ALU.add),
                    inc=True, drain=True, waits=(mm,))
                p.op(S, lambda e, bi=bi, zf=zf: e.activation(
                    c.Pb[:, bi, :], zf, AF.Sigmoid, scale=1.0 / GLA_TEMP), inc=True,
                    waits=(zv,))
                se = p.op(S, lambda e, bi=bi, zf=zf: e.activation(
                    c.dump[:, 2 + bi % 2, 0:512], zf, AF.Exp, scale=-1.0 / GLA_TEMP),
                    inc=True)
                p.op(V, lambda e, bi=bi: e.tensor_scalar(
                    c.Pib[:, bi, :], c.dump[:, 2 + bi % 2, 0:512], 1.0, None, ALU.add),
                    inc=True, drain=True, waits=(se,))
            for bi in range(1, 4):
                p.op(V, lambda e, bi=bi: e.tensor_tensor(
                    c.Pb[:, bi, :], c.Pb[:, bi, :], c.Pb[:, bi - 1, :], ALU.mult),
                    drain=True)
            for bi in range(1, 4):
                p.op(V, lambda e, bi=bi: e.tensor_tensor(
                    c.Pib[:, bi, :], c.Pib[:, bi, :], c.Pib[:, bi - 1, :], ALU.mult),
                    drain=True, inc=(bi == 3))
            if upto3 == 'al1' and sub == DBG_SUB and hh == 0:
                p.barrier((T, V, S))
                return
            slot_consumer = {0: p.now(T), 1: p.now(T)}
            for wd, kind in ((c.wqT, "q"), (c.wkT, "k"), (c.wvT, "v"), (c.wogT, "og")):
                for k in range(KT):
                    slot = k % 2
                    wld = p.dma(Y, f"ws{slot}", lambda e, wd=wd, k=k, h0=h0,
                                slot=slot: e.dma_start(
                        out=c.wbufA[:, slot, 0:512],
                        in_=wd[k * 128:(k + 1) * 128, h0:h0 + 512]),
                        waits=(slot_consumer[slot],))

                    def mm_w(e, k=k, tiles=tiles):
                        ins = None
                        for bi in range(4):
                            ts = tiles[bi] * 128
                            ins = e.matmul(c.bank(4 + bi), c.xT[:, k, ts:ts + 128],
                                           c.wbufA[:, k % 2, 0:512], start=(k == 0),
                                           stop=(k == KT - 1))
                        return ins
                    last_mm = p.op(T, mm_w, inc=True,
                                   waits=(wld,) + ((p.now(S), p.now(V)) if k == 0 else ()))
                    slot_consumer[slot] = last_mm
                sil = {}
                for bi in range(4):
                    if kind == "q":
                        p.op(S, lambda e, bi=bi: e.copy(out=c.qb[:, bi, :],
                                                        in_=c.bank(4 + bi)), inc=True,
                             waits=(last_mm,))
                    elif kind == "k":
                        p.op(S, lambda e, bi=bi: e.copy(out=c.kb[:, bi, :],
                                                        in_=c.bank(4 + bi)), inc=True,
                             waits=(last_mm,))
                    elif kind == "v":
                        p.op(S, lambda e, bi=bi: e.copy(out=c.vb[:, bi, :],
                                                        in_=c.bank(4 + bi)), inc=True,
                             waits=(last_mm,))
                    else:
                        ov = p.op(V, lambda e, bi=bi, h0=h0: e.tensor_tensor(
                            c.dump[:, bi % 2, 0:512], c.bank(4 + bi),
                            c.bcast["bog"][:, h0:h0 + 512], ALU.add), inc=True,
                            drain=True,
                            waits=(last_mm,) + ((sil[bi - 2],) if bi - 2 in sil else ()))
                        sil[bi] = p.op(S, lambda e, bi=bi, h0=h0: e.activation(
                            c.og[:, bi, h0:h0 + 512], c.dump[:, bi % 2, 0:512],
                            AF.Silu), inc=True, waits=(ov,))
            p.barrier((T, V, S))
            if upto3 in ('q', 'k', 'v', 'og', 'alpha') and sub == DBG_SUB and hh == 0:
                src = {'q': c.qb, 'k': c.kb, 'v': c.vb, 'og': c.og, 'alpha': c.Pb}[upto3]
                wid = 1024 if upto3 == 'og' else 512
                for bi in range(4):
                    p.op(V, lambda e, bi=bi, src=src, wid=wid: e.tensor_copy(
                        c.dacc[:, bi, 0:wid], src[:, bi, 0:wid]), inc=True, drain=True)
                p.barrier((T, V, S))
                return
            for bi in range(4):
                pi = pairs.index((bi, bi))
                tm = c.dump[:, 2 + bi % 2, 0:512] if bi < 2 else c.dump[:, 2 + bi % 2, 512:1024]
                p.op(V, lambda e, bi=bi, tm=tm: e.tensor_tensor(
                    tm, c.qb[:, bi, :], c.kb[:, bi, :], ALU.mult), drain=True)
                p.op(V, lambda e, pi=pi, tm=tm: e.tensor_reduce(
                    c.sc[:, pi, :], tm.rearrange("p (h k) -> p h k", k=DK), AX.X,
                    ALU.add), drain=True, inc=True)
            for bi in range(4):
                p.op(V, lambda e, bi=bi: e.tensor_tensor(
                    c.qb[:, bi, :], c.qb[:, bi, :], c.Pb[:, bi, :], ALU.mult))
            for bi in range(4):
                p.op(V, lambda e, bi=bi: e.tensor_tensor(
                    c.kb[:, bi, :], c.kb[:, bi, :], c.Pib[:, bi, :], ALU.mult),
                    inc=(bi == 3))
            for pi, (t_, s_) in enumerate(pairs):
                if t_ == s_:
                    continue
                tm = c.dump[:, 2 + pi % 2, 0:512] if pi % 4 < 2 else c.dump[:, 2 + pi % 2, 512:1024]
                p.op(V, lambda e, t_=t_, s_=s_, tm=tm: e.tensor_tensor(
                    tm, c.qb[:, t_, :], c.kb[:, s_, :], ALU.mult), drain=True)
                p.op(V, lambda e, pi=pi, tm=tm: e.tensor_reduce(
                    c.sc[:, pi, :], tm.rearrange("p (h k) -> p h k", k=DK), AX.X,
                    ALU.add), drain=True, inc=True)
            p.op(V, lambda e: e.memset(c.oac[:], 0.0), drain=True)
            for pi, (t_, s_) in enumerate(pairs):
                tm = c.dump[:, 2 + pi % 2, 0:512] if pi % 4 < 2 else c.dump[:, 2 + pi % 2, 512:1024]
                scbc = c.sc[:, pi, :].unsqueeze(2).to_broadcast([128, 8, DK])
                p.op(V, lambda e, s_=s_, tm=tm, scbc=scbc: e.tensor_tensor(
                    tm.rearrange("p (h k) -> p h k", k=DK), scbc,
                    c.vb[:, s_, :].rearrange("p (h k) -> p h k", k=DK), ALU.mult),
                    drain=True)
                p.op(V, lambda e, t_=t_, tm=tm: e.tensor_tensor(
                    c.oac[:, t_, :], c.oac[:, t_, :], tm, ALU.add), drain=True)
            if upto3 == 'o' and sub == DBG_SUB and hh == 0:
                for bi in range(4):
                    p.op(V, lambda e, bi=bi: e.tensor_copy(
                        c.dacc[:, bi, 0:512], c.oac[:, bi, :]), inc=True, drain=True)
                p.barrier((T, V, S))
                return
            for bi in range(4):
                i8 = c.sc[:, 10, :]
                r8 = c.sc[:, 11, :]
                t8 = c.cols[:, 40:48]
                ov = c.oac[:, bi, :]
                o3 = ov.rearrange("p (h k) -> p h k", k=DK)
                p.op(V, lambda e, o3=o3, i8=i8: e.tensor_reduce(i8, o3, AX.X, ALU.add),
                     drain=True, inc=True)
                sq = c.dump[:, 2, 0:512]
                sqo = p.op(S, lambda e, ov=ov, sq=sq: e.activation(sq, ov, AF.Square),
                           inc=True, waits=(p.now(V),))
                p.op(V, lambda e, sq=sq, r8=r8: e.tensor_reduce(
                    r8, sq.rearrange("p (h k) -> p h k", k=DK), AX.X, ALU.add),
                    drain=True, waits=(sqo,))
                p.op(V, lambda e, i8=i8: e.tensor_scalar(i8, i8, 1.0 / DK, None,
                                                         ALU.mult), drain=True)
                p.op(V, lambda e, i8=i8, t8=t8: e.tensor_tensor(t8, i8, i8, ALU.mult),
                     drain=True)
                p.op(V, lambda e, r8=r8, t8=t8: e.scalar_tensor_tensor(
                    r8, r8, 1.0 / DK, t8, ALU.mult, ALU.subtract), drain=True)
                sr = p.op(S, lambda e, r8=r8: e.activation(r8, r8, AF.Sqrt, bias=c.epsc[:]),
                          inc=True, waits=(p.now(V),))
                p.op(V, lambda e, r8=r8: e.reciprocal(r8, r8), drain=True, waits=(sr,))
                p.op(V, lambda e, o3=o3, i8=i8: e.tensor_tensor(
                    o3, o3, i8.unsqueeze(2).to_broadcast([128, 8, DK]), ALU.subtract),
                    drain=True)
                p.op(V, lambda e, o3=o3, r8=r8: e.tensor_tensor(
                    o3, o3, r8.unsqueeze(2).to_broadcast([128, 8, DK]), ALU.mult),
                    drain=True)
                p.op(V, lambda e, bi=bi, ov=ov, h0=h0: e.tensor_tensor(
                    c.og[:, bi, h0:h0 + 512], ov, c.og[:, bi, h0:h0 + 512], ALU.mult),
                    inc=True, drain=True)
            p.barrier((V, S, T))
            if upto3 == 'oln' and sub == DBG_SUB and hh == 0:
                for bi in range(4):
                    p.op(V, lambda e, bi=bi: e.tensor_copy(
                        c.dacc[:, bi, 0:512], c.oac[:, bi, :]), inc=True, drain=True)
                p.barrier((T, V, S))
                return
        if upto3 == 'og2' and sub == DBG_SUB:
            for bi in range(4):
                p.op(V, lambda e, bi=bi: e.tensor_copy(
                    c.dacc[:, bi, :], c.og[:, bi, :]), inc=True, drain=True)
            p.barrier((T, V, S))
            return
        if upto3 == 'hh1' and sub == DBG_SUB:
            p.barrier((T, V, S))
            return
        for bi in range(4):
            for k in range(KT):
                bnk = k % 4
                mm = p.op(T, lambda e, bi=bi, k=k, bnk=bnk: e.transpose(
                    c.bankB(bnk)[:, 0:128], c.og[:, bi, k * 128:(k + 1) * 128],
                    c.identB[:]), inc=True, waits=(p.now(S), p.now(V)))
                p.op(S, lambda e, bi=bi, k=k, bnk=bnk: e.copy(
                    out=c.ogT[:, k, bi * 128:(bi + 1) * 128],
                    in_=c.bankB(bnk)[:, 0:128]), inc=True, waits=(mm,))
        p.barrier((T, S, V))
        slot_consumer = {0: p.now(T), 1: p.now(T)}
        for bi in range(4):
            tt = tiles[bi]
            for k in range(KT):
                slot = k % 2
                wld = p.dma(Y, f"ws{slot}", lambda e, k=k, slot=slot: e.dma_start(
                    out=c.wbufA[:, slot, :], in_=c.woT[k * 128:(k + 1) * 128, :]),
                    waits=(slot_consumer[slot],))

                def mm_wo(e, bi=bi, k=k, slot=slot):
                    e.matmul(c.bank(4), c.ogT[:, k, bi * 128:(bi + 1) * 128],
                             c.wbufA[:, slot, 0:512], start=(k == 0),
                             stop=(k == KT - 1))
                    return e.matmul(c.bank(5), c.ogT[:, k, bi * 128:(bi + 1) * 128],
                                    c.wbufA[:, slot, 512:1024], start=(k == 0),
                                    stop=(k == KT - 1))
                slot_consumer[slot] = p.op(T, mm_wo, inc=True,
                     waits=(wld,) + ((p.now(V), p.now(S)) if k == 0 else ()))
            ev = p.op(V, lambda e, tt=tt: e.tensor_copy(c.dacc[:, tt, 0:512],
                                                        c.bank(4)), drain=True,
                      waits=(p.now(T),))
            p.op(V, lambda e, tt=tt: e.tensor_copy(c.dacc[:, tt, 512:1024],
                                                   c.bank(5)), inc=True, drain=True)
        if upto3 == 'wo1' and sub == DBG_SUB:
            p.barrier((T, V, S))
            return
    p.barrier((T, V, S))
    if upto3 == 'gla':
        return
    for tt in range(TT):
        v = c.dump[:, 1, :]
        vv = p.op(V, lambda e, tt=tt: e.tensor_tensor(v, c.resid[:, tt, :],
                                                      c.dacc[:, tt, :], ALU.add),
                  inc=True, drain=True, waits=(p.now(S),))
        ln_apply(c, v, c.dump[:, 0, :], kind="rms", scale_bc=c.bcast["scale2"][:],
                 vwait=(vv,))
        for k in range(KT):
            bnk = k % 4
            mm = p.op(T, lambda e, k=k, bnk=bnk: e.transpose(
                c.bank(bnk)[:, 0:128], c.dump[:, 0, k * 128:(k + 1) * 128],
                c.identF[:]), inc=True, waits=(p.now(V), p.now(S)))
            p.op(S, lambda e, k=k, bnk=bnk, tt=tt: e.copy(
                out=c.xT[:, k, tt * 128:(tt + 1) * 128], in_=c.bank(bnk)[:, 0:128]),
                inc=True, waits=(mm,))
        p.barrier((T, V, S))
    if upto3 == 'rms2':
        return
    ffn_slab(c, [("g", c.wgateT, c.wupT, c.wdownT, None, DHID // 512)], c.dacc, step=3)
    p.barrier((T, V, S))
    for tt in range(TT):
        v = c.dump[:, 1, :]
        vv = p.op(V, lambda e, tt=tt: e.scalar_tensor_tensor(
            v, c.resid[:, tt, :], 2.0, c.dacc[:, tt, :], ALU.mult, ALU.add),
            inc=True, drain=True, waits=(p.now(S),))
        ln_apply(c, v, c.resid[:, tt, :], kind="ln", vwait=(vv,))
    p.barrier((T, V, S))



def step4(c, out_ap):
    p = c.p
    transpose_act(c, lambda tt: c.resid[:, tt, :], c.xT)
    for tt in range(TT):
        ts = tt * 128

        def mm_r(e, ts=ts):
            ins = None
            for k in range(KT):
                ins = e.matmul(c.bank(6)[:, 0:8], c.xT[:, k, ts:ts + 128],
                               c.router1[:, k, :], start=(k == 0), stop=(k == KT - 1))
            return ins
        mm = p.op(T, mm_r, inc=True, waits=(p.now(S), p.now(V)))
        p.wait(V, mm)
        softmax_topk(c, c.bank(6)[:, 0:E1], E1, 6, c.w8[:, tt, :], False)
    p.barrier((T, V, S))
    p.op(V, lambda e: e.memset(c.dacc[:], 0.0), inc=True, drain=True)
    specs = []
    for ei in range(E1):
        specs.append(("exp", ei, c.f1gate[ei], c.f1up[ei], c.f1down[ei]))
    specs.append(("shr", 0, c.wsgate[:, 0:512], c.wsup[:, 0:512], c.wsdown[0:512, :]))
    specs.append(("shr", 1, c.wsgate[:, 512:1024], c.wsup[:, 512:1024],
                  c.wsdown[512:1024, :]))
    ffn_slab(c, [(kind, wg, wu, wd, ei, 1) for kind, ei, wg, wu, wd in specs],
             c.dacc, step=4)
    p.barrier((T, V, S))
    for tt in range(TT):
        v = c.dump[:, 1, :]
        vv = p.op(V, lambda e, tt=tt: e.tensor_tensor(v, c.resid[:, tt, :],
                                                      c.dacc[:, tt, :], ALU.add),
                  inc=True, drain=True, waits=(p.now(S),))
        lo = ln_apply(c, v, c.dump[:, 0, :], kind="ln", vwait=(vv,))
        p.dma(G, "st", lambda e, tt=tt: e.dma_start(
            out=out_ap[tt * 128:(tt + 1) * 128, :], in_=c.dump[:, 0, :]),
            waits=(lo,))
        p.wait(V, p.dnow("st"))
        p.wait(S, p.dnow("st"))
    p.wait(G, p.dnow("st"))
    p.op(G, lambda e: e.nop())


def ffn_slab(c, units, dacc, step):
    p = c.p
    slab_last_T = p.now(T)
    zS = ("e", S, 0)
    zV = ("e", V, 0)
    zT = ("e", T, 0)
    sgsu_tok = {}
    hv_tok = {}
    cp_tok = {}
    dn_tok = {}
    dacc_tok = {}
    ti = -1
    for kind, wg, wu, wd, ei, nfc in units:
        for fc in range(nfc):
            f0 = fc * 512
            lds = []
            for k in range(KT):
                if step == 3:
                    ga = wg[k * 128:(k + 1) * 128, f0:f0 + 512]
                    ua = wu[k * 128:(k + 1) * 128, f0:f0 + 512]
                else:
                    ga = wg[k * 128:(k + 1) * 128, 0:512]
                    ua = wu[k * 128:(k + 1) * 128, 0:512]
                lds.append(p.dma(Y, "wsl", lambda e, k=k, ga=ga: e.dma_start(
                    out=c.slab_g[:, k, :], in_=ga), waits=(slab_last_T,)))
                lds.append(p.dma(G, "wsl2", lambda e, k=k, ua=ua: e.dma_start(
                    out=c.slab_u[:, k, :], in_=ua), waits=(slab_last_T,)))
            for k4 in range(4):
                if step == 3:
                    da = wd[f0 + k4 * 128:f0 + (k4 + 1) * 128, :]
                else:
                    da = wd[k4 * 128:(k4 + 1) * 128, :]
                lds.append(p.dma(Y, "wsl", lambda e, k4=k4, da=da: e.dma_start(
                    out=c.slab_d[:, k4, :], in_=da), waits=(slab_last_T,)))
            slab_ok = (p.dnow("wsl"), p.dnow("wsl2"))
            for tt in range(TT):
                ti += 1
                ts = tt * 128
                q = ti % 2
                gl = c.u1[:, 2 * q, :]
                ul = c.u1[:, 2 * q + 1, :]

                def mm_gu(e, ts=ts):
                    ins = None
                    for k in range(KT):
                        e.matmul(c.bank(0), c.xT[:, k, ts:ts + 128],
                                 c.slab_g[:, k, :], start=(k == 0), stop=(k == KT - 1))
                        ins = e.matmul(c.bank(1), c.xT[:, k, ts:ts + 128],
                                       c.slab_u[:, k, :], start=(k == 0),
                                       stop=(k == KT - 1))
                    return ins
                gu = p.op(T, mm_gu, inc=True,
                          waits=slab_ok + (sgsu_tok.get(ti - 1, zS),))
                sg = p.op(S, lambda e, gl=gl: e.activation(gl, c.bank(0),
                                                           AF.Silu), inc=True,
                          waits=(gu, hv_tok.get(ti - 2, zV)))
                su = p.op(S, lambda e, ul=ul: e.copy(out=ul, in_=c.bank(1)),
                          inc=True)
                sgsu_tok[ti] = su
                hv = p.op(V, lambda e, gl=gl, ul=ul: e.tensor_tensor(
                    ul, gl, ul, ALU.mult),
                    inc=True, drain=True, waits=(su,))
                if kind == "exp":
                    hv = p.op(V, lambda e, ei=ei, tt=tt, ul=ul: e.tensor_scalar(
                        ul, ul, c.w8[:, tt, ei:ei + 1],
                        None, ALU.mult), inc=True, drain=True)
                hv_tok[ti] = hv
                last_cp = None
                for k4 in range(4):
                    bnk = 2 + k4 % 2
                    mm = p.op(T, lambda e, k4=k4, bnk=bnk, ul=ul: e.transpose(
                        c.bankB(bnk)[:, 0:128], ul[:, k4 * 128:(k4 + 1) * 128],
                        c.identB[:]), inc=True,
                        waits=(hv, cp_tok.get(k4 % 2, zS)))
                    last_cp = p.op(S, lambda e, k4=k4, bnk=bnk: e.copy(
                        out=c.hT[:, k4 * 128:(k4 + 1) * 128],
                        in_=c.bankB(bnk)[:, 0:128]), inc=True,
                        waits=(mm, dn_tok.get(ti - 1, zT)) if k4 == 0 else (mm,))
                    cp_tok[k4 % 2] = last_cp

                def mm_dn(e):
                    ins = None
                    for k4 in range(4):
                        e.matmul(c.bank(4), c.hT[:, k4 * 128:(k4 + 1) * 128],
                                 c.slab_d[:, k4, 0:512], start=(k4 == 0),
                                 stop=(k4 == 3))
                        ins = e.matmul(c.bank(5), c.hT[:, k4 * 128:(k4 + 1) * 128],
                                       c.slab_d[:, k4, 512:1024], start=(k4 == 0),
                                       stop=(k4 == 3))
                    return ins
                dn = p.op(T, mm_dn, inc=True,
                          waits=(last_cp, dacc_tok.get(ti - 1, zV)))
                dn_tok[ti] = dn
                slab_last_T = dn
                p.op(V, lambda e, tt=tt: e.tensor_tensor(
                    dacc[:, tt, 0:512], dacc[:, tt, 0:512], c.bank(4), ALU.add),
                    drain=True, waits=(dn,))
                dacc_tok[ti] = p.op(V, lambda e, tt=tt: e.tensor_tensor(
                    dacc[:, tt, 512:1024], dacc[:, tt, 512:1024], c.bank(5), ALU.add),
                    inc=True, drain=True)



WNAMES = [
    "L0_ffn_base_W", "L0_ffn_base_b", "L0_ffn_shared_A", "L0_ffn_shared_B",
    "L0_ffn_routed_A", "L0_ffn_routed_B", "L0_ffn_router_W", "L0_ffn_router_b",
    "L1_attn_scale1", "L1_attn_scale2", "L1_attn_wq", "L1_attn_wk", "L1_attn_wv",
    "L1_attn_wg1", "L1_attn_bg1", "L1_attn_wg2", "L1_attn_bg2", "L1_attn_wog",
    "L1_attn_bog", "L1_attn_wo", "L1_attn_w_gate", "L1_attn_w_up", "L1_attn_w_down",
    "L1_ffn_router_w", "L1_ffn_w_gate", "L1_ffn_w_up", "L1_ffn_w_down",
    "L1_ffn_ws_gate", "L1_ffn_ws_up", "L1_ffn_ws_down",
]

_WSHAPES = {
    "L0_ffn_base_W": (1024, 1024), "L0_ffn_base_b": (1024,),
    "L0_ffn_shared_A": (16, 1024), "L0_ffn_shared_B": (1024, 16),
    "L0_ffn_routed_A": (6, 16, 1024), "L0_ffn_routed_B": (6, 1024, 16),
    "L0_ffn_router_W": (6, 1024), "L0_ffn_router_b": (6,),
    "L1_attn_scale1": (1024,), "L1_attn_scale2": (1024,),
    "L1_attn_wq": (1024, 1024), "L1_attn_wk": (1024, 1024),
    "L1_attn_wv": (1024, 1024), "L1_attn_wg1": (16, 1024),
    "L1_attn_bg1": (16,), "L1_attn_wg2": (1024, 16), "L1_attn_bg2": (1024,),
    "L1_attn_wog": (1024, 1024), "L1_attn_bog": (1024,),
    "L1_attn_wo": (1024, 1024), "L1_attn_w_gate": (4096, 1024),
    "L1_attn_w_up": (4096, 1024), "L1_attn_w_down": (1024, 4096),
    "L1_ffn_router_w": (1024, 8), "L1_ffn_w_gate": (8, 1024, 512),
    "L1_ffn_w_up": (8, 1024, 512), "L1_ffn_w_down": (8, 512, 1024),
    "L1_ffn_ws_gate": (1024, 1024), "L1_ffn_ws_up": (1024, 1024),
    "L1_ffn_ws_down": (1024, 1024),
}


def _build_fused():
    nc = bass.Bass("TRN2", target_bir_lowering=False)
    tens = {}
    ein = lambda nm, shp, dt=F32: nc.dram_tensor(nm, list(shp), dt,
                                                 kind="ExternalInput").ap()
    tens["x0sched"] = ein("x0sched", [9, 4, 512, 1024])
    for s in "fr":
        tens[f"in_proj_{s}"] = ein(f"in_proj_{s}", [4096, 1024])
        tens[f"out_proj_{s}"] = ein(f"out_proj_{s}", [1024, 2048])
        tens[f"x_proj_{s}"] = ein(f"x_proj_{s}", [96, 2048])
        tens[f"dt_w_{s}"] = ein(f"dt_w_{s}", [2048, 64])
        tens[f"smallsF_{s}"] = ein(f"smallsF_{s}", [128, 368])
    tens["sel16"] = ein("sel16", [16, 2048], BF16)
    tens["masks"] = ein("masks", [128, 35])
    for nm in WNAMES:
        tens[nm] = ein(nm, _WSHAPES[nm])
    tens["x0s"] = ein("x0s", [2048, 1024])
    tens["identF"] = ein("identF", [128, 128])
    tens["identB"] = ein("identB", [128, 128], BF16)
    tens["identJ"] = ein("identJ", [128, 128], BF16)
    out = nc.dram_tensor("out", [2048, 1024], F32, kind="ExternalOutput").ap()

    p = Prog(inc_all=True)
    stack = ExitStack()
    with stack:
        psum = stack.enter_context(nc.psum_tensor("psum", [128, 4096], F32))
        st1 = ExitStack()
        scr = build_mamba_fused(nc, p, st1, tens, psum)
        p.barrier(tuple(ENGINES))
        chans = list(p.dcnt)
        for eng in ENGINES:
            p.wait(eng, *[p.dnow(ch) for ch in chans])
        st1.close()
        c = alloc_common(nc, p, stack, tens, psum)
        step1_fused(c, tens["x0s"], scr)
        step2(c)
        step3(c)
        step4(c, out)
        p.build(nc, stack)
    return nc


def kernel(**inputs):
    inputs = {k: np.asarray(v, np.float32) for k, v in inputs.items()}
    x_full = inputs["L0_attn_hidden_states"]
    flip = x_full[:, ::-1, :]

    nc = _build_fused()

    sel = np.zeros((16, 2048), BF)
    for n in range(16):
        sel[n, n * 128:(n + 1) * 128] = 1
    identF = np.eye(128, dtype=np.float32)
    identB = np.eye(128, dtype=BF)
    identJ = np.eye(128, dtype=np.float32)[::-1].astype(BF)
    smallsF = {}
    for s in "fr":
        A = np.asarray(inputs[f"L0_attn_A_log_{s}"]).reshape(16, 128, 16)
        A = A.transpose(1, 0, 2).reshape(128, 256)
        dtb = inputs[f"L0_attn_dt_b_{s}"].reshape(16, 128).T
        convb = inputs[f"L0_attn_conv_b_{s}"].reshape(16, 128).T
        Dp = inputs[f"L0_attn_D_{s}"].reshape(16, 128).T
        convw = inputs[f"L0_attn_conv_w_{s}"].reshape(2048, 4)
        convw = convw.reshape(16, 128, 4).transpose(1, 0, 2).reshape(128, 64)
        smallsF[s] = np.ascontiguousarray(
            np.concatenate([A, dtb, convb, Dp, convw], axis=1), np.float32)

    in_maps = []
    for cid in range(8):
        keep = np.ones(7, np.float32)
        keep[0] = 0.0
        if cid < 7:
            keep[cid] = 0.0
        sf = np.zeros(7, np.float32)
        if cid >= 1:
            sf[cid - 1] = 1.0
        sr = np.zeros(7, np.float32)
        if cid < 7:
            sr[6] = 1.0
        mrow = np.concatenate([keep, sf, 1.0 - sf, sr, 1.0 - sr])
        masks = np.ascontiguousarray(np.tile(mrow, (128, 1)), np.float32)

        x0sched = np.zeros((9, 4, 512, 1024), np.float32)
        for b in range(4):
            for pos in range(7):
                if pos < cid:
                    x0sched[pos, b] = x_full[b, 512 * pos:512 * (pos + 1)]
                else:
                    k = pos - cid
                    x0sched[pos, b] = flip[b, 512 * k:512 * (k + 1)]
            x0sched[7, b] = x_full[b, 512 * cid:512 * (cid + 1)]
            x0sched[8, b] = flip[b, 512 * (7 - cid):512 * (8 - cid)]
        x0s2 = np.ascontiguousarray(np.concatenate(
            [x_full[b, 512 * cid:512 * (cid + 1)] for b in range(4)], 0))

        im = {nm: inputs[nm] for nm in WNAMES}
        for s in "fr":
            im[f"in_proj_{s}"] = inputs[f"L0_attn_in_proj_{s}"]
            im[f"out_proj_{s}"] = inputs[f"L0_attn_out_proj_{s}"]
            im[f"x_proj_{s}"] = inputs[f"L0_attn_x_proj_{s}"]
            im[f"dt_w_{s}"] = inputs[f"L0_attn_dt_w_{s}"]
            im[f"smallsF_{s}"] = smallsF[s]
        im["x0sched"] = np.ascontiguousarray(x0sched)
        im["sel16"] = sel
        im["masks"] = masks
        im["x0s"] = x0s2
        im["identF"] = identF
        im["identB"] = identB
        im["identJ"] = np.ascontiguousarray(identJ)
        in_maps.append(im)

    res = run_bass_kernel_spmd(nc, in_maps, core_ids=list(range(8)))

    out = np.zeros((4, 4096, 1024), np.float32)
    for cid in range(8):
        o = np.asarray(res.results[cid]["out"]).reshape(4, 512, 1024)
        out[:, 512 * cid:512 * (cid + 1), :] = o
    return out


# version 08
# speedup vs baseline: 1.2117x; 1.0100x over previous
import numpy as np
import ml_dtypes
import concourse.bass as bass
import concourse.mybir as mybir
from concourse.bass_utils import run_bass_kernel_spmd
from contextlib import ExitStack

"""Builders for the mk_seq_postln_d2 megakernel (dev module; inlined into kernel.py later)."""

F32 = mybir.dt.float32
BF16 = mybir.dt.bfloat16
AF = mybir.ActivationFunctionType
ALU = mybir.AluOpType
AX = mybir.AxisListType
BF = ml_dtypes.bfloat16
ENGINES = ("tensor", "vector", "scalar", "gpsimd", "sync")


class Prog:

    def __init__(self, auto_drain=False, inc_all=False):
        self.auto_drain = auto_drain
        self.inc_all = inc_all
        self.items = []
        self.ecnt = dict.fromkeys(ENGINES, 0)
        self.dcnt = {}

    def op(self, eng, fn, inc=False, drain=False, waits=()):
        if self.inc_all:
            inc = True
        self.items.append(("op", eng, fn, inc, drain, tuple(waits)))
        if inc:
            self.ecnt[eng] += 1
        return ("e", eng, self.ecnt[eng])

    def dma(self, eng, name, fn, waits=()):
        self.dcnt[name] = self.dcnt.get(name, 0) + 16
        self.items.append(("dma", eng, fn, name, tuple(waits)))
        return ("d", name, self.dcnt[name])

    def arm_dma(self, eng, name, fn, ndma, waits=()):
        self.dcnt[name] = self.dcnt.get(name, 0) + 16 * ndma
        self.items.append(("arm", eng, fn, name, tuple(waits), ndma))
        return ("d", name, self.dcnt[name])

    def now(self, eng):
        return ("e", eng, self.ecnt[eng])

    def dnow(self, name):
        return ("d", name, self.dcnt.get(name, 0))

    def barrier(self, engs):
        snaps = [self.now(e) for e in engs]
        out = []
        for e in engs:
            ws = [s for s in snaps if s[1] != e]
            self.items.append(("waits", e, ws))
        return snaps

    def wait(self, eng, *tokens):
        self.items.append(("waits", eng, tuple(tokens)))

    def build(self, nc, stack, verbose=False):
        esem = {e: stack.enter_context(nc.semaphore(f"es_{e}")) for e in ENGINES}
        dsem = {n: stack.enter_context(nc.semaphore(f"ds_{n}")) for n in self.dcnt}
        block = stack.enter_context(nc.Block())
        per = {e: [] for e in ENGINES}
        for it in self.items:
            per[it[1]].append(it)

        def run_engine(e, eng_items):
            seen = {}

            def handle(engine):
                cnt = 0
                dc = {}
                for it in eng_items:
                    if it[0] == "waits":
                        for tk in it[2]:
                            _wait(engine, tk)
                        continue
                    if it[0] == "op":
                        _, _, fn, inc, drain, waits = it
                        for tk in waits:
                            _wait(engine, tk)
                        instr = fn(engine)
                        if drain or (self.auto_drain and e in ("vector", "scalar", "gpsimd")):
                            instr = engine.drain()
                        if inc:
                            cnt += 1
                            (instr if instr is not None else engine.nop()).then_inc(esem[e], 1)
                    elif it[0] == "arm":
                        _, _, fn, name, waits, ndma = it
                        for tk in waits:
                            _wait(engine, tk)
                        for arm in engine.switch_core_id(n=128):
                            cid = arm.logical % 8
                            fn(engine, cid, dsem[name])
                    else:
                        _, _, fn, name, waits = it
                        for tk in waits:
                            _wait(engine, tk)
                        d = fn(engine)
                        d.then_inc(dsem[name], 16)

            def _wait(engine, tk):
                kind, key, val = tk
                if val <= 0:
                    return
                if kind == "e" and key == e:
                    return
                prev = seen.get((kind, key), -1)
                if val <= prev:
                    return
                seen[(kind, key)] = val
                sem = esem[key] if kind == "e" else dsem[key]
                engine.wait_ge(sem, val)

            return handle

        for e in ENGINES:
            getattr(block, e if e != "sync" else "sync")(run_engine(e, per[e]))
        return esem, dsem


def ceil_div(a, b):
    return (a + b - 1) // b


def build_mamba_fused(nc, p, stack, tens, psum):
    L, DM, DI, NS, DTR, Tc = 4096, 1024, 2048, 16, 64, 512
    NPOS = 9
    DTt = DI // 128
    KT = DM // 128
    MT = Tc // 128
    NP = DTR + 2 * NS

    sb = lambda n, s, d=F32: stack.enter_context(nc.sbuf_tensor(n, s, d))
    dram = lambda n, s, d=BF16: nc.dram_tensor(n, s, d, kind="Internal").ap()
    bank = lambda b, par=128: psum[0:par, b * 512:(b + 1) * 512]

    xbf = dram("xbf", [NPOS, 4, Tc, DM])
    inp_bf = {s: dram(f"inp_bf_{s}", [2 * DI, DM]) for s in "fr"}
    outp_bf = {s: dram(f"outp_bf_{s}", [DM, DI]) for s in "fr"}
    xp_bf = {s: dram(f"xp_bf_{s}", [128, DI]) for s in "fr"}
    dtw_bf = {s: dram(f"dtw_bf_{s}", [DI, 128]) for s in "fr"}
    inpT = {s: dram(f"inpT_{s}", [128, KT, 2 * DI]) for s in "fr"}
    bfb = {s: dram(f"bfb_{s}", [128, 4096]) for s in "fr"}
    outT = {s: dram(f"outT_{s}", [DI, DM]) for s in "fr"}
    sched_x = dram("sched_x", [7, 128, KT, DI])
    sched_bfb = dram("sched_bfb", [7, 128, 4096])
    sched_smF = nc.dram_tensor("sched_smF", [7, 128, 368], F32, kind="Internal").ap()
    scr = dram("scr", [2, 4, Tc, DM])

    slab_q = sb("slab_q", [128, 2, KT, 512], BF16)
    smallsB = sb("smallsB", [128, 4096], BF16)
    smallsF_sb = sb("smallsF_sb", [128, 368])
    A_sb = sb("A_sb", [128, DTt, NS])
    sel_sb = sb("sel_sb", [NS, NS * 128], BF16)
    masks_sb = sb("masks_sb", [128, 35])
    x0cT = sb("x0cT", [128, KT, Tc], BF16)
    conv_in = sb("conv_in", [128, 4, Tc + 3], BF16)
    halo = sb("halo", [128, DTt, 3])
    hF = sb("hF", [128, DTt, 3])
    hR = sb("hR", [128, DTt, 3])
    x_silu = sb("x_silu", [128, DTt, Tc], BF16)
    delta = sb("delta", [128, DTt, Tc], BF16)
    u_bf = sb("u_bf", [128, DTt, Tc], BF16)
    y_acc = sb("y_acc", [128, DTt, Tc])
    carry = sb("carry", [128, DTt * NS])
    cF = sb("cF", [128, DTt * NS])
    cR = sb("cR", [128, DTt * NS])
    dtr_bf = sb("dtr_bf", [DTR, Tc], BF16)
    Btile = sb("Btile", [NS, Tc], BF16)
    Ctile = sb("Ctile", [NS, Tc], BF16)
    dA_b = sb("dA_b", [128, 4, Tc])
    dBu_b = sb("dBu_b", [128, 8, Tc], BF16)
    h_b = sb("h_b", [128, DTt, Tc], BF16)
    m_b = sb("m_b", [128, 4, Tc], BF16)
    Bb_sb = sb("Bb_sb", [128, 2, Tc], BF16)
    A_pos = sb("A_pos", [128, DTt, NS])
    biasb = sb("biasb", [128, DTt, NS])
    P3 = sb("P3", [128, DTt, NS])
    hloc = sb("hloc", [128, DTt, NS])
    ones512 = sb("ones512", [128, Tc], BF16)
    zc = sb("zc", [128, 1])
    dce_sb = sb("dce_sb", [128, DTt])
    Bb16 = h_b
    z_bf = m_b
    tmp_c = dBu_b
    yq = delta
    attn_sb = sb("attn_sb", [128, DM], BF16)
    sp_t = dA_b
    sp_r = sb("sp_r", [128, 4, Tc], BF16)
    outT_t = sp_r.rearrange("p (a b) c -> p a (b c)", a=2)
    stage_t = slab_q[:, 0].rearrange("p a b -> p (a b)")
    halo_fl = halo.rearrange("p a b -> p (a b)")
    hF_fl = hF.rearrange("p a b -> p (a b)")
    hR_fl = hR.rearrange("p a b -> p (a b)")
    x_projT = smallsB[:, 0:2048].rearrange("p (a b) -> p a b", b=128)
    dt_wT = smallsB[:, 2048:4096]
    dtb_sb = smallsF_sb[:, 256:272]
    convb_sb = smallsF_sb[:, 272:288]
    D_sb = smallsF_sb[:, 288:304]
    convw_sb = smallsF_sb[:, 304:368].rearrange("p (a b) -> p a b", b=4)
    Alog_v = smallsF_sb[:, 0:256].rearrange("p (a b) -> p a b", b=NS)

    T, V, S, G, Y = "tensor", "vector", "scalar", "gpsimd", "sync"

    p.dma(G, "cast", lambda e: e.dma_start(out=xbf, in_=tens["x0sched"]))
    for s in "fr":
        p.dma(G, "cast", lambda e, s=s: e.dma_start(
            out=inp_bf[s], in_=tens[f"in_proj_{s}"]))
        p.dma(G, "cast", lambda e, s=s: e.dma_start(
            out=outp_bf[s], in_=tens[f"out_proj_{s}"]))
        p.dma(G, "cast", lambda e, s=s: e.dma_start(
            out=xp_bf[s][0:NP, :], in_=tens[f"x_proj_{s}"]))
        p.dma(G, "cast", lambda e, s=s: e.dma_start(
            out=dtw_bf[s][:, 0:DTR], in_=tens[f"dt_w_{s}"]))
    cast_done = p.dnow("cast")
    p.dma(Y, "ld", lambda e: e.dma_start(out=sel_sb[:], in_=tens["sel16"]))
    p.dma(Y, "ld", lambda e: e.dma_start(out=masks_sb[:], in_=tens["masks"]))
    ld_done = p.dnow("ld")
    for s in "fr":
        for k in range(KT):
            tpt = p.dma(Y, "tp", lambda e, s=s, k=k: e.dma_start(
                out=stage_t[:], in_=inp_bf[s][:, k * 128:(k + 1) * 128],
                transpose=True), waits=(cast_done, p.dnow("stg")))
            p.dma(G, "stg", lambda e, s=s, k=k: e.dma_start(
                out=inpT[s][:, k, :], in_=stage_t[:]), waits=(tpt,))
        for dt in range(DTt):
            tpt = p.dma(Y, "tp", lambda e, s=s, dt=dt: e.dma_start(
                out=stage_t[:, 0:128], in_=xp_bf[s][:, dt * 128:(dt + 1) * 128],
                transpose=True), waits=(cast_done, p.dnow("stg")))
            p.dma(G, "stg", lambda e, s=s, dt=dt: e.dma_start(
                out=bfb[s][:, dt * 128:(dt + 1) * 128], in_=stage_t[:, 0:128]),
                waits=(tpt,))
        tpt = p.dma(Y, "tp", lambda e, s=s: e.dma_start(
            out=stage_t[:, 0:2048], in_=dtw_bf[s], transpose=True),
            waits=(cast_done, p.dnow("stg")))
        p.dma(G, "stg", lambda e, s=s: e.dma_start(
            out=bfb[s][:, 2048:4096], in_=stage_t[:, 0:2048]), waits=(tpt,))
        for kt in range(DTt):
            tpt = p.dma(Y, "tp", lambda e, s=s, kt=kt: e.dma_start(
                out=stage_t[:, 0:DM], in_=outp_bf[s][:, kt * 128:(kt + 1) * 128],
                transpose=True), waits=(cast_done, p.dnow("stg")))
            p.dma(G, "stg", lambda e, s=s, kt=kt: e.dma_start(
                out=outT[s][kt * 128:(kt + 1) * 128, :], in_=stage_t[:, 0:DM]),
                waits=(tpt,))
    prep_done = (p.dnow("stg"), p.dnow("tp"), cast_done)
    for pos in range(7):
        def arm_fn(e, cid, sem, pos=pos):
            s = "f" if pos < cid else "r"
            e.dma_start(out=sched_x[pos], in_=inpT[s][:, :, 0:DI]).then_inc(sem, 16)
            e.dma_start(out=sched_bfb[pos], in_=bfb[s]).then_inc(sem, 16)
            return e.dma_start(out=sched_smF[pos],
                               in_=tens[f"smallsF_{s}"]).then_inc(sem, 16)
        p.arm_dma(Y, "sch", arm_fn, 3, waits=prep_done if pos == 0 else ())
    sch_done = p.dnow("sch")
    p.op(V, lambda e: e.memset(ones512[:], 1.0), inc=True)
    p.op(V, lambda e: e.memset(zc[:], 0.0), inc=True, drain=True)
    p.wait(T, ld_done, p.now(V), p.now(S))
    p.wait(V, ld_done)
    p.wait(S, ld_done)

    XB = (0, 1, 5, 6)
    for b in range(4):
      zs0 = p.op(V, lambda e: e.memset(carry[:], 0.0), inc=True)
      p.op(V, lambda e: e.memset(cF[:], 0.0), inc=True)
      p.op(V, lambda e: e.memset(cR[:], 0.0), inc=True)
      p.op(V, lambda e: e.memset(halo[:], 0.0), inc=True)
      p.op(V, lambda e: e.memset(hF[:], 0.0), inc=True)
      p.op(V, lambda e: e.memset(hR[:], 0.0), inc=True, drain=True)
      for pos in range(NPOS):
        own = pos >= 7
        dirx = 0 if pos == 7 else 1
        if pos < 7:
            smB_src, smF_src = sched_bfb[pos], sched_smF[pos]
        elif pos == 7:
            smB_src, smF_src = bfb["f"], tens["smallsF_f"]
        else:
            smB_src, smF_src = bfb["r"], tens["smallsF_r"]
        lw = (sch_done, p.now(T), p.now(V), p.now(S))
        p.dma(Y, "sml", lambda e, smB_src=smB_src: e.dma_start(
            out=smallsB[:], in_=smB_src), waits=lw)
        p.dma(Y, "sml", lambda e, smF_src=smF_src: e.dma_start(
            out=smallsF_sb[:], in_=smF_src))
        sml_ok = p.dnow("sml")
        e1 = p.op(S, lambda e: e.activation(A_pos[:], Alog_v, AF.Exp), inc=True,
                  waits=(sml_ok,))
        ae = p.op(V, lambda e: e.tensor_scalar(A_sb[:], A_pos[:], -1.0, None,
                                               ALU.mult), inc=True, waits=(e1,),
                  drain=True)
        if pos < 7:
            km = masks_sb[:, pos:pos + 1]
            p.op(V, lambda e, km=km: e.tensor_scalar(carry[:], carry[:], km, None,
                                                     ALU.mult), inc=True, drain=True)
            p.op(V, lambda e, km=km: e.tensor_scalar(halo_fl[:], halo_fl[:], km,
                                                     None, ALU.mult), inc=True,
                 drain=True)
        elif pos == 7:
            p.op(V, lambda e: e.tensor_copy(carry[:], cF[:]), inc=True, drain=True)
            p.op(V, lambda e: e.tensor_copy(halo[:], hF[:]), inc=True, drain=True)
        else:
            p.op(V, lambda e: e.tensor_copy(carry[:], cR[:]), inc=True, drain=True)
            p.op(V, lambda e: e.tensor_copy(halo[:], hR[:]), inc=True, drain=True)
        pro_v = p.now(V)

        xw = []
        for k in range(KT):
            xw.append(p.dma(Y, "x0c", lambda e, k=k, pos=pos, b=b: e.dma_start(
                out=x0cT[:, k, :], in_=xbf[pos, b, :, k * 128:(k + 1) * 128],
                transpose=True), waits=(cast_done,) if k == 0 else ()))
        x0c_ok = xw[-1]

        silu_snap = {}
        slab_war = {0: p.now(T), 1: p.now(T)}
        xslab_src = sched_x[pos] if pos < 7 else inpT["f" if pos == 7 else "r"]
        for g in range(0, DTt, 4):
            quad = (g, g + 1, g + 2, g + 3)
            buf = (g // 4) % 2
            sl = p.dma(Y, "slb", lambda e, buf=buf, g=g, xslab_src=xslab_src:
                       e.dma_start(out=slab_q[:, buf],
                                   in_=xslab_src[:, :, g * 128:g * 128 + 512]),
                       waits=(slab_war[buf], sch_done))
            mm_snap = {}
            for dt in quad:
                def mm_xz(e, dt=dt, buf=buf):
                    ins = None
                    for k in range(KT):
                        ins = e.matmul(bank(XB[dt % 4]),
                                       slab_q[:, buf, k, (dt % 4) * 128:(dt % 4 + 1) * 128],
                                       x0cT[:, k, :], start=(k == 0), stop=(k == KT - 1))
                    return ins
                mm_snap[dt] = p.op(T, mm_xz, inc=True,
                                   waits=(x0c_ok, sl, pro_v,
                                          silu_snap.get(g - 4, ("e", S, 0))))
            slab_war[buf] = mm_snap[quad[-1]]
            ev_snap = {}
            for dt in quad:
                ev_snap[dt] = p.op(S, lambda e, dt=dt: e.copy(
                    out=conv_in[:, dt % 4, 3:], in_=bank(XB[dt % 4])),
                    inc=True, waits=(mm_snap[dt],))
            for dt in quad:
                p.op(V, lambda e, dt=dt: e.tensor_copy(conv_in[:, dt % 4, 0:3], halo[:, dt, :]),
                     waits=(ev_snap[dt], silu_snap.get(g - 4, ("e", S, 0))))
            for j in range(4):
                for dt in quad:
                    if j == 0:
                        p.op(V, lambda e, dt=dt: e.tensor_scalar(
                            sp_t[:, dt % 4, :], conv_in[:, dt % 4, 0:Tc],
                            convw_sb[:, dt, 0:1], None, ALU.mult))
                    else:
                        p.op(V, lambda e, dt=dt, j=j: e.scalar_tensor_tensor(
                            sp_t[:, dt % 4, :], conv_in[:, dt % 4, j:Tc + j],
                            convw_sb[:, dt, j:j + 1], sp_t[:, dt % 4, :], ALU.mult, ALU.add))
            conv_v = {}
            for dt in quad:
                conv_v[dt] = p.op(V, lambda e, dt=dt: e.tensor_copy(
                    halo[:, dt, :], conv_in[:, dt % 4, Tc:Tc + 3]), inc=True)
            for dt in quad:
                silu_snap[g] = p.op(S, lambda e, dt=dt: e.activation(
                    x_silu[:, dt, :], sp_t[:, dt % 4, :], AF.Silu, bias=convb_sb[:, dt:dt + 1]),
                    inc=True, waits=(conv_v[dt],))
            for dt in quad:
                def mm_xdbl(e, dt=dt):
                    st = (dt == 0)
                    sp = (dt == DTt - 1)
                    e.matmul(bank(2, DTR), x_projT[:, dt, 0:DTR], x_silu[:, dt, :],
                             start=st, stop=sp)
                    e.matmul(bank(3, NS), x_projT[:, dt, DTR:DTR + NS], x_silu[:, dt, :],
                             start=st, stop=sp)
                    return e.matmul(bank(4, NS), x_projT[:, dt, DTR + NS:DTR + 2 * NS],
                                    x_silu[:, dt, :], start=st, stop=sp)
                p.op(T, mm_xdbl, inc=True, waits=(silu_snap[g],))
        xdbl_T = p.now(T)
        ev_dtr = p.op(S, lambda e: e.copy(out=dtr_bf[:], in_=bank(2, DTR)), inc=True,
                      waits=(xdbl_T,))
        ev_B = p.op(S, lambda e: e.copy(out=Btile[:], in_=bank(3, NS)), inc=True)
        ev_C = p.op(S, lambda e: e.copy(out=Ctile[:], in_=bank(4, NS)), inc=True)

        dmm = {}
        dv_hist = {}
        for g in range(0, DTt, 4):
            quad = (g, g + 1, g + 2, g + 3)
            for dt in quad:
                dmm[dt] = p.op(T, lambda e, dt=dt: e.matmul(
                    bank(XB[dt % 4]), dt_wT[0:DTR, dt * 128:(dt + 1) * 128], dtr_bf[:],
                    start=True, stop=True), inc=True,
                    waits=(ev_dtr, p.now(S)))
            sp_done = {}
            for dt in quad:
                p.op(S, lambda e, dt=dt: e.activation(
                    sp_t[:, dt % 4, 0:Tc], bank(XB[dt % 4]), AF.Abs, bias=dtb_sb[:, dt:dt + 1]),
                    waits=(dmm[dt], dv_hist.get(g - 4, ("e", V, 0))))
            for dt in quad:
                p.op(S, lambda e, dt=dt: e.activation(
                    sp_r[:, dt % 4, 0:Tc], bank(XB[dt % 4]),
                    AF.Relu, bias=dtb_sb[:, dt:dt + 1]))
            for dt in quad:
                p.op(S, lambda e, dt=dt: e.activation(
                    sp_t[:, dt % 4, 0:Tc], sp_t[:, dt % 4, 0:Tc], AF.Exp, scale=-1.0))
            for dt in quad:
                sp_done[dt] = p.op(S, lambda e, dt=dt: e.activation(
                    sp_t[:, dt % 4, 0:Tc], sp_t[:, dt % 4, 0:Tc], AF.Ln, bias=1.0), inc=True)
            dv = {}
            for dt in quad:
                dv[dt] = p.op(V, lambda e, dt=dt: e.tensor_tensor(
                    delta[:, dt, :], sp_r[:, dt % 4, 0:Tc],
                    sp_t[:, dt % 4, 0:Tc], ALU.add),
                    inc=True, waits=(sp_done[dt],))
            for dt in quad:
                p.op(V, lambda e, dt=dt: e.tensor_tensor(
                    u_bf[:, dt, :], delta[:, dt, :], x_silu[:, dt, :], ALU.mult), inc=True)
            dv_hist[g] = p.now(V)
        if own:
            yz = p.op(V, lambda e: e.memset(y_acc[:], 0.0), inc=True)
        p.barrier((T, V, S, G))

        carry3 = carry.rearrange("p (a b) -> p a b", b=NS)
        if own:
            scan_v_snap = {}
            dbu_tok = {}
            ebB_hist = {}
            n_last_v = {}
            pend = []

            def emit_carry(nq):
                cn, cg = nq
                p.op(V, lambda e, cn=cn, cg=cg: e.tensor_copy(
                    carry3[:, cg:cg + 4, cn:cn + 1],
                    h_b[:, cg:cg + 4, Tc - 1:Tc]), inc=True)

            for n in range(NS):
                bB = 2 if n % 2 == 0 else 3
                bC = 7 if n % 2 == 0 else 4
                war = n_last_v.get(n - 2, ("e", V, 0))
                war2 = ebB_hist.get(n - 2, ("e", S, 0))
                selL = sel_sb[:, n * 128:(n + 1) * 128]
                bc1 = p.op(T, lambda e, bB=bB, selL=selL: e.matmul(
                    bank(bB), selL, Btile[:], start=True, stop=True),
                    inc=True, waits=(ev_B, ev_C, war, war2))
                bc2 = p.op(T, lambda e, bC=bC, selL=selL: e.matmul(
                    bank(bC), selL, Ctile[:], start=True, stop=True), inc=True)
                dbu_war = dbu_tok.get((n - 2, DTt - 1), ("e", G, 0))
                ebB = p.op(S, lambda e, n=n, bB=bB: e.copy(
                    out=Bb_sb[:, n % 2, :], in_=bank(bB)), inc=True,
                    waits=(bc1, dbu_war))
                ebB_hist[n] = ebB
                for g in range(0, DTt, 4):
                    quad = (g, g + 1, g + 2, g + 3)
                    da = {}
                    for dt in quad:
                        prev = (scan_v_snap.get((n, dt - 4))
                                or scan_v_snap.get((n - 1, DTt - 4 + (dt % 4)))
                                or ("e", V, 0))
                        da[dt] = p.op(S, lambda e, dt=dt, n=n: e.activation(
                            dA_b[:, dt % 4, :], delta[:, dt, :], AF.Exp,
                            scale=A_sb[:, dt, n:n + 1]), inc=True, waits=(prev,))
                    for dt in quad:
                        wprev = (scan_v_snap.get((n, dt - 8))
                                 or scan_v_snap.get((n - 1, (dt + 8) % DTt))
                                 or ("e", V, 0))
                        dbu_tok[(n, dt)] = p.op(G, lambda e, dt=dt, n=n: e.tensor_tensor(
                            dBu_b[:, dt % 8, :], u_bf[:, dt, :], Bb_sb[:, n % 2, :],
                            ALU.mult), inc=True, waits=(ebB, wprev))
                    for dt in quad:
                        scan_v_snap[(n, dt)] = p.op(V, lambda e, dt=dt, n=n: e.tensor_tensor_scan(
                            h_b[:, dt, :], dA_b[:, dt % 4, :], dBu_b[:, dt % 8, :],
                            carry3[:, dt, n:n + 1], ALU.mult, ALU.add), inc=True,
                            waits=(da[dt], dbu_tok[(n, dt)]))
                    for dt in quad:
                        p.op(V, lambda e, dt=dt, bC=bC: e.tensor_tensor(
                            m_b[:, dt % 4, :], h_b[:, dt, :], bank(bC), ALU.mult),
                            waits=(bc2,) if dt == g else ())
                    if pend:
                        emit_carry(pend.pop(0))
                    pend.append((n, g))
                    for dt in quad:
                        p.op(V, lambda e, dt=dt: e.tensor_tensor(
                            y_acc[:, dt, :], y_acc[:, dt, :], m_b[:, dt % 4, :], ALU.add),
                            inc=(dt == DTt - 1))
                n_last_v[n] = p.now(V)
            for _ in range(2):
                p.op(V, lambda e: e.memset(m_b[0:1, 0, 0:1], 0.0), inc=True)
            emit_carry(pend.pop(0))
        else:
            ebB_tok = {}
            last_dbu = ("e", V, 0)
            for n in range(NS):
                bB = 2 if n % 2 == 0 else 3
                selL = sel_sb[:, n * 128:(n + 1) * 128]
                war2 = ebB_tok.get(n - 2, ("e", S, 0))
                bc1 = p.op(T, lambda e, bB=bB, selL=selL: e.matmul(
                    bank(bB), selL, Btile[:], start=True, stop=True),
                    inc=True, waits=(ev_B, ev_C, war2) if n < 2 else (war2,))
                ebB_tok[n] = p.op(S, lambda e, n=n, bB=bB: e.copy(
                    out=Bb16[:, n, :], in_=bank(bB)), inc=True, waits=(bc1,))
            ebB_all = p.now(S)
            for dt in range(DTt):
                p.op(V, lambda e, dt=dt: e.tensor_reduce(
                    dce_sb[:, dt:dt + 1], delta[:, dt, :], AX.X, ALU.add), inc=True)
            for dt in range(DTt):
                p.op(V, lambda e, dt=dt: e.tensor_scalar(
                    biasb[:, dt, :], A_sb[:, dt, :], dce_sb[:, dt:dt + 1], None,
                    ALU.mult), inc=True)
            bias_done = p.op(S, lambda e: e.activation(
                P3[:], biasb[:], AF.Exp), inc=True, waits=(p.now(V),))
            cum_tok = {}
            wS_last = {}
            wu_last = {}
            pend_red = []
            qidx = 0
            for dt in range(DTt):
                dcv = dA_b[:, dt % 2, :]
                cum_tok[dt] = p.op(V, lambda e, dt=dt, dcv=dcv: e.tensor_tensor_scan(
                    dcv, ones512[:], delta[:, dt, :], zc[:], ALU.mult, ALU.add),
                    inc=True, waits=(wS_last.get(dt - 2, ("e", S, 0)),))
                for n4 in range(0, NS, 4):
                    mbuf = m_b if qidx % 2 == 0 else dBu_b
                    qidx += 1
                    wq = {}
                    for n in range(n4, n4 + 4):
                        ws = ()
                        if n == n4:
                            ws = (cum_tok[dt], bias_done,
                                  wu_last.get((dt, n4 - 4))
                                  or wu_last.get((dt - 1, NS - 4))
                                  or ("e", V, 0))
                        wq[n] = p.op(S, lambda e, dt=dt, n=n, dcv=dcv: e.activation(
                            sp_r[:, n % 4, :], dcv, AF.Exp,
                            scale=A_pos[:, dt, n:n + 1],
                            bias=biasb[:, dt, n:n + 1]), inc=True, waits=ws)
                    for n in range(n4, n4 + 4):
                        tok = p.op(V, lambda e, dt=dt, n=n, mbuf=mbuf: e.tensor_tensor(
                            mbuf[:, n % 4, :], sp_r[:, n % 4, :], u_bf[:, dt, :],
                            ALU.mult), inc=True, waits=(wq[n], ebB_all))
                        if n == n4 + 3:
                            wu_last[(dt, n4)] = tok
                    if pend_red:
                        p.op(V, pend_red.pop(0), inc=True)
                    for n in range(n4, n4 + 4):
                        p.op(V, lambda e, dt=dt, n=n, mbuf=mbuf: e.tensor_tensor(
                            mbuf[:, n % 4, :], mbuf[:, n % 4, :], Bb16[:, n, :],
                            ALU.mult), inc=True)
                    pend_red.append(lambda e, dt=dt, n4=n4, mbuf=mbuf: e.tensor_reduce(
                        hloc[:, dt, n4:n4 + 4], mbuf[:, 0:4, :], AX.X, ALU.add))
                    wS_last[dt] = wq[n4 + 3]
            for _ in range(3):
                p.op(V, lambda e: e.memset(zc[:], 0.0), inc=True)
            p.op(V, pend_red.pop(0), inc=True)
            for dt in range(DTt):
                p.op(V, lambda e, dt=dt: e.tensor_tensor(
                    carry3[:, dt, :], carry3[:, dt, :], P3[:, dt, :], ALU.mult),
                    inc=True, waits=(bias_done,) if dt == 0 else ())
            for dt in range(DTt):
                p.op(V, lambda e, dt=dt: e.tensor_tensor(
                    carry3[:, dt, :], carry3[:, dt, :], hloc[:, dt, :], ALU.add),
                    inc=True)
        p.barrier((T, V, S, G))

        if own:
            zset = inpT["f" if pos == 7 else "r"]
            z_war = {0: p.now(T), 1: p.now(T)}
            for g in range(0, DTt, 4):
                quad = (g, g + 1, g + 2, g + 3)
                buf = (g // 4) % 2
                zl = p.dma(Y, "slb", lambda e, buf=buf, g=g, zset=zset: e.dma_start(
                    out=slab_q[:, buf],
                    in_=zset[:, :, DI + g * 128:DI + g * 128 + 512]),
                    waits=(z_war[buf],))
                zmm = {}
                for dt in quad:
                    def mm_z(e, dt=dt, buf=buf):
                        ins = None
                        for k in range(KT):
                            ins = e.matmul(bank(XB[dt % 4]),
                                           slab_q[:, buf, k, (dt % 4) * 128:(dt % 4 + 1) * 128],
                                           x0cT[:, k, :], start=(k == 0), stop=(k == KT - 1))
                        return ins
                    zmm[dt] = p.op(T, mm_z, inc=True, waits=(zl, p.now(V), p.now(S)))
                z_war[buf] = zmm[quad[-1]]
                zs = {}
                for dt in quad:
                    zs[dt] = p.op(S, lambda e, dt=dt: e.activation(
                        z_bf[:, dt % 4, :], bank(XB[dt % 4]), AF.Silu), inc=True, waits=(zmm[dt],))
                for dt in quad:
                    p.op(V, lambda e, dt=dt: e.tensor_scalar(
                        tmp_c[:, dt % 4, :], x_silu[:, dt, :], D_sb[:, dt:dt + 1], None, ALU.mult),
                        waits=(zs[dt],))
                for dt in quad:
                    p.op(V, lambda e, dt=dt: e.tensor_tensor(
                        tmp_c[:, dt % 4, :], y_acc[:, dt, :], tmp_c[:, dt % 4, :], ALU.add))
                for dt in quad:
                    p.op(V, lambda e, dt=dt: e.tensor_tensor(
                        yq[:, dt, :], tmp_c[:, dt % 4, :], z_bf[:, dt % 4, :], ALU.mult), inc=True)
            p.barrier((T, V, S))

            outT_src = outT["f" if pos == 7 else "r"]
            dmm2 = {}
            for dt in range(DTt):
                lw = (dmm2[dt - 2],) if dt - 2 in dmm2 else (p.now(V),)
                otw = p.dma(Y, f"outT{dt % 2}", lambda e, dt=dt, outT_src=outT_src: e.dma_start(
                    out=outT_t[:, dt % 2, :], in_=outT_src[dt * 128:(dt + 1) * 128, :]),
                    waits=lw)

                def mm_out(e, dt=dt):
                    st = (dt == 0)
                    sp = (dt == DTt - 1)
                    ins = None
                    for m in range(MT):
                        for hh in range(2):
                            ins = e.matmul(bank(m * 2 + hh), yq[:, dt, m * 128:(m + 1) * 128],
                                           outT_t[:, dt % 2, hh * 512:(hh + 1) * 512],
                                           start=st, stop=sp)
                    return ins
                dmm2[dt] = p.op(T, mm_out, inc=True, waits=(otw,))
            out_T = p.now(T)
            for m in range(MT):
                evm = p.op(S, lambda e, m=m: e.copy(out=attn_sb[:], in_=psum[:, m * 1024:(m + 1) * 1024]),
                           inc=True, waits=(out_T, p.dnow("st")))
                p.dma(G, "st", lambda e, m=m, dirx=dirx, b=b: e.dma_start(
                    out=scr[dirx, b, m * 128:(m + 1) * 128, :], in_=attn_sb[:]),
                    waits=(evm,))
        else:
            sfm = masks_sb[:, 7 + pos:8 + pos]
            nsf = masks_sb[:, 14 + pos:15 + pos]
            srm = masks_sb[:, 21 + pos:22 + pos]
            nsr = masks_sb[:, 28 + pos:29 + pos]
            p.op(V, lambda e: e.tensor_scalar(cF[:], cF[:], nsf, None, ALU.mult),
                 inc=True, drain=True)
            p.op(V, lambda e: e.scalar_tensor_tensor(cF[:], carry[:], sfm, cF[:],
                                                     ALU.mult, ALU.add), inc=True,
                 drain=True)
            p.op(V, lambda e: e.tensor_scalar(cR[:], cR[:], nsr, None, ALU.mult),
                 inc=True, drain=True)
            p.op(V, lambda e: e.scalar_tensor_tensor(cR[:], carry[:], srm, cR[:],
                                                     ALU.mult, ALU.add), inc=True,
                 drain=True)
            p.op(V, lambda e: e.tensor_scalar(hF_fl[:], hF_fl[:], nsf, None, ALU.mult),
                 inc=True, drain=True)
            p.op(V, lambda e: e.scalar_tensor_tensor(hF_fl[:], halo_fl[:], sfm, hF_fl[:],
                                                     ALU.mult, ALU.add), inc=True,
                 drain=True)
            p.op(V, lambda e: e.tensor_scalar(hR_fl[:], hR_fl[:], nsr, None, ALU.mult),
                 inc=True, drain=True)
            p.op(V, lambda e: e.scalar_tensor_tensor(hR_fl[:], halo_fl[:], srm, hR_fl[:],
                                                     ALU.mult, ALU.add), inc=True,
                 drain=True)
        p.barrier((T, V, S, G, Y))
    p.wait(G, p.dnow("st"))
    p.op(G, lambda e: e.nop())
    return scr



"""Phase-B builders: x1=LN(x0+pf+pr); L0_ffn; L1_attn (GLA block); L1_ffn; final LN.

Token layout per core: 2048 tokens = 4 batches x 512 seq positions, b-major
(row = b*512 + l_local). 16 token-tiles of 128. GLA "time" = batch axis (4
steps); sub s (s=0..3) covers tiles {s, 4+s, 8+s, 12+s} = same l-range, b=0..3.
Canonical activation layout: (token-part 128, feature-free) f32 residual.
Matmul inputs are transposed on demand via TensorE into xT (feature-part,
token-free) bf16.

PSUM rule: one accumulation group per bank (first_mm clears the whole bank).
Same-engine dependent V/S ops carry drain=True (v1; modest op counts here).
"""

F32 = mybir.dt.float32
BF16 = mybir.dt.bfloat16
AF = mybir.ActivationFunctionType
ALU = mybir.AluOpType
AX = mybir.AxisListType
BF = ml_dtypes.bfloat16

T, V, S, G, Y = "tensor", "vector", "scalar", "gpsimd", "sync"

D = 1024
TT = 16
KT = 8
N = 2048
NH = 16
DK = 64
DHID = 4096
E0, R0 = 6, 16
E1, F1, FS = 8, 512, 1024
GLA_TEMP = 16.0
DBG_SUB = 0


class Ctx:
    pass


def phase2_casts(nc, p, tens):
    dram = lambda n, s, d=BF16: nc.dram_tensor(n, s, d, kind="Internal").ap()
    w = {}

    def cast(nm, src, shape):
        dd = dram(f"cst_{nm}", shape)
        p.dma(G, "cast2", lambda e, dd=dd, src=src: e.dma_start(out=dd, in_=src))
        w[nm] = dd
        return dd

    tn = tens
    specs = [("baseW", "L0_ffn_base_W", D, D), ("wq", "L1_attn_wq", D, D),
             ("wk", "L1_attn_wk", D, D), ("wv", "L1_attn_wv", D, D),
             ("wog", "L1_attn_wog", D, D), ("wo", "L1_attn_wo", D, D),
             ("wgate", "L1_attn_w_gate", DHID, D),
             ("wup", "L1_attn_w_up", DHID, D),
             ("wdown", "L1_attn_w_down", D, DHID)]
    w["trans_specs"] = []
    for nm, key, rows, cols in specs:
        raw = cast(nm + "_raw", tn[key], [rows, cols])
        out = dram(nm + "T", [cols, rows])
        w["trans_specs"].append((nm, raw, out, rows, cols))
        w[nm + "T"] = out
    cast("f1gate", tn["L1_ffn_w_gate"], [E1, D, F1])
    cast("f1up", tn["L1_ffn_w_up"], [E1, D, F1])
    cast("f1down", tn["L1_ffn_w_down"], [E1, F1, D])
    cast("wsgate", tn["L1_ffn_ws_gate"], [D, FS])
    cast("wsup", tn["L1_ffn_ws_up"], [D, FS])
    cast("wsdown", tn["L1_ffn_ws_down"], [FS, D])
    cast("sA_raw", tn["L0_ffn_shared_A"], [R0, D])
    cast("wg1_raw", tn["L1_attn_wg1"], [16, D])
    w["rW_raw"] = dram("rW_raw", [16, D])
    p.dma(G, "cast2", lambda e: e.dma_start(out=w["rW_raw"][0:E0, :],
                                           in_=tn["L0_ffn_router_W"]))
    w["wg2_pad"] = dram("wg2_pad", [D, 128])
    p.dma(G, "cast2", lambda e: e.dma_start(out=w["wg2_pad"][:, 0:16],
                                           in_=tn["L1_attn_wg2"]))
    w["sB_pad"] = dram("sB_pad", [D, 128])
    p.dma(G, "cast2", lambda e: e.dma_start(out=w["sB_pad"][:, 0:R0],
                                           in_=tn["L0_ffn_shared_B"]))
    w["rA_raw"] = []
    w["rB_pad"] = []
    for ei in range(E0):
        ra = dram(f"rA_raw{ei}", [R0, D])
        p.dma(G, "cast2", lambda e, ra=ra, ei=ei: e.dma_start(
            out=ra, in_=tn["L0_ffn_routed_A"][ei]))
        w["rA_raw"].append(ra)
        rb = dram(f"rB_pad{ei}", [D, 128])
        p.dma(G, "cast2", lambda e, rb=rb, ei=ei: e.dma_start(
            out=rb[:, 0:R0], in_=tn["L0_ffn_routed_B"][ei]))
        w["rB_pad"].append(rb)
    cast("r1_raw", tn["L1_ffn_router_w"], [D, 8])
    return w


def alloc_common(nc, p, stack, tens, psum, w2):
    c = Ctx()
    c.nc, c.p, c.stack, c.tens = nc, p, stack, tens
    sb = lambda n, s, d=F32: stack.enter_context(nc.sbuf_tensor(n, s, d))
    dram = lambda n, s, d=BF16: nc.dram_tensor(n, s, d, kind="Internal").ap()
    c.dram = dram

    c.resid = sb("resid", [128, TT, D])
    c.dacc = sb("dacc", [128, TT, D], BF16)
    c.xT = sb("xT", [128, KT, N], BF16)
    c.dump = sb("dump", [128, 4, D])
    c.cols = sb("cols", [128, 64])
    c.arena = sb("arena", [128, 24, 512], BF16)
    c.qkv = c.arena[:, 0:12, :]
    c.qb = c.qkv[:, 0:4, :]
    c.kb = c.qkv[:, 4:8, :]
    c.vb = c.qkv[:, 8:12, :]
    c.ogT = c.arena[:, 0:8, :]
    c.og = c.arena[:, 12:20, :].rearrange("p (a b) c -> p a (b c)", b=2)
    c.slab_g = c.arena[:, 0:8, :]
    c.slab_u = c.arena[:, 8:16, :]
    c.slab_d = c.arena[:, 16:24, :].rearrange("p (a b) c -> p a (b c)", b=2)
    c.Pb = sb("Pb", [128, 4, 512], BF16)
    c.Pib = sb("Pib", [128, 4, 512], BF16)
    c.oac = None
    c.sc = sb("sc", [128, 12, 8])
    c.u1 = sb("u1", [128, 4, 512], BF16)
    c.gh_f = c.u1[:, 0:2, :]
    c.uh_f = c.u1[:, 2:4, :]
    c.ghT = c.u1[0:16].rearrange("p a b -> p (a b)")
    c.hT = sb("hT", [128, 512], BF16)
    c.hTs = sb("hTs", [16, 128], BF16)
    c.hTr = sb("hTr", [96, 128], BF16)
    c.w8 = sb("w8", [128, TT, 8])
    c.wbufA = sb("wbufA", [128, 2, 1024], BF16)
    c.wg1T = sb("wg1T", [128, KT, 16], BF16)
    c.wg2T = sb("wg2T", [128, D], BF16)
    c.sharedAT = sb("sharedAT", [128, KT, 16], BF16)
    c.sharedBT = sb("sharedBT", [128, D], BF16)
    c.routedAT = sb("routedAT", [128, KT, E0 * R0], BF16)
    c.routedBcat = sb("routedBcat", [96, D], BF16)
    c.routerWT = sb("routerWT", [128, KT, 16], BF16)
    c.router1 = sb("router1", [128, KT, 8], BF16)
    c.bg1c = sb("bg1c", [16, 1])
    c.rbc = sb("rbc", [128, 8])
    c.bcast = {}
    for nm in ("base_b", "bg2", "bog", "scale1", "scale2"):
        c.bcast[nm] = sb(f"bc_{nm}", [128, D], BF16)
    c.ones1 = sb("ones1", [1, 128])
    c.rowbuf = None
    c.identF = sb("identF_sb", [128, 128])
    c.epsc = sb("epsc", [128, 1])
    c.identB = sb("identB_sb", [128, 128], BF16)
    c.identJ = sb("identJ_sb", [128, 128], BF16)
    c.psum = psum
    c.bank = lambda b, par=128: c.psum[0:par, b * 512:(b + 1) * 512]
    _psB = c.psum.bitcast(BF16)
    c.bankB = lambda b, par=128: _psB[0:par, b * 1024:(b + 1) * 1024]
    c.nstat = 0
    c.rowbuf = c.dump[0:1, 0, :]
    c.oac = c.dump[:, 0:2, :].rearrange("p a (b f) -> p (a b) f", f=512)

    p.dma(Y, "wld", lambda e: e.dma_start(out=c.identF[:], in_=tens["identF"]))
    p.dma(Y, "wld", lambda e: e.dma_start(out=c.identB[:], in_=tens["identB"]))
    p.dma(Y, "wld", lambda e: e.dma_start(out=c.identJ[:], in_=tens["identJ"]))

    tn = tens
    c.trans_specs = w2["trans_specs"]
    c.baseWT = w2["baseWT"]
    c.wqT = w2["wqT"]
    c.wkT = w2["wkT"]
    c.wvT = w2["wvT"]
    c.wogT = w2["wogT"]
    c.woT = w2["woT"]
    c.wgateT = w2["wgateT"]
    c.wupT = w2["wupT"]
    c.wdownT = w2["wdownT"]
    c.f1gate = w2["f1gate"]
    c.f1up = w2["f1up"]
    c.f1down = w2["f1down"]
    c.wsgate = w2["wsgate"]
    c.wsup = w2["wsup"]
    c.wsdown = w2["wsdown"]
    c.sA_raw = w2["sA_raw"]
    c.wg1_raw = w2["wg1_raw"]
    c.rW_raw = w2["rW_raw"]
    c.wg2_pad = w2["wg2_pad"]
    c.sB_pad = w2["sB_pad"]
    c.rA_raw = w2["rA_raw"]
    c.rB_pad = w2["rB_pad"]
    c.r1_raw = w2["r1_raw"]
    cast_done = p.dnow("cast2")

    stg = c.wbufA[:, 0, :]
    for nm, raw, out, rows, cols in c.trans_specs:
        for kk in range(cols // 128):
            for r0 in range(0, rows, 1024):
                rc = min(1024, rows - r0)
                tpt = p.dma(Y, "tps", lambda e, raw=raw, kk=kk, r0=r0, rc=rc: e.dma_start(
                    out=stg[:, 0:rc], in_=raw[r0:r0 + rc, kk * 128:(kk + 1) * 128],
                    transpose=True), waits=(cast_done, p.dnow("stg")))
                p.dma(G, "stg", lambda e, out=out, kk=kk, r0=r0, rc=rc: e.dma_start(
                    out=out[kk * 128:(kk + 1) * 128, r0:r0 + rc], in_=stg[:, 0:rc]),
                    waits=(tpt,))
    for kk in range(KT):
        p.dma(Y, "tp", lambda e, kk=kk: e.dma_start(
            out=c.sharedAT[:, kk, :], in_=c.sA_raw[:, kk * 128:(kk + 1) * 128],
            transpose=True), waits=(cast_done,))
        p.dma(Y, "tp", lambda e, kk=kk: e.dma_start(
            out=c.wg1T[:, kk, :], in_=c.wg1_raw[:, kk * 128:(kk + 1) * 128],
            transpose=True), waits=(cast_done,))
        p.dma(Y, "tp", lambda e, kk=kk: e.dma_start(
            out=c.routerWT[:, kk, :], in_=c.rW_raw[:, kk * 128:(kk + 1) * 128],
            transpose=True), waits=(cast_done,))
        for ei in range(E0):
            p.dma(Y, "tp", lambda e, kk=kk, ei=ei: e.dma_start(
                out=c.routedAT[:, kk, ei * R0:(ei + 1) * R0],
                in_=c.rA_raw[ei][:, kk * 128:(kk + 1) * 128], transpose=True),
                waits=(cast_done,))
        p.dma(Y, "tp", lambda e, kk=kk: e.dma_start(
            out=c.wg2T[:, kk * 128:(kk + 1) * 128],
            in_=c.wg2_pad[kk * 128:(kk + 1) * 128, :], transpose=True),
            waits=(cast_done,))
        p.dma(Y, "tp", lambda e, kk=kk: e.dma_start(
            out=c.sharedBT[:, kk * 128:(kk + 1) * 128],
            in_=c.sB_pad[kk * 128:(kk + 1) * 128, :], transpose=True),
            waits=(cast_done,))
        p.dma(Y, "wld", lambda e, kk=kk: e.dma_start(
            out=c.router1[:, kk, :], in_=c.r1_raw[kk * 128:(kk + 1) * 128, :]),
            waits=(cast_done,))
    rb_stage = c.dacc[:, 0, :]
    for ei in range(E0):
        for kk in range(KT):
            tpt = p.dma(Y, "tpr", lambda e, ei=ei, kk=kk: e.dma_start(
                out=rb_stage[:, 0:128], in_=c.rB_pad[ei][kk * 128:(kk + 1) * 128, :],
                transpose=True), waits=(cast_done, p.dnow("rbmv")))
            p.dma(G, "rbmv", lambda e, ei=ei, kk=kk: e.dma_start(
                out=c.routedBcat[ei * R0:(ei + 1) * R0, kk * 128:(kk + 1) * 128],
                in_=rb_stage[0:R0, 0:128]), waits=(tpt,))

    mset = p.op(V, lambda e: e.memset(c.ones1[:], 1.0), inc=True)
    p.op(V, lambda e: e.memset(c.epsc[:], 1e-5), inc=True, drain=True)
    for nm, src in (("base_b", tn["L0_ffn_base_b"]), ("bg2", tn["L1_attn_bg2"]),
                    ("bog", tn["L1_attn_bog"]), ("scale1", tn["L1_attn_scale1"]),
                    ("scale2", tn["L1_attn_scale2"])):
        bl = p.dma(Y, "rbl", lambda e, src=src: e.dma_start(
            out=c.rowbuf[:], in_=src.unsqueeze(0)), waits=(p.now(T),))

        def bc_mm(e):
            e.matmul(c.bank(0), c.ones1[:], c.rowbuf[:, 0:512], start=True, stop=True)
            return e.matmul(c.bank(1), c.ones1[:], c.rowbuf[:, 512:1024], start=True,
                            stop=True)
        mm = p.op(T, bc_mm, inc=True, waits=(bl, mset, p.now(S)))
        p.op(S, lambda e, nm=nm: e.copy(out=c.bcast[nm][:, 0:512], in_=c.bank(0)),
             waits=(mm,))
        p.op(S, lambda e, nm=nm: e.copy(out=c.bcast[nm][:, 512:1024], in_=c.bank(1)),
             inc=True)
    p.dma(Y, "wld", lambda e: e.dma_start(out=c.bg1c[:],
                                          in_=tn["L1_attn_bg1"].unsqueeze(1)))
    rb_ld = p.dma(Y, "rbl", lambda e: e.dma_start(
        out=c.rowbuf[:, 0:E0], in_=tn["L0_ffn_router_b"].unsqueeze(0)),
        waits=(p.now(T),))
    mm = p.op(T, lambda e: e.matmul(c.psum[0:128, 0:8], c.ones1[:], c.rowbuf[:, 0:8],
                                    start=True, stop=True), inc=True,
              waits=(rb_ld, p.now(S)))
    p.op(S, lambda e: e.copy(out=c.rbc[:], in_=c.psum[0:128, 0:8]), inc=True,
         waits=(mm,))
    p.barrier((T, V, S))
    p.wait(T, p.dnow("tp"), p.dnow("wld"), p.dnow("stg"), p.dnow("rbmv"), p.dnow("tps"), p.dnow("tpr"))
    p.wait(V, p.dnow("tp"), p.dnow("wld"))
    p.wait(S, p.dnow("tp"), p.dnow("wld"), p.dnow("rbmv"))
    p.wait(G, p.dnow("tp"))
    return c



def ln_apply(c, v_ap, out_ap, kind="ln", scale_bc=None, eps=1e-5, rms_eps=1e-6, vwait=()):
    p = c.p
    i = c.nstat % 4
    c.nstat += 1
    sq = c.cols[:, i * 8:i * 8 + 1]
    sm = c.cols[:, i * 8 + 1:i * 8 + 2]
    v1 = c.cols[:, i * 8 + 2:i * 8 + 3]
    v2 = c.cols[:, i * 8 + 3:i * 8 + 4]
    v3 = c.cols[:, i * 8 + 4:i * 8 + 5]
    d0 = c.dump[:, 2, :]
    d1 = c.dump[:, 3, :]
    s1 = p.op(S, lambda e: e.activation(d0, v_ap, AF.Square, accum_out=sq), inc=True,
               waits=tuple(vwait) + (p.now(V),))
    if kind == "ln":
        s2 = p.op(S, lambda e: e.activation(d1, v_ap, AF.Copy, accum_out=sm), inc=True)
        p.op(V, lambda e: e.tensor_scalar(sm, sm, 1.0 / D, None, ALU.mult), drain=True,
             waits=(s2,))
        p.op(V, lambda e: e.tensor_tensor(v1, sm, sm, ALU.mult), drain=True)
        vv2 = p.op(V, lambda e: e.scalar_tensor_tensor(v2, sq, 1.0 / D, v1, ALU.mult,
                                                       ALU.subtract), drain=True,
                   inc=True)
        s3 = p.op(S, lambda e: e.activation(v2, v2, AF.Sqrt, bias=c.epsc[:]), inc=True,
                  waits=(vv2,))
        p.op(V, lambda e: e.reciprocal(v3, v2), drain=True, waits=(s3,))
        r2 = p.op(V, lambda e: e.scalar_tensor_tensor(v1, sm, -1.0, v3, ALU.mult,
                                                      ALU.mult), drain=True, inc=True)
        return p.op(V, lambda e: e.tensor_scalar(out_ap, v_ap, v3, v1, ALU.mult,
                                                 ALU.add), inc=True, drain=True,
                    waits=(r2,))
    else:
        s3 = p.op(S, lambda e: e.activation(v2, sq, AF.Sqrt), inc=True,
                  waits=(s1, p.now(V)))
        p.op(V, lambda e: e.tensor_scalar(v2, v2, 1.0 / np.sqrt(D), rms_eps, ALU.mult,
                                          ALU.add), drain=True, waits=(s3,))
        r1 = p.op(V, lambda e: e.reciprocal(v3, v2), drain=True, inc=True)
        sa = p.op(V, lambda e: e.tensor_scalar(d1, v_ap, v3, None, ALU.mult), inc=True,
                  drain=True, waits=(r1,))
        return p.op(V, lambda e: e.tensor_tensor(out_ap, d1, scale_bc, ALU.mult),
                    inc=True, waits=(sa,), drain=True)


def transpose_act(c, src_fn, dst):
    p = c.p
    prev = {}
    for tt in range(TT):
        for k in range(KT):
            bnk = (tt * KT + k) % 4
            w = (prev[bnk],) if bnk in prev else ()
            mm = p.op(T, lambda e, tt=tt, k=k, bnk=bnk: e.transpose(
                c.bank(bnk)[:, 0:128], src_fn(tt)[:, k * 128:(k + 1) * 128],
                c.identF[:]), inc=True, waits=w)
            prev[bnk] = p.op(S, lambda e, tt=tt, k=k, bnk=bnk: e.copy(
                out=dst[:, k, tt * 128:(tt + 1) * 128], in_=c.bank(bnk)[:, 0:128]),
                inc=True, waits=(mm,))
    p.barrier((T, S, V))


def softmax_topk(c, psum_ap, ncand, keepk, wout_ap, renorm, bias_bc=None, waits=()):
    p = c.p
    i = c.nstat % 4
    c.nstat += 1
    mx = c.cols[:, i * 8 + 5:i * 8 + 6]
    sm = c.cols[:, i * 8 + 6:i * 8 + 7]
    s2 = c.cols[:, i * 8 + 7:i * 8 + 8]
    lg = c.dump[:, 2, 0:8]
    cnt = c.dump[:, 3, 0:8]
    if bias_bc is not None:
        p.op(V, lambda e: e.tensor_tensor(lg[:, 0:ncand], psum_ap,
                                          bias_bc[:, 0:ncand], ALU.add), drain=True,
             waits=tuple(waits))
    else:
        p.op(V, lambda e: e.tensor_copy(lg[:, 0:ncand], psum_ap), drain=True,
             waits=tuple(waits))
    p.op(V, lambda e: e.tensor_reduce(mx, lg[:, 0:ncand], AX.X, ALU.max, negate=True),
         drain=True, inc=True)
    sx = p.op(S, lambda e: e.activation(lg[:, 0:ncand], lg[:, 0:ncand], AF.Exp,
                                        bias=mx, accum_out=sm), inc=True,
              waits=(p.now(V),))
    p.op(V, lambda e: e.reciprocal(sm, sm), drain=True, waits=(sx,))
    p.op(V, lambda e: e.tensor_scalar(lg[:, 0:ncand], lg[:, 0:ncand], sm, None,
                                      ALU.mult), drain=True)
    p.op(V, lambda e: e.memset(cnt[:, 0:ncand], 0.0), drain=True)
    for j in range(ncand):
        p.op(V, lambda e, j=j: e.scalar_tensor_tensor(
            cnt[:, 0:ncand], lg[:, 0:ncand], lg[:, j:j + 1], cnt[:, 0:ncand],
            ALU.is_lt, ALU.add), drain=True)
    p.op(V, lambda e: e.tensor_scalar(cnt[:, 0:ncand], cnt[:, 0:ncand],
                                      float(keepk) - 0.5, None, ALU.is_lt), drain=True)
    if ncand < 8:
        p.op(V, lambda e: e.memset(wout_ap[:, ncand:8], 0.0), drain=True)
    last = p.op(V, lambda e: e.tensor_tensor(wout_ap[:, 0:ncand], lg[:, 0:ncand],
                                             cnt[:, 0:ncand], ALU.mult), inc=True,
                drain=True)
    if renorm:
        p.op(V, lambda e: e.tensor_reduce(s2, wout_ap[:, 0:ncand], AX.X, ALU.add),
             drain=True)
        p.op(V, lambda e: e.tensor_scalar(s2, s2, 1e-6, None, ALU.add), drain=True)
        p.op(V, lambda e: e.reciprocal(s2, s2), drain=True)
        last = p.op(V, lambda e: e.tensor_scalar(wout_ap[:, 0:ncand],
                                                 wout_ap[:, 0:ncand], s2, None,
                                                 ALU.mult), inc=True, drain=True)
    return last



def step1_fused(c, x0_ap, scr):
    p = c.p
    for tt in range(TT):
        b, q = tt // 4, tt % 4
        lw = (p.now(V), p.now(S), p.now(T))
        if tt == 0:
            lw = lw + (p.dnow("rbmv"), p.dnow("tpr"))
        l3 = p.dma(Y, "ld1", lambda e, tt=tt: e.dma_start(
            out=c.dump[:, 2, :], in_=x0_ap[tt * 128:(tt + 1) * 128, :]), waits=lw)
        pf_sb = c.dacc[:, tt, :]
        l1 = p.dma(Y, "ld1", lambda e, pf_sb=pf_sb, b=b, q=q: e.dma_start(
            out=pf_sb, in_=scr[0, b, q * 128:(q + 1) * 128, :]))
        prb = c.Pb[:, 0:2, :].rearrange("p a b -> p (a b)")
        l2 = p.dma(Y, "ld1", lambda e, prb=prb, b=b, q=q: e.dma_start(
            out=prb, in_=scr[1, b, (3 - q) * 128:(4 - q) * 128, :]))

        def mm_j(e, prb=prb):
            e.matmul(c.bank(0), c.identJ[:], prb[:, 0:512], start=True, stop=True)
            return e.matmul(c.bank(1), c.identJ[:], prb[:, 512:1024], start=True,
                            stop=True)
        mmj = p.op(T, mm_j, inc=True, waits=(l2, p.now(V), p.now(S)))
        v = c.dump[:, 0, :]
        p.op(V, lambda e, pf_sb=pf_sb: e.tensor_tensor(
            v, c.dump[:, 2, :], pf_sb, ALU.add), waits=(l1, l3), drain=True)
        p.op(V, lambda e: e.tensor_tensor(v[:, 0:512], v[:, 0:512], c.bank(0),
                                          ALU.add), waits=(mmj,), drain=True)
        vv = p.op(V, lambda e: e.tensor_tensor(v[:, 512:1024], v[:, 512:1024],
                                               c.bank(1), ALU.add),
                  inc=True, drain=True)
        ln_apply(c, v, c.resid[:, tt, :], kind="ln", vwait=(vv,))
    p.barrier((T, V, S))



def step2(c):
    p = c.p
    transpose_act(c, lambda tt: c.resid[:, tt, :], c.xT)
    for tt in range(TT):
        ts = tt * 128

        def mm_small(e, ts=ts):
            ins = None
            for k in range(KT):
                st, sp = (k == 0), (k == KT - 1)
                e.matmul(c.bank(4)[:, 0:16], c.xT[:, k, ts:ts + 128],
                         c.sharedAT[:, k, :], start=st, stop=sp)
                e.matmul(c.bank(6)[:, 0:96], c.xT[:, k, ts:ts + 128],
                         c.routedAT[:, k, :], start=st, stop=sp)
                ins = e.matmul(c.bank(5)[:, 0:8], c.xT[:, k, ts:ts + 128],
                               c.routerWT[:, k, 0:8], start=st, stop=sp)
            return ins
        mm = p.op(T, mm_small, inc=True, waits=(p.now(S), p.now(V)))
        softmax_topk(c, c.bank(5)[:, 0:E0], E0, 2, c.w8[:, tt, :], True, bias_bc=c.rbc,
                     waits=(mm,))
        hsb = c.gh_f.rearrange("p a b -> p (a b)")[:, 0:112]
        ev1 = p.op(S, lambda e, hsb=hsb: e.copy(out=hsb[:, 0:16], in_=c.bank(4)[:, 0:16]),
                   waits=(mm, p.now(V)))
        ev2 = p.op(S, lambda e, hsb=hsb: e.copy(out=hsb[:, 16:112],
                                                in_=c.bank(6)[:, 0:96]), inc=True)
        for ei in range(E0):
            hv = p.op(V, lambda e, ei=ei, hsb=hsb, tt=tt: e.tensor_scalar(
                hsb[:, 16 + ei * R0:16 + (ei + 1) * R0],
                hsb[:, 16 + ei * R0:16 + (ei + 1) * R0],
                c.w8[:, tt, ei:ei + 1], None, ALU.mult), drain=True, waits=(ev2,),
                inc=(ei == E0 - 1))
        mm2 = p.op(T, lambda e, hsb=hsb: e.transpose(c.bankB(7)[0:16, 0:128],
                                                     hsb[:, 0:16], c.identB[:]),
                   waits=(hv, p.now(S)))
        mm2b = p.op(T, lambda e, hsb=hsb: e.transpose(c.bankB(6)[0:96, 0:128],
                                                      hsb[:, 16:112], c.identB[:]),
                    inc=True)
        p.op(S, lambda e: e.copy(out=c.hTs[:], in_=c.bankB(7)[0:16, 0:128]),
             waits=(mm2b,))
        evT = p.op(S, lambda e: e.copy(out=c.hTr[:], in_=c.bankB(6)[0:96, 0:128]),
                   inc=True)

        def mm_main(e, ts=ts):
            ins = None
            for half in range(2):
                f0 = half * 512
                for k in range(KT):
                    e.matmul(c.bank(half), c.xT[:, k, ts:ts + 128],
                             c.wbufA[:, (half * KT + k) % 2, 0:512],
                             start=(k == 0), stop=False)
                e.matmul(c.bank(half), c.hT[0:16, 0:128], c.sharedBT[0:16, f0:f0 + 512],
                         start=False, stop=False)
                ins = e.matmul(c.bank(half), c.hT[16:112, 0:128],
                               c.routedBcat[:, f0:f0 + 512], start=False, stop=True)
            return ins
        slot_consumer = {0: p.now(T), 1: p.now(T)}
        for half in range(2):
            f0 = half * 512
            for k in range(KT):
                slot = (half * KT + k) % 2
                wld = p.dma(Y, f"ws{slot}", lambda e, k=k, f0=f0, slot=slot: e.dma_start(
                    out=c.wbufA[:, slot, 0:512],
                    in_=c.baseWT[k * 128:(k + 1) * 128, f0:f0 + 512]),
                    waits=(slot_consumer[slot],))
                slot_consumer[slot] = p.op(T, lambda e, k=k, half=half, ts=ts, slot=slot: e.matmul(
                    c.bank(half), c.xT[:, k, ts:ts + 128], c.wbufA[:, slot, 0:512],
                    start=(k == 0), stop=False), inc=True,
                    waits=(wld,) + ((evT,) if (half == 0 and k == 0) else ()))
            p.op(T, lambda e, half=half, f0=f0: e.matmul(
                c.bank(half), c.hTs[:], c.sharedBT[0:16, f0:f0 + 512],
                start=False, stop=False))
            p.op(T, lambda e, half=half, f0=f0: e.matmul(
                c.bank(half), c.hTr[:], c.routedBcat[:, f0:f0 + 512],
                start=False, stop=True), inc=True)
        mm3 = p.now(T)
        v = c.dump[:, 0, :]
        p.op(V, lambda e: e.tensor_tensor(v[:, 0:512], c.bank(0),
                                          c.bcast["base_b"][:, 0:512], ALU.add),
             waits=(mm3,))
        p.op(V, lambda e: e.tensor_tensor(v[:, 512:1024], c.bank(1),
                                          c.bcast["base_b"][:, 512:1024], ALU.add),
             drain=True)
        vv = p.op(V, lambda e, tt=tt: e.tensor_tensor(v, v, c.resid[:, tt, :], ALU.add),
                  inc=True, drain=True)
        ln_apply(c, v, c.resid[:, tt, :], kind="ln", vwait=(vv,))
    p.barrier((T, V, S))



def step3(c, upto3='all'):
    p = c.p
    if upto3 == 'rms1':
        for tt in range(TT):
            ln_apply(c, c.resid[:, tt, :], c.dacc[:, tt, :], kind="rms",
                     scale_bc=c.bcast["scale1"][:])
        p.barrier((T, V, S))
        return
    for tt in range(TT):
        ln_apply(c, c.resid[:, tt, :], c.dump[:, 1, :], kind="rms",
                 scale_bc=c.bcast["scale1"][:])
        for k in range(KT):
            bnk = k % 4
            mm = p.op(T, lambda e, k=k, bnk=bnk: e.transpose(
                c.bank(bnk)[:, 0:128], c.dump[:, 1, k * 128:(k + 1) * 128],
                c.identF[:]), inc=True, waits=(p.now(V), p.now(S)))
            p.op(S, lambda e, k=k, bnk=bnk, tt=tt: e.copy(
                out=c.xT[:, k, tt * 128:(tt + 1) * 128], in_=c.bank(bnk)[:, 0:128]),
                inc=True, waits=(mm,))
        p.barrier((T, V, S))
    for fc in range(4):
        f0 = fc * 512

        def mm_gh(e, f0=f0):
            ins = None
            for k in range(KT):
                ins = e.matmul(c.bank(4, 16), c.wg1T[:, k, :], c.xT[:, k, f0:f0 + 512],
                               start=(k == 0), stop=(k == KT - 1))
            return ins
        mm = p.op(T, mm_gh, inc=True, waits=(p.now(S),))
        p.op(S, lambda e, f0=f0: e.activation(c.ghT[:, f0:f0 + 512], c.bank(4, 16),
                                              AF.Silu, bias=c.bg1c[:]), inc=True,
             waits=(mm,))
    p.barrier((T, V, S))
    if upto3 == 'xt':
        p.op(V, lambda e: e.memset(c.dacc[:], 0.0), inc=True, drain=True)
        for k in range(KT):
            p.op(V, lambda e, k=k: e.tensor_copy(c.dacc[:, k, :],
                                                 c.xT[:, k, 0:1024]), inc=True,
                 drain=True)
        p.barrier((T, V, S))
        return
    p.op(V, lambda e: e.memset(c.dacc[:], 0.0), inc=True, drain=True)
    pairs = [(t, s) for t in range(4) for s in range(t + 1)]
    for sub in range(4):
        tiles = [sub + 4 * b for b in range(4)]
        for hh in range(2):
            h0 = hh * 512
            for bi in range(4):
                ts = tiles[bi] * 128
                mm = p.op(T, lambda e, bi=bi, ts=ts, h0=h0: e.matmul(
                    c.bank(bi), c.ghT[:, ts:ts + 128], c.wg2T[0:16, h0:h0 + 512],
                    start=True, stop=True), inc=True, waits=(p.now(S), p.now(V)))
                zf = c.dump[:, bi % 2, 0:512]
                zv = p.op(V, lambda e, bi=bi, zf=zf, h0=h0: e.tensor_tensor(
                    zf, c.bank(bi), c.bcast["bg2"][:, h0:h0 + 512], ALU.add),
                    inc=True, drain=True, waits=(mm,))
                p.op(S, lambda e, bi=bi, zf=zf: e.activation(
                    c.Pb[:, bi, :], zf, AF.Sigmoid, scale=1.0 / GLA_TEMP), inc=True,
                    waits=(zv,))
                se = p.op(S, lambda e, bi=bi, zf=zf: e.activation(
                    c.dump[:, 2 + bi % 2, 0:512], zf, AF.Exp, scale=-1.0 / GLA_TEMP),
                    inc=True)
                p.op(V, lambda e, bi=bi: e.tensor_scalar(
                    c.Pib[:, bi, :], c.dump[:, 2 + bi % 2, 0:512], 1.0, None, ALU.add),
                    inc=True, drain=True, waits=(se,))
            for bi in range(1, 4):
                p.op(V, lambda e, bi=bi: e.tensor_tensor(
                    c.Pb[:, bi, :], c.Pb[:, bi, :], c.Pb[:, bi - 1, :], ALU.mult),
                    drain=True)
            for bi in range(1, 4):
                p.op(V, lambda e, bi=bi: e.tensor_tensor(
                    c.Pib[:, bi, :], c.Pib[:, bi, :], c.Pib[:, bi - 1, :], ALU.mult),
                    drain=True, inc=(bi == 3))
            if upto3 == 'al1' and sub == DBG_SUB and hh == 0:
                p.barrier((T, V, S))
                return
            slot_consumer = {0: p.now(T), 1: p.now(T)}
            for wd, kind in ((c.wqT, "q"), (c.wkT, "k"), (c.wvT, "v"), (c.wogT, "og")):
                for k in range(KT):
                    slot = k % 2
                    wld = p.dma(Y, f"ws{slot}", lambda e, wd=wd, k=k, h0=h0,
                                slot=slot: e.dma_start(
                        out=c.wbufA[:, slot, 0:512],
                        in_=wd[k * 128:(k + 1) * 128, h0:h0 + 512]),
                        waits=(slot_consumer[slot],))

                    def mm_w(e, k=k, tiles=tiles):
                        ins = None
                        for bi in range(4):
                            ts = tiles[bi] * 128
                            ins = e.matmul(c.bank(4 + bi), c.xT[:, k, ts:ts + 128],
                                           c.wbufA[:, k % 2, 0:512], start=(k == 0),
                                           stop=(k == KT - 1))
                        return ins
                    last_mm = p.op(T, mm_w, inc=True,
                                   waits=(wld,) + ((p.now(S), p.now(V)) if k == 0 else ()))
                    slot_consumer[slot] = last_mm
                sil = {}
                for bi in range(4):
                    if kind == "q":
                        p.op(S, lambda e, bi=bi: e.copy(out=c.qb[:, bi, :],
                                                        in_=c.bank(4 + bi)), inc=True,
                             waits=(last_mm,))
                    elif kind == "k":
                        p.op(S, lambda e, bi=bi: e.copy(out=c.kb[:, bi, :],
                                                        in_=c.bank(4 + bi)), inc=True,
                             waits=(last_mm,))
                    elif kind == "v":
                        p.op(S, lambda e, bi=bi: e.copy(out=c.vb[:, bi, :],
                                                        in_=c.bank(4 + bi)), inc=True,
                             waits=(last_mm,))
                    else:
                        ov = p.op(V, lambda e, bi=bi, h0=h0: e.tensor_tensor(
                            c.dump[:, bi % 2, 0:512], c.bank(4 + bi),
                            c.bcast["bog"][:, h0:h0 + 512], ALU.add), inc=True,
                            drain=True,
                            waits=(last_mm,) + ((sil[bi - 2],) if bi - 2 in sil else ()))
                        sil[bi] = p.op(S, lambda e, bi=bi, h0=h0: e.activation(
                            c.og[:, bi, h0:h0 + 512], c.dump[:, bi % 2, 0:512],
                            AF.Silu), inc=True, waits=(ov,))
            p.barrier((T, V, S))
            if upto3 in ('q', 'k', 'v', 'og', 'alpha') and sub == DBG_SUB and hh == 0:
                src = {'q': c.qb, 'k': c.kb, 'v': c.vb, 'og': c.og, 'alpha': c.Pb}[upto3]
                wid = 1024 if upto3 == 'og' else 512
                for bi in range(4):
                    p.op(V, lambda e, bi=bi, src=src, wid=wid: e.tensor_copy(
                        c.dacc[:, bi, 0:wid], src[:, bi, 0:wid]), inc=True, drain=True)
                p.barrier((T, V, S))
                return
            for bi in range(4):
                pi = pairs.index((bi, bi))
                tm = c.dump[:, 2 + bi % 2, 0:512] if bi < 2 else c.dump[:, 2 + bi % 2, 512:1024]
                p.op(V, lambda e, bi=bi, tm=tm: e.tensor_tensor(
                    tm, c.qb[:, bi, :], c.kb[:, bi, :], ALU.mult), drain=True)
                p.op(V, lambda e, pi=pi, tm=tm: e.tensor_reduce(
                    c.sc[:, pi, :], tm.rearrange("p (h k) -> p h k", k=DK), AX.X,
                    ALU.add), drain=True, inc=True)
            for bi in range(4):
                p.op(V, lambda e, bi=bi: e.tensor_tensor(
                    c.qb[:, bi, :], c.qb[:, bi, :], c.Pb[:, bi, :], ALU.mult))
            for bi in range(4):
                p.op(V, lambda e, bi=bi: e.tensor_tensor(
                    c.kb[:, bi, :], c.kb[:, bi, :], c.Pib[:, bi, :], ALU.mult),
                    inc=(bi == 3))
            for pi, (t_, s_) in enumerate(pairs):
                if t_ == s_:
                    continue
                tm = c.dump[:, 2 + pi % 2, 0:512] if pi % 4 < 2 else c.dump[:, 2 + pi % 2, 512:1024]
                p.op(V, lambda e, t_=t_, s_=s_, tm=tm: e.tensor_tensor(
                    tm, c.qb[:, t_, :], c.kb[:, s_, :], ALU.mult), drain=True)
                p.op(V, lambda e, pi=pi, tm=tm: e.tensor_reduce(
                    c.sc[:, pi, :], tm.rearrange("p (h k) -> p h k", k=DK), AX.X,
                    ALU.add), drain=True, inc=True)
            p.op(V, lambda e: e.memset(c.oac[:], 0.0), drain=True)
            for pi, (t_, s_) in enumerate(pairs):
                tm = c.dump[:, 2 + pi % 2, 0:512] if pi % 4 < 2 else c.dump[:, 2 + pi % 2, 512:1024]
                scbc = c.sc[:, pi, :].unsqueeze(2).to_broadcast([128, 8, DK])
                p.op(V, lambda e, s_=s_, tm=tm, scbc=scbc: e.tensor_tensor(
                    tm.rearrange("p (h k) -> p h k", k=DK), scbc,
                    c.vb[:, s_, :].rearrange("p (h k) -> p h k", k=DK), ALU.mult),
                    drain=True)
                p.op(V, lambda e, t_=t_, tm=tm: e.tensor_tensor(
                    c.oac[:, t_, :], c.oac[:, t_, :], tm, ALU.add), drain=True)
            if upto3 == 'o' and sub == DBG_SUB and hh == 0:
                for bi in range(4):
                    p.op(V, lambda e, bi=bi: e.tensor_copy(
                        c.dacc[:, bi, 0:512], c.oac[:, bi, :]), inc=True, drain=True)
                p.barrier((T, V, S))
                return
            for bi in range(4):
                i8 = c.sc[:, 10, :]
                r8 = c.sc[:, 11, :]
                t8 = c.cols[:, 40:48]
                ov = c.oac[:, bi, :]
                o3 = ov.rearrange("p (h k) -> p h k", k=DK)
                p.op(V, lambda e, o3=o3, i8=i8: e.tensor_reduce(i8, o3, AX.X, ALU.add),
                     drain=True, inc=True)
                sq = c.dump[:, 2, 0:512]
                sqo = p.op(S, lambda e, ov=ov, sq=sq: e.activation(sq, ov, AF.Square),
                           inc=True, waits=(p.now(V),))
                p.op(V, lambda e, sq=sq, r8=r8: e.tensor_reduce(
                    r8, sq.rearrange("p (h k) -> p h k", k=DK), AX.X, ALU.add),
                    drain=True, waits=(sqo,))
                p.op(V, lambda e, i8=i8: e.tensor_scalar(i8, i8, 1.0 / DK, None,
                                                         ALU.mult), drain=True)
                p.op(V, lambda e, i8=i8, t8=t8: e.tensor_tensor(t8, i8, i8, ALU.mult),
                     drain=True)
                p.op(V, lambda e, r8=r8, t8=t8: e.scalar_tensor_tensor(
                    r8, r8, 1.0 / DK, t8, ALU.mult, ALU.subtract), drain=True)
                sr = p.op(S, lambda e, r8=r8: e.activation(r8, r8, AF.Sqrt, bias=c.epsc[:]),
                          inc=True, waits=(p.now(V),))
                p.op(V, lambda e, r8=r8: e.reciprocal(r8, r8), drain=True, waits=(sr,))
                p.op(V, lambda e, o3=o3, i8=i8: e.tensor_tensor(
                    o3, o3, i8.unsqueeze(2).to_broadcast([128, 8, DK]), ALU.subtract),
                    drain=True)
                p.op(V, lambda e, o3=o3, r8=r8: e.tensor_tensor(
                    o3, o3, r8.unsqueeze(2).to_broadcast([128, 8, DK]), ALU.mult),
                    drain=True)
                p.op(V, lambda e, bi=bi, ov=ov, h0=h0: e.tensor_tensor(
                    c.og[:, bi, h0:h0 + 512], ov, c.og[:, bi, h0:h0 + 512], ALU.mult),
                    inc=True, drain=True)
            p.barrier((V, S, T))
            if upto3 == 'oln' and sub == DBG_SUB and hh == 0:
                for bi in range(4):
                    p.op(V, lambda e, bi=bi: e.tensor_copy(
                        c.dacc[:, bi, 0:512], c.oac[:, bi, :]), inc=True, drain=True)
                p.barrier((T, V, S))
                return
        if upto3 == 'og2' and sub == DBG_SUB:
            for bi in range(4):
                p.op(V, lambda e, bi=bi: e.tensor_copy(
                    c.dacc[:, bi, :], c.og[:, bi, :]), inc=True, drain=True)
            p.barrier((T, V, S))
            return
        if upto3 == 'hh1' and sub == DBG_SUB:
            p.barrier((T, V, S))
            return
        for bi in range(4):
            for k in range(KT):
                bnk = k % 4
                mm = p.op(T, lambda e, bi=bi, k=k, bnk=bnk: e.transpose(
                    c.bankB(bnk)[:, 0:128], c.og[:, bi, k * 128:(k + 1) * 128],
                    c.identB[:]), inc=True, waits=(p.now(S), p.now(V)))
                p.op(S, lambda e, bi=bi, k=k, bnk=bnk: e.copy(
                    out=c.ogT[:, k, bi * 128:(bi + 1) * 128],
                    in_=c.bankB(bnk)[:, 0:128]), inc=True, waits=(mm,))
        p.barrier((T, S, V))
        slot_consumer = {0: p.now(T), 1: p.now(T)}
        for bi in range(4):
            tt = tiles[bi]
            for k in range(KT):
                slot = k % 2
                wld = p.dma(Y, f"ws{slot}", lambda e, k=k, slot=slot: e.dma_start(
                    out=c.wbufA[:, slot, :], in_=c.woT[k * 128:(k + 1) * 128, :]),
                    waits=(slot_consumer[slot],))

                def mm_wo(e, bi=bi, k=k, slot=slot):
                    e.matmul(c.bank(4), c.ogT[:, k, bi * 128:(bi + 1) * 128],
                             c.wbufA[:, slot, 0:512], start=(k == 0),
                             stop=(k == KT - 1))
                    return e.matmul(c.bank(5), c.ogT[:, k, bi * 128:(bi + 1) * 128],
                                    c.wbufA[:, slot, 512:1024], start=(k == 0),
                                    stop=(k == KT - 1))
                slot_consumer[slot] = p.op(T, mm_wo, inc=True,
                     waits=(wld,) + ((p.now(V), p.now(S)) if k == 0 else ()))
            ev = p.op(V, lambda e, tt=tt: e.tensor_copy(c.dacc[:, tt, 0:512],
                                                        c.bank(4)), drain=True,
                      waits=(p.now(T),))
            p.op(V, lambda e, tt=tt: e.tensor_copy(c.dacc[:, tt, 512:1024],
                                                   c.bank(5)), inc=True, drain=True)
        if upto3 == 'wo1' and sub == DBG_SUB:
            p.barrier((T, V, S))
            return
    p.barrier((T, V, S))
    if upto3 == 'gla':
        return
    for tt in range(TT):
        v = c.dump[:, 1, :]
        vv = p.op(V, lambda e, tt=tt: e.tensor_tensor(v, c.resid[:, tt, :],
                                                      c.dacc[:, tt, :], ALU.add),
                  inc=True, drain=True, waits=(p.now(S),))
        ln_apply(c, v, c.dump[:, 0, :], kind="rms", scale_bc=c.bcast["scale2"][:],
                 vwait=(vv,))
        for k in range(KT):
            bnk = k % 4
            mm = p.op(T, lambda e, k=k, bnk=bnk: e.transpose(
                c.bank(bnk)[:, 0:128], c.dump[:, 0, k * 128:(k + 1) * 128],
                c.identF[:]), inc=True, waits=(p.now(V), p.now(S)))
            p.op(S, lambda e, k=k, bnk=bnk, tt=tt: e.copy(
                out=c.xT[:, k, tt * 128:(tt + 1) * 128], in_=c.bank(bnk)[:, 0:128]),
                inc=True, waits=(mm,))
        p.barrier((T, V, S))
    if upto3 == 'rms2':
        return
    ffn_slab(c, [("g", c.wgateT, c.wupT, c.wdownT, None, DHID // 512)], c.dacc, step=3)
    p.barrier((T, V, S))
    for tt in range(TT):
        v = c.dump[:, 1, :]
        vv = p.op(V, lambda e, tt=tt: e.scalar_tensor_tensor(
            v, c.resid[:, tt, :], 2.0, c.dacc[:, tt, :], ALU.mult, ALU.add),
            inc=True, drain=True, waits=(p.now(S),))
        ln_apply(c, v, c.resid[:, tt, :], kind="ln", vwait=(vv,))
    p.barrier((T, V, S))



def step4(c, out_ap):
    p = c.p
    transpose_act(c, lambda tt: c.resid[:, tt, :], c.xT)
    for tt in range(TT):
        ts = tt * 128

        def mm_r(e, ts=ts):
            ins = None
            for k in range(KT):
                ins = e.matmul(c.bank(6)[:, 0:8], c.xT[:, k, ts:ts + 128],
                               c.router1[:, k, :], start=(k == 0), stop=(k == KT - 1))
            return ins
        mm = p.op(T, mm_r, inc=True, waits=(p.now(S), p.now(V)))
        p.wait(V, mm)
        softmax_topk(c, c.bank(6)[:, 0:E1], E1, 6, c.w8[:, tt, :], False)
    p.barrier((T, V, S))
    p.op(V, lambda e: e.memset(c.dacc[:], 0.0), inc=True, drain=True)
    specs = []
    for ei in range(E1):
        specs.append(("exp", ei, c.f1gate[ei], c.f1up[ei], c.f1down[ei]))
    specs.append(("shr", 0, c.wsgate[:, 0:512], c.wsup[:, 0:512], c.wsdown[0:512, :]))
    specs.append(("shr", 1, c.wsgate[:, 512:1024], c.wsup[:, 512:1024],
                  c.wsdown[512:1024, :]))
    ffn_slab(c, [(kind, wg, wu, wd, ei, 1) for kind, ei, wg, wu, wd in specs],
             c.dacc, step=4)
    p.barrier((T, V, S))
    for tt in range(TT):
        v = c.dump[:, 1, :]
        vv = p.op(V, lambda e, tt=tt: e.tensor_tensor(v, c.resid[:, tt, :],
                                                      c.dacc[:, tt, :], ALU.add),
                  inc=True, drain=True, waits=(p.now(S),))
        lo = ln_apply(c, v, c.dump[:, 0, :], kind="ln", vwait=(vv,))
        p.dma(G, "st", lambda e, tt=tt: e.dma_start(
            out=out_ap[tt * 128:(tt + 1) * 128, :], in_=c.dump[:, 0, :]),
            waits=(lo,))
        p.wait(V, p.dnow("st"))
        p.wait(S, p.dnow("st"))
    p.wait(G, p.dnow("st"))
    p.op(G, lambda e: e.nop())


def ffn_slab(c, units, dacc, step):
    p = c.p
    slab_last_T = p.now(T)
    zS = ("e", S, 0)
    zV = ("e", V, 0)
    zT = ("e", T, 0)
    sgsu_tok = {}
    hv_tok = {}
    cp_tok = {}
    dn_tok = {}
    dacc_tok = {}
    ti = -1
    for kind, wg, wu, wd, ei, nfc in units:
        for fc in range(nfc):
            f0 = fc * 512
            lds = []
            for k in range(KT):
                if step == 3:
                    ga = wg[k * 128:(k + 1) * 128, f0:f0 + 512]
                    ua = wu[k * 128:(k + 1) * 128, f0:f0 + 512]
                else:
                    ga = wg[k * 128:(k + 1) * 128, 0:512]
                    ua = wu[k * 128:(k + 1) * 128, 0:512]
                lds.append(p.dma(Y, "wsl", lambda e, k=k, ga=ga: e.dma_start(
                    out=c.slab_g[:, k, :], in_=ga), waits=(slab_last_T,)))
                lds.append(p.dma(G, "wsl2", lambda e, k=k, ua=ua: e.dma_start(
                    out=c.slab_u[:, k, :], in_=ua), waits=(slab_last_T,)))
            for k4 in range(4):
                if step == 3:
                    da = wd[f0 + k4 * 128:f0 + (k4 + 1) * 128, :]
                else:
                    da = wd[k4 * 128:(k4 + 1) * 128, :]
                lds.append(p.dma(Y, "wsl", lambda e, k4=k4, da=da: e.dma_start(
                    out=c.slab_d[:, k4, :], in_=da), waits=(slab_last_T,)))
            slab_ok = (p.dnow("wsl"), p.dnow("wsl2"))
            for tt in range(TT):
                ti += 1
                ts = tt * 128
                q = ti % 2
                gl = c.u1[:, 2 * q, :]
                ul = c.u1[:, 2 * q + 1, :]

                def mm_gu(e, ts=ts):
                    ins = None
                    for k in range(KT):
                        e.matmul(c.bank(0), c.xT[:, k, ts:ts + 128],
                                 c.slab_g[:, k, :], start=(k == 0), stop=(k == KT - 1))
                        ins = e.matmul(c.bank(1), c.xT[:, k, ts:ts + 128],
                                       c.slab_u[:, k, :], start=(k == 0),
                                       stop=(k == KT - 1))
                    return ins
                gu = p.op(T, mm_gu, inc=True,
                          waits=slab_ok + (sgsu_tok.get(ti - 1, zS),))
                sg = p.op(S, lambda e, gl=gl: e.activation(gl, c.bank(0),
                                                           AF.Silu), inc=True,
                          waits=(gu, hv_tok.get(ti - 2, zV)))
                su = p.op(S, lambda e, ul=ul: e.copy(out=ul, in_=c.bank(1)),
                          inc=True)
                sgsu_tok[ti] = su
                hv = p.op(V, lambda e, gl=gl, ul=ul: e.tensor_tensor(
                    ul, gl, ul, ALU.mult),
                    inc=True, drain=True, waits=(su,))
                if kind == "exp":
                    hv = p.op(V, lambda e, ei=ei, tt=tt, ul=ul: e.tensor_scalar(
                        ul, ul, c.w8[:, tt, ei:ei + 1],
                        None, ALU.mult), inc=True, drain=True)
                hv_tok[ti] = hv
                last_cp = None
                for k4 in range(4):
                    bnk = 2 + k4 % 2
                    mm = p.op(T, lambda e, k4=k4, bnk=bnk, ul=ul: e.transpose(
                        c.bankB(bnk)[:, 0:128], ul[:, k4 * 128:(k4 + 1) * 128],
                        c.identB[:]), inc=True,
                        waits=(hv, cp_tok.get(k4 % 2, zS)))
                    last_cp = p.op(S, lambda e, k4=k4, bnk=bnk: e.copy(
                        out=c.hT[:, k4 * 128:(k4 + 1) * 128],
                        in_=c.bankB(bnk)[:, 0:128]), inc=True,
                        waits=(mm, dn_tok.get(ti - 1, zT)) if k4 == 0 else (mm,))
                    cp_tok[k4 % 2] = last_cp

                def mm_dn(e):
                    ins = None
                    for k4 in range(4):
                        e.matmul(c.bank(4), c.hT[:, k4 * 128:(k4 + 1) * 128],
                                 c.slab_d[:, k4, 0:512], start=(k4 == 0),
                                 stop=(k4 == 3))
                        ins = e.matmul(c.bank(5), c.hT[:, k4 * 128:(k4 + 1) * 128],
                                       c.slab_d[:, k4, 512:1024], start=(k4 == 0),
                                       stop=(k4 == 3))
                    return ins
                dn = p.op(T, mm_dn, inc=True,
                          waits=(last_cp, dacc_tok.get(ti - 1, zV)))
                dn_tok[ti] = dn
                slab_last_T = dn
                p.op(V, lambda e, tt=tt: e.tensor_tensor(
                    dacc[:, tt, 0:512], dacc[:, tt, 0:512], c.bank(4), ALU.add),
                    drain=True, waits=(dn,))
                dacc_tok[ti] = p.op(V, lambda e, tt=tt: e.tensor_tensor(
                    dacc[:, tt, 512:1024], dacc[:, tt, 512:1024], c.bank(5), ALU.add),
                    inc=True, drain=True)



WNAMES = [
    "L0_ffn_base_W", "L0_ffn_base_b", "L0_ffn_shared_A", "L0_ffn_shared_B",
    "L0_ffn_routed_A", "L0_ffn_routed_B", "L0_ffn_router_W", "L0_ffn_router_b",
    "L1_attn_scale1", "L1_attn_scale2", "L1_attn_wq", "L1_attn_wk", "L1_attn_wv",
    "L1_attn_wg1", "L1_attn_bg1", "L1_attn_wg2", "L1_attn_bg2", "L1_attn_wog",
    "L1_attn_bog", "L1_attn_wo", "L1_attn_w_gate", "L1_attn_w_up", "L1_attn_w_down",
    "L1_ffn_router_w", "L1_ffn_w_gate", "L1_ffn_w_up", "L1_ffn_w_down",
    "L1_ffn_ws_gate", "L1_ffn_ws_up", "L1_ffn_ws_down",
]

_WSHAPES = {
    "L0_ffn_base_W": (1024, 1024), "L0_ffn_base_b": (1024,),
    "L0_ffn_shared_A": (16, 1024), "L0_ffn_shared_B": (1024, 16),
    "L0_ffn_routed_A": (6, 16, 1024), "L0_ffn_routed_B": (6, 1024, 16),
    "L0_ffn_router_W": (6, 1024), "L0_ffn_router_b": (6,),
    "L1_attn_scale1": (1024,), "L1_attn_scale2": (1024,),
    "L1_attn_wq": (1024, 1024), "L1_attn_wk": (1024, 1024),
    "L1_attn_wv": (1024, 1024), "L1_attn_wg1": (16, 1024),
    "L1_attn_bg1": (16,), "L1_attn_wg2": (1024, 16), "L1_attn_bg2": (1024,),
    "L1_attn_wog": (1024, 1024), "L1_attn_bog": (1024,),
    "L1_attn_wo": (1024, 1024), "L1_attn_w_gate": (4096, 1024),
    "L1_attn_w_up": (4096, 1024), "L1_attn_w_down": (1024, 4096),
    "L1_ffn_router_w": (1024, 8), "L1_ffn_w_gate": (8, 1024, 512),
    "L1_ffn_w_up": (8, 1024, 512), "L1_ffn_w_down": (8, 512, 1024),
    "L1_ffn_ws_gate": (1024, 1024), "L1_ffn_ws_up": (1024, 1024),
    "L1_ffn_ws_down": (1024, 1024),
}


def _build_fused():
    nc = bass.Bass("TRN2", target_bir_lowering=False)
    tens = {}
    ein = lambda nm, shp, dt=F32: nc.dram_tensor(nm, list(shp), dt,
                                                 kind="ExternalInput").ap()
    tens["x0sched"] = ein("x0sched", [9, 4, 512, 1024])
    for s in "fr":
        tens[f"in_proj_{s}"] = ein(f"in_proj_{s}", [4096, 1024])
        tens[f"out_proj_{s}"] = ein(f"out_proj_{s}", [1024, 2048])
        tens[f"x_proj_{s}"] = ein(f"x_proj_{s}", [96, 2048])
        tens[f"dt_w_{s}"] = ein(f"dt_w_{s}", [2048, 64])
        tens[f"smallsF_{s}"] = ein(f"smallsF_{s}", [128, 368])
    tens["sel16"] = ein("sel16", [16, 2048], BF16)
    tens["masks"] = ein("masks", [128, 35])
    for nm in WNAMES:
        tens[nm] = ein(nm, _WSHAPES[nm])
    tens["x0s"] = ein("x0s", [2048, 1024])
    tens["identF"] = ein("identF", [128, 128])
    tens["identB"] = ein("identB", [128, 128], BF16)
    tens["identJ"] = ein("identJ", [128, 128], BF16)
    out = nc.dram_tensor("out", [2048, 1024], F32, kind="ExternalOutput").ap()

    p = Prog(inc_all=True)
    stack = ExitStack()
    with stack:
        psum = stack.enter_context(nc.psum_tensor("psum", [128, 4096], F32))
        w2 = phase2_casts(nc, p, tens)
        st1 = ExitStack()
        scr = build_mamba_fused(nc, p, st1, tens, psum)
        p.barrier(tuple(ENGINES))
        chans = list(p.dcnt)
        for eng in ENGINES:
            p.wait(eng, *[p.dnow(ch) for ch in chans])
        st1.close()
        c = alloc_common(nc, p, stack, tens, psum, w2)
        step1_fused(c, tens["x0s"], scr)
        step2(c)
        step3(c)
        step4(c, out)
        p.build(nc, stack)
    return nc


def kernel(**inputs):
    inputs = {k: np.asarray(v, np.float32) for k, v in inputs.items()}
    x_full = inputs["L0_attn_hidden_states"]
    flip = x_full[:, ::-1, :]

    nc = _build_fused()

    sel = np.zeros((16, 2048), BF)
    for n in range(16):
        sel[n, n * 128:(n + 1) * 128] = 1
    identF = np.eye(128, dtype=np.float32)
    identB = np.eye(128, dtype=BF)
    identJ = np.eye(128, dtype=np.float32)[::-1].astype(BF)
    smallsF = {}
    for s in "fr":
        A = np.asarray(inputs[f"L0_attn_A_log_{s}"]).reshape(16, 128, 16)
        A = A.transpose(1, 0, 2).reshape(128, 256)
        dtb = inputs[f"L0_attn_dt_b_{s}"].reshape(16, 128).T
        convb = inputs[f"L0_attn_conv_b_{s}"].reshape(16, 128).T
        Dp = inputs[f"L0_attn_D_{s}"].reshape(16, 128).T
        convw = inputs[f"L0_attn_conv_w_{s}"].reshape(2048, 4)
        convw = convw.reshape(16, 128, 4).transpose(1, 0, 2).reshape(128, 64)
        smallsF[s] = np.ascontiguousarray(
            np.concatenate([A, dtb, convb, Dp, convw], axis=1), np.float32)

    in_maps = []
    for cid in range(8):
        keep = np.ones(7, np.float32)
        keep[0] = 0.0
        if cid < 7:
            keep[cid] = 0.0
        sf = np.zeros(7, np.float32)
        if cid >= 1:
            sf[cid - 1] = 1.0
        sr = np.zeros(7, np.float32)
        if cid < 7:
            sr[6] = 1.0
        mrow = np.concatenate([keep, sf, 1.0 - sf, sr, 1.0 - sr])
        masks = np.ascontiguousarray(np.tile(mrow, (128, 1)), np.float32)

        x0sched = np.zeros((9, 4, 512, 1024), np.float32)
        for b in range(4):
            for pos in range(7):
                if pos < cid:
                    x0sched[pos, b] = x_full[b, 512 * pos:512 * (pos + 1)]
                else:
                    k = pos - cid
                    x0sched[pos, b] = flip[b, 512 * k:512 * (k + 1)]
            x0sched[7, b] = x_full[b, 512 * cid:512 * (cid + 1)]
            x0sched[8, b] = flip[b, 512 * (7 - cid):512 * (8 - cid)]
        x0s2 = np.ascontiguousarray(np.concatenate(
            [x_full[b, 512 * cid:512 * (cid + 1)] for b in range(4)], 0))

        im = {nm: inputs[nm] for nm in WNAMES}
        for s in "fr":
            im[f"in_proj_{s}"] = inputs[f"L0_attn_in_proj_{s}"]
            im[f"out_proj_{s}"] = inputs[f"L0_attn_out_proj_{s}"]
            im[f"x_proj_{s}"] = inputs[f"L0_attn_x_proj_{s}"]
            im[f"dt_w_{s}"] = inputs[f"L0_attn_dt_w_{s}"]
            im[f"smallsF_{s}"] = smallsF[s]
        im["x0sched"] = np.ascontiguousarray(x0sched)
        im["sel16"] = sel
        im["masks"] = masks
        im["x0s"] = x0s2
        im["identF"] = identF
        im["identB"] = identB
        im["identJ"] = np.ascontiguousarray(identJ)
        in_maps.append(im)

    res = run_bass_kernel_spmd(nc, in_maps, core_ids=list(range(8)))

    out = np.zeros((4, 4096, 1024), np.float32)
    for cid in range(8):
        o = np.asarray(res.results[cid]["out"]).reshape(4, 512, 1024)
        out[:, 512 * cid:512 * (cid + 1), :] = o
    return out


# version 09 (final)
# speedup vs baseline: 1.2207x; 1.0074x over previous
import numpy as np
import ml_dtypes
import concourse.bass as bass
import concourse.mybir as mybir
from concourse.bass_utils import run_bass_kernel_spmd
from contextlib import ExitStack

"""Builders for the mk_seq_postln_d2 megakernel (dev module; inlined into kernel.py later)."""

F32 = mybir.dt.float32
BF16 = mybir.dt.bfloat16
AF = mybir.ActivationFunctionType
ALU = mybir.AluOpType
AX = mybir.AxisListType
BF = ml_dtypes.bfloat16
ENGINES = ("tensor", "vector", "scalar", "gpsimd", "sync")


class Prog:

    def __init__(self, auto_drain=False, inc_all=False):
        self.auto_drain = auto_drain
        self.inc_all = inc_all
        self.items = []
        self.ecnt = dict.fromkeys(ENGINES, 0)
        self.dcnt = {}

    def op(self, eng, fn, inc=False, drain=False, waits=()):
        if self.inc_all:
            inc = True
        self.items.append(("op", eng, fn, inc, drain, tuple(waits)))
        if inc:
            self.ecnt[eng] += 1
        return ("e", eng, self.ecnt[eng])

    def dma(self, eng, name, fn, waits=()):
        self.dcnt[name] = self.dcnt.get(name, 0) + 16
        self.items.append(("dma", eng, fn, name, tuple(waits)))
        return ("d", name, self.dcnt[name])

    def arm_dma(self, eng, name, fn, ndma, waits=()):
        self.dcnt[name] = self.dcnt.get(name, 0) + 16 * ndma
        self.items.append(("arm", eng, fn, name, tuple(waits), ndma))
        return ("d", name, self.dcnt[name])

    def now(self, eng):
        return ("e", eng, self.ecnt[eng])

    def dnow(self, name):
        return ("d", name, self.dcnt.get(name, 0))

    def barrier(self, engs):
        snaps = [self.now(e) for e in engs]
        out = []
        for e in engs:
            ws = [s for s in snaps if s[1] != e]
            self.items.append(("waits", e, ws))
        return snaps

    def wait(self, eng, *tokens):
        self.items.append(("waits", eng, tuple(tokens)))

    def build(self, nc, stack, verbose=False):
        esem = {e: stack.enter_context(nc.semaphore(f"es_{e}")) for e in ENGINES}
        dsem = {n: stack.enter_context(nc.semaphore(f"ds_{n}")) for n in self.dcnt}
        block = stack.enter_context(nc.Block())
        per = {e: [] for e in ENGINES}
        for it in self.items:
            per[it[1]].append(it)

        def run_engine(e, eng_items):
            seen = {}

            def handle(engine):
                cnt = 0
                dc = {}
                for it in eng_items:
                    if it[0] == "waits":
                        for tk in it[2]:
                            _wait(engine, tk)
                        continue
                    if it[0] == "op":
                        _, _, fn, inc, drain, waits = it
                        for tk in waits:
                            _wait(engine, tk)
                        instr = fn(engine)
                        if drain or (self.auto_drain and e in ("vector", "scalar", "gpsimd")):
                            instr = engine.drain()
                        if inc:
                            cnt += 1
                            (instr if instr is not None else engine.nop()).then_inc(esem[e], 1)
                    elif it[0] == "arm":
                        _, _, fn, name, waits, ndma = it
                        for tk in waits:
                            _wait(engine, tk)
                        for arm in engine.switch_core_id(n=128):
                            cid = arm.logical % 8
                            fn(engine, cid, dsem[name])
                    else:
                        _, _, fn, name, waits = it
                        for tk in waits:
                            _wait(engine, tk)
                        d = fn(engine)
                        d.then_inc(dsem[name], 16)

            def _wait(engine, tk):
                kind, key, val = tk
                if val <= 0:
                    return
                if kind == "e" and key == e:
                    return
                prev = seen.get((kind, key), -1)
                if val <= prev:
                    return
                seen[(kind, key)] = val
                sem = esem[key] if kind == "e" else dsem[key]
                engine.wait_ge(sem, val)

            return handle

        for e in ENGINES:
            getattr(block, e if e != "sync" else "sync")(run_engine(e, per[e]))
        return esem, dsem


def ceil_div(a, b):
    return (a + b - 1) // b


def build_mamba_fused(nc, p, stack, tens, psum):
    L, DM, DI, NS, DTR, Tc = 4096, 1024, 2048, 16, 64, 512
    NPOS = 9
    DTt = DI // 128
    KT = DM // 128
    MT = Tc // 128
    NP = DTR + 2 * NS

    sb = lambda n, s, d=F32: stack.enter_context(nc.sbuf_tensor(n, s, d))
    dram = lambda n, s, d=BF16: nc.dram_tensor(n, s, d, kind="Internal").ap()
    bank = lambda b, par=128: psum[0:par, b * 512:(b + 1) * 512]

    xbf = dram("xbf", [NPOS, 4, Tc, DM])
    inp_bf = {s: dram(f"inp_bf_{s}", [2 * DI, DM]) for s in "fr"}
    outp_bf = {s: dram(f"outp_bf_{s}", [DM, DI]) for s in "fr"}
    xp_bf = {s: dram(f"xp_bf_{s}", [128, DI]) for s in "fr"}
    dtw_bf = {s: dram(f"dtw_bf_{s}", [DI, 128]) for s in "fr"}
    inpT = {s: dram(f"inpT_{s}", [128, KT, 2 * DI]) for s in "fr"}
    bfb = {s: dram(f"bfb_{s}", [128, 4096]) for s in "fr"}
    outT = {s: dram(f"outT_{s}", [DI, DM]) for s in "fr"}
    sched_x = dram("sched_x", [7, 128, KT, DI])
    sched_bfb = dram("sched_bfb", [7, 128, 4096])
    sched_smF = nc.dram_tensor("sched_smF", [7, 128, 368], F32, kind="Internal").ap()
    scr = dram("scr", [2, 4, Tc, DM])

    slab_q = sb("slab_q", [128, 2, KT, 512], BF16)
    smallsB = sb("smallsB", [128, 4096], BF16)
    smallsF_sb = sb("smallsF_sb", [128, 368])
    A_sb = sb("A_sb", [128, DTt, NS])
    sel_sb = sb("sel_sb", [NS, NS * 128], BF16)
    masks_sb = sb("masks_sb", [128, 35])
    x0cT = sb("x0cT", [128, KT, Tc], BF16)
    conv_in = sb("conv_in", [128, 4, Tc + 3], BF16)
    halo = sb("halo", [128, DTt, 3])
    hF = sb("hF", [128, DTt, 3])
    hR = sb("hR", [128, DTt, 3])
    x_silu = sb("x_silu", [128, DTt, Tc], BF16)
    delta = sb("delta", [128, DTt, Tc], BF16)
    u_bf = sb("u_bf", [128, DTt, Tc], BF16)
    y_acc = sb("y_acc", [128, DTt, Tc])
    carry = sb("carry", [128, DTt * NS])
    cF = sb("cF", [128, DTt * NS])
    cR = sb("cR", [128, DTt * NS])
    dtr_bf = sb("dtr_bf", [DTR, Tc], BF16)
    Btile = sb("Btile", [NS, Tc], BF16)
    Ctile = sb("Ctile", [NS, Tc], BF16)
    dA_b = sb("dA_b", [128, 4, Tc])
    dBu_b = sb("dBu_b", [128, 8, Tc], BF16)
    h_b = sb("h_b", [128, DTt, Tc], BF16)
    m_b = sb("m_b", [128, 4, Tc], BF16)
    Bb_sb = sb("Bb_sb", [128, 2, Tc], BF16)
    A_pos = sb("A_pos", [128, DTt, NS])
    biasb = sb("biasb", [128, DTt, NS])
    P3 = sb("P3", [128, DTt, NS])
    hloc = sb("hloc", [128, DTt, NS])
    ones512 = sb("ones512", [128, Tc], BF16)
    zc = sb("zc", [128, 1])
    dce_sb = sb("dce_sb", [128, DTt])
    Bb16 = h_b
    z_bf = m_b
    tmp_c = dBu_b
    yq = delta
    attn_sb = sb("attn_sb", [128, DM], BF16)
    sp_t = dA_b
    sp_r = sb("sp_r", [128, 4, Tc], BF16)
    outT_t = sp_r.rearrange("p (a b) c -> p a (b c)", a=2)
    stage_t = slab_q[:, 0].rearrange("p a b -> p (a b)")
    halo_fl = halo.rearrange("p a b -> p (a b)")
    hF_fl = hF.rearrange("p a b -> p (a b)")
    hR_fl = hR.rearrange("p a b -> p (a b)")
    x_projT = smallsB[:, 0:2048].rearrange("p (a b) -> p a b", b=128)
    dt_wT = smallsB[:, 2048:4096]
    dtb_sb = smallsF_sb[:, 256:272]
    convb_sb = smallsF_sb[:, 272:288]
    D_sb = smallsF_sb[:, 288:304]
    convw_sb = smallsF_sb[:, 304:368].rearrange("p (a b) -> p a b", b=4)
    Alog_v = smallsF_sb[:, 0:256].rearrange("p (a b) -> p a b", b=NS)

    T, V, S, G, Y = "tensor", "vector", "scalar", "gpsimd", "sync"

    p.dma(G, "cast", lambda e: e.dma_start(out=xbf, in_=tens["x0sched"]))
    for s in "fr":
        p.dma(G, "cast", lambda e, s=s: e.dma_start(
            out=inp_bf[s], in_=tens[f"in_proj_{s}"]))
        p.dma(G, "cast", lambda e, s=s: e.dma_start(
            out=outp_bf[s], in_=tens[f"out_proj_{s}"]))
        p.dma(G, "cast", lambda e, s=s: e.dma_start(
            out=xp_bf[s][0:NP, :], in_=tens[f"x_proj_{s}"]))
        p.dma(G, "cast", lambda e, s=s: e.dma_start(
            out=dtw_bf[s][:, 0:DTR], in_=tens[f"dt_w_{s}"]))
    cast_done = p.dnow("cast")
    p.dma(Y, "ld", lambda e: e.dma_start(out=sel_sb[:], in_=tens["sel16"]))
    p.dma(Y, "ld", lambda e: e.dma_start(out=masks_sb[:], in_=tens["masks"]))
    ld_done = p.dnow("ld")
    zmt = p.op(V, lambda e: e.memset(stage_t[:, 0:64], 0.0), inc=True, drain=True)
    for s in "fr":
        for k in range(DTt):
            p.dma(G, "stg", lambda e, s=s, k=k: e.dma_start(
                out=dtw_bf[s][k * 128:(k + 1) * 128, DTR:128],
                in_=stage_t[0:128, 0:64]), waits=(zmt, cast_done))
    for s in "fr":
        for k in range(KT):
            tpt = p.dma(Y, "tp", lambda e, s=s, k=k: e.dma_start(
                out=stage_t[:], in_=inp_bf[s][:, k * 128:(k + 1) * 128],
                transpose=True), waits=(cast_done, p.dnow("stg")))
            p.dma(G, "stg", lambda e, s=s, k=k: e.dma_start(
                out=inpT[s][:, k, :], in_=stage_t[:]), waits=(tpt,))
        for dt in range(DTt):
            tpt = p.dma(Y, "tp", lambda e, s=s, dt=dt: e.dma_start(
                out=stage_t[:, 0:NP], in_=xp_bf[s][0:NP, dt * 128:(dt + 1) * 128],
                transpose=True), waits=(cast_done, p.dnow("stg")))
            p.dma(G, "stg", lambda e, s=s, dt=dt: e.dma_start(
                out=bfb[s][:, dt * 128:(dt + 1) * 128], in_=stage_t[:, 0:128]),
                waits=(tpt,))
        tpt = p.dma(Y, "tp", lambda e, s=s: e.dma_start(
            out=stage_t[:, 0:2048], in_=dtw_bf[s], transpose=True),
            waits=(cast_done, p.dnow("stg")))
        p.dma(G, "stg", lambda e, s=s: e.dma_start(
            out=bfb[s][:, 2048:4096], in_=stage_t[:, 0:2048]), waits=(tpt,))
        for kt in range(DTt):
            tpt = p.dma(Y, "tp", lambda e, s=s, kt=kt: e.dma_start(
                out=stage_t[:, 0:DM], in_=outp_bf[s][:, kt * 128:(kt + 1) * 128],
                transpose=True), waits=(cast_done, p.dnow("stg")))
            p.dma(G, "stg", lambda e, s=s, kt=kt: e.dma_start(
                out=outT[s][kt * 128:(kt + 1) * 128, :], in_=stage_t[:, 0:DM]),
                waits=(tpt,))
    prep_done = (p.dnow("stg"), p.dnow("tp"), cast_done)
    for pos in range(7):
        def arm_fn(e, cid, sem, pos=pos):
            s = "f" if pos < cid else "r"
            e.dma_start(out=sched_x[pos], in_=inpT[s][:, :, 0:DI]).then_inc(sem, 16)
            e.dma_start(out=sched_bfb[pos], in_=bfb[s]).then_inc(sem, 16)
            return e.dma_start(out=sched_smF[pos],
                               in_=tens[f"smallsF_{s}"]).then_inc(sem, 16)
        p.arm_dma(Y, "sch", arm_fn, 3, waits=prep_done if pos == 0 else ())
    sch_done = p.dnow("sch")
    p.op(V, lambda e: e.memset(ones512[:], 1.0), inc=True)
    p.op(V, lambda e: e.memset(zc[:], 0.0), inc=True, drain=True)
    p.wait(T, ld_done, p.now(V), p.now(S))
    p.wait(V, ld_done)
    p.wait(S, ld_done)

    XB = (0, 1, 5, 6)
    for b in range(4):
      zs0 = p.op(V, lambda e: e.memset(carry[:], 0.0), inc=True)
      p.op(V, lambda e: e.memset(cF[:], 0.0), inc=True)
      p.op(V, lambda e: e.memset(cR[:], 0.0), inc=True)
      p.op(V, lambda e: e.memset(halo[:], 0.0), inc=True)
      p.op(V, lambda e: e.memset(hF[:], 0.0), inc=True)
      p.op(V, lambda e: e.memset(hR[:], 0.0), inc=True, drain=True)
      for pos in range(NPOS):
        own = pos >= 7
        dirx = 0 if pos == 7 else 1
        if pos < 7:
            smB_src, smF_src = sched_bfb[pos], sched_smF[pos]
        elif pos == 7:
            smB_src, smF_src = bfb["f"], tens["smallsF_f"]
        else:
            smB_src, smF_src = bfb["r"], tens["smallsF_r"]
        lw = (sch_done, p.now(T), p.now(V), p.now(S))
        p.dma(Y, "sml", lambda e, smB_src=smB_src: e.dma_start(
            out=smallsB[:], in_=smB_src), waits=lw)
        p.dma(Y, "sml", lambda e, smF_src=smF_src: e.dma_start(
            out=smallsF_sb[:], in_=smF_src))
        sml_ok = p.dnow("sml")
        e1 = p.op(S, lambda e: e.activation(A_pos[:], Alog_v, AF.Exp), inc=True,
                  waits=(sml_ok,))
        ae = p.op(V, lambda e: e.tensor_scalar(A_sb[:], A_pos[:], -1.0, None,
                                               ALU.mult), inc=True, waits=(e1,),
                  drain=True)
        if pos < 7:
            km = masks_sb[:, pos:pos + 1]
            p.op(V, lambda e, km=km: e.tensor_scalar(carry[:], carry[:], km, None,
                                                     ALU.mult), inc=True, drain=True)
            p.op(V, lambda e, km=km: e.tensor_scalar(halo_fl[:], halo_fl[:], km,
                                                     None, ALU.mult), inc=True,
                 drain=True)
        elif pos == 7:
            p.op(V, lambda e: e.tensor_copy(carry[:], cF[:]), inc=True, drain=True)
            p.op(V, lambda e: e.tensor_copy(halo[:], hF[:]), inc=True, drain=True)
        else:
            p.op(V, lambda e: e.tensor_copy(carry[:], cR[:]), inc=True, drain=True)
            p.op(V, lambda e: e.tensor_copy(halo[:], hR[:]), inc=True, drain=True)
        pro_v = p.now(V)

        xw = []
        for k in range(KT):
            xw.append(p.dma(Y, "x0c", lambda e, k=k, pos=pos, b=b: e.dma_start(
                out=x0cT[:, k, :], in_=xbf[pos, b, :, k * 128:(k + 1) * 128],
                transpose=True), waits=(cast_done,) if k == 0 else ()))
        x0c_ok = xw[-1]

        silu_snap = {}
        slab_war = {0: p.now(T), 1: p.now(T)}
        xslab_src = sched_x[pos] if pos < 7 else inpT["f" if pos == 7 else "r"]
        for g in range(0, DTt, 4):
            quad = (g, g + 1, g + 2, g + 3)
            buf = (g // 4) % 2
            sl = p.dma(Y, "slb", lambda e, buf=buf, g=g, xslab_src=xslab_src:
                       e.dma_start(out=slab_q[:, buf],
                                   in_=xslab_src[:, :, g * 128:g * 128 + 512]),
                       waits=(slab_war[buf], sch_done))
            mm_snap = {}
            for dt in quad:
                def mm_xz(e, dt=dt, buf=buf):
                    ins = None
                    for k in range(KT):
                        ins = e.matmul(bank(XB[dt % 4]),
                                       slab_q[:, buf, k, (dt % 4) * 128:(dt % 4 + 1) * 128],
                                       x0cT[:, k, :], start=(k == 0), stop=(k == KT - 1))
                    return ins
                mm_snap[dt] = p.op(T, mm_xz, inc=True,
                                   waits=(x0c_ok, sl, pro_v,
                                          silu_snap.get(g - 4, ("e", S, 0))))
            slab_war[buf] = mm_snap[quad[-1]]
            ev_snap = {}
            for dt in quad:
                ev_snap[dt] = p.op(S, lambda e, dt=dt: e.copy(
                    out=conv_in[:, dt % 4, 3:], in_=bank(XB[dt % 4])),
                    inc=True, waits=(mm_snap[dt],))
            for dt in quad:
                p.op(V, lambda e, dt=dt: e.tensor_copy(conv_in[:, dt % 4, 0:3], halo[:, dt, :]),
                     waits=(ev_snap[dt], silu_snap.get(g - 4, ("e", S, 0))))
            for j in range(4):
                for dt in quad:
                    if j == 0:
                        p.op(V, lambda e, dt=dt: e.tensor_scalar(
                            sp_t[:, dt % 4, :], conv_in[:, dt % 4, 0:Tc],
                            convw_sb[:, dt, 0:1], None, ALU.mult))
                    else:
                        p.op(V, lambda e, dt=dt, j=j: e.scalar_tensor_tensor(
                            sp_t[:, dt % 4, :], conv_in[:, dt % 4, j:Tc + j],
                            convw_sb[:, dt, j:j + 1], sp_t[:, dt % 4, :], ALU.mult, ALU.add))
            conv_v = {}
            for dt in quad:
                conv_v[dt] = p.op(V, lambda e, dt=dt: e.tensor_copy(
                    halo[:, dt, :], conv_in[:, dt % 4, Tc:Tc + 3]), inc=True)
            for dt in quad:
                silu_snap[g] = p.op(S, lambda e, dt=dt: e.activation(
                    x_silu[:, dt, :], sp_t[:, dt % 4, :], AF.Silu, bias=convb_sb[:, dt:dt + 1]),
                    inc=True, waits=(conv_v[dt],))
            for dt in quad:
                def mm_xdbl(e, dt=dt):
                    st = (dt == 0)
                    sp = (dt == DTt - 1)
                    e.matmul(bank(2, DTR), x_projT[:, dt, 0:DTR], x_silu[:, dt, :],
                             start=st, stop=sp)
                    e.matmul(bank(3, NS), x_projT[:, dt, DTR:DTR + NS], x_silu[:, dt, :],
                             start=st, stop=sp)
                    return e.matmul(bank(4, NS), x_projT[:, dt, DTR + NS:DTR + 2 * NS],
                                    x_silu[:, dt, :], start=st, stop=sp)
                p.op(T, mm_xdbl, inc=True, waits=(silu_snap[g],))
        xdbl_T = p.now(T)
        ev_dtr = p.op(S, lambda e: e.copy(out=dtr_bf[:], in_=bank(2, DTR)), inc=True,
                      waits=(xdbl_T,))
        ev_B = p.op(S, lambda e: e.copy(out=Btile[:], in_=bank(3, NS)), inc=True)
        ev_C = p.op(S, lambda e: e.copy(out=Ctile[:], in_=bank(4, NS)), inc=True)

        dmm = {}
        dv_hist = {}
        for g in range(0, DTt, 4):
            quad = (g, g + 1, g + 2, g + 3)
            for dt in quad:
                dmm[dt] = p.op(T, lambda e, dt=dt: e.matmul(
                    bank(XB[dt % 4]), dt_wT[0:DTR, dt * 128:(dt + 1) * 128], dtr_bf[:],
                    start=True, stop=True), inc=True,
                    waits=(ev_dtr, p.now(S)))
            sp_done = {}
            for dt in quad:
                p.op(S, lambda e, dt=dt: e.activation(
                    sp_t[:, dt % 4, 0:Tc], bank(XB[dt % 4]), AF.Abs, bias=dtb_sb[:, dt:dt + 1]),
                    waits=(dmm[dt], dv_hist.get(g - 4, ("e", V, 0))))
            for dt in quad:
                p.op(S, lambda e, dt=dt: e.activation(
                    sp_r[:, dt % 4, 0:Tc], bank(XB[dt % 4]),
                    AF.Relu, bias=dtb_sb[:, dt:dt + 1]))
            for dt in quad:
                p.op(S, lambda e, dt=dt: e.activation(
                    sp_t[:, dt % 4, 0:Tc], sp_t[:, dt % 4, 0:Tc], AF.Exp, scale=-1.0))
            for dt in quad:
                sp_done[dt] = p.op(S, lambda e, dt=dt: e.activation(
                    sp_t[:, dt % 4, 0:Tc], sp_t[:, dt % 4, 0:Tc], AF.Ln, bias=1.0), inc=True)
            dv = {}
            for dt in quad:
                dv[dt] = p.op(V, lambda e, dt=dt: e.tensor_tensor(
                    delta[:, dt, :], sp_r[:, dt % 4, 0:Tc],
                    sp_t[:, dt % 4, 0:Tc], ALU.add),
                    inc=True, waits=(sp_done[dt],))
            for dt in quad:
                p.op(V, lambda e, dt=dt: e.tensor_tensor(
                    u_bf[:, dt, :], delta[:, dt, :], x_silu[:, dt, :], ALU.mult), inc=True)
            dv_hist[g] = p.now(V)
        if own:
            yz = p.op(V, lambda e: e.memset(y_acc[:], 0.0), inc=True)
        p.barrier((T, V, S, G))

        carry3 = carry.rearrange("p (a b) -> p a b", b=NS)
        if own:
            scan_v_snap = {}
            dbu_tok = {}
            ebB_hist = {}
            n_last_v = {}
            pend = []

            def emit_carry(nq):
                cn, cg = nq
                p.op(V, lambda e, cn=cn, cg=cg: e.tensor_copy(
                    carry3[:, cg:cg + 4, cn:cn + 1],
                    h_b[:, cg:cg + 4, Tc - 1:Tc]), inc=True)

            for n in range(NS):
                bB = 2 if n % 2 == 0 else 3
                bC = 7 if n % 2 == 0 else 4
                war = n_last_v.get(n - 2, ("e", V, 0))
                war2 = ebB_hist.get(n - 2, ("e", S, 0))
                selL = sel_sb[:, n * 128:(n + 1) * 128]
                bc1 = p.op(T, lambda e, bB=bB, selL=selL: e.matmul(
                    bank(bB), selL, Btile[:], start=True, stop=True),
                    inc=True, waits=(ev_B, ev_C, war, war2))
                bc2 = p.op(T, lambda e, bC=bC, selL=selL: e.matmul(
                    bank(bC), selL, Ctile[:], start=True, stop=True), inc=True)
                dbu_war = dbu_tok.get((n - 2, DTt - 1), ("e", G, 0))
                ebB = p.op(S, lambda e, n=n, bB=bB: e.copy(
                    out=Bb_sb[:, n % 2, :], in_=bank(bB)), inc=True,
                    waits=(bc1, dbu_war))
                ebB_hist[n] = ebB
                for g in range(0, DTt, 4):
                    quad = (g, g + 1, g + 2, g + 3)
                    da = {}
                    for dt in quad:
                        prev = (scan_v_snap.get((n, dt - 4))
                                or scan_v_snap.get((n - 1, DTt - 4 + (dt % 4)))
                                or ("e", V, 0))
                        da[dt] = p.op(S, lambda e, dt=dt, n=n: e.activation(
                            dA_b[:, dt % 4, :], delta[:, dt, :], AF.Exp,
                            scale=A_sb[:, dt, n:n + 1]), inc=True, waits=(prev,))
                    for dt in quad:
                        wprev = (scan_v_snap.get((n, dt - 8))
                                 or scan_v_snap.get((n - 1, (dt + 8) % DTt))
                                 or ("e", V, 0))
                        dbu_tok[(n, dt)] = p.op(G, lambda e, dt=dt, n=n: e.tensor_tensor(
                            dBu_b[:, dt % 8, :], u_bf[:, dt, :], Bb_sb[:, n % 2, :],
                            ALU.mult), inc=True, waits=(ebB, wprev))
                    for dt in quad:
                        scan_v_snap[(n, dt)] = p.op(V, lambda e, dt=dt, n=n: e.tensor_tensor_scan(
                            h_b[:, dt, :], dA_b[:, dt % 4, :], dBu_b[:, dt % 8, :],
                            carry3[:, dt, n:n + 1], ALU.mult, ALU.add), inc=True,
                            waits=(da[dt], dbu_tok[(n, dt)]))
                    for dt in quad:
                        p.op(V, lambda e, dt=dt, bC=bC: e.tensor_tensor(
                            m_b[:, dt % 4, :], h_b[:, dt, :], bank(bC), ALU.mult),
                            waits=(bc2,) if dt == g else ())
                    if pend:
                        emit_carry(pend.pop(0))
                    pend.append((n, g))
                    for dt in quad:
                        p.op(V, lambda e, dt=dt: e.tensor_tensor(
                            y_acc[:, dt, :], y_acc[:, dt, :], m_b[:, dt % 4, :], ALU.add),
                            inc=(dt == DTt - 1))
                n_last_v[n] = p.now(V)
            for _ in range(2):
                p.op(V, lambda e: e.memset(m_b[0:1, 0, 0:1], 0.0), inc=True)
            emit_carry(pend.pop(0))
        else:
            ebB_tok = {}
            last_dbu = ("e", V, 0)
            for n in range(NS):
                bB = 2 if n % 2 == 0 else 3
                selL = sel_sb[:, n * 128:(n + 1) * 128]
                war2 = ebB_tok.get(n - 2, ("e", S, 0))
                bc1 = p.op(T, lambda e, bB=bB, selL=selL: e.matmul(
                    bank(bB), selL, Btile[:], start=True, stop=True),
                    inc=True, waits=(ev_B, ev_C, war2) if n < 2 else (war2,))
                ebB_tok[n] = p.op(S, lambda e, n=n, bB=bB: e.copy(
                    out=Bb16[:, n, :], in_=bank(bB)), inc=True, waits=(bc1,))
            ebB_all = p.now(S)
            for dt in range(DTt):
                p.op(V, lambda e, dt=dt: e.tensor_reduce(
                    dce_sb[:, dt:dt + 1], delta[:, dt, :], AX.X, ALU.add), inc=True)
            for dt in range(DTt):
                p.op(V, lambda e, dt=dt: e.tensor_scalar(
                    biasb[:, dt, :], A_sb[:, dt, :], dce_sb[:, dt:dt + 1], None,
                    ALU.mult), inc=True)
            bias_done = p.op(S, lambda e: e.activation(
                P3[:], biasb[:], AF.Exp), inc=True, waits=(p.now(V),))
            cum_tok = {}
            wS_last = {}
            wu_last = {}
            pend_red = []
            qidx = 0
            for dt in range(DTt):
                dcv = dA_b[:, dt % 2, :]
                cum_tok[dt] = p.op(V, lambda e, dt=dt, dcv=dcv: e.tensor_tensor_scan(
                    dcv, ones512[:], delta[:, dt, :], zc[:], ALU.mult, ALU.add),
                    inc=True, waits=(wS_last.get(dt - 2, ("e", S, 0)),))
                for n4 in range(0, NS, 4):
                    mbuf = m_b if qidx % 2 == 0 else dBu_b
                    qidx += 1
                    wq = {}
                    for n in range(n4, n4 + 4):
                        ws = ()
                        if n == n4:
                            ws = (cum_tok[dt], bias_done,
                                  wu_last.get((dt, n4 - 4))
                                  or wu_last.get((dt - 1, NS - 4))
                                  or ("e", V, 0))
                        wq[n] = p.op(S, lambda e, dt=dt, n=n, dcv=dcv: e.activation(
                            sp_r[:, n % 4, :], dcv, AF.Exp,
                            scale=A_pos[:, dt, n:n + 1],
                            bias=biasb[:, dt, n:n + 1]), inc=True, waits=ws)
                    for n in range(n4, n4 + 4):
                        tok = p.op(V, lambda e, dt=dt, n=n, mbuf=mbuf: e.tensor_tensor(
                            mbuf[:, n % 4, :], sp_r[:, n % 4, :], u_bf[:, dt, :],
                            ALU.mult), inc=True, waits=(wq[n], ebB_all))
                        if n == n4 + 3:
                            wu_last[(dt, n4)] = tok
                    if pend_red:
                        p.op(V, pend_red.pop(0), inc=True)
                    for n in range(n4, n4 + 4):
                        p.op(V, lambda e, dt=dt, n=n, mbuf=mbuf: e.tensor_tensor(
                            mbuf[:, n % 4, :], mbuf[:, n % 4, :], Bb16[:, n, :],
                            ALU.mult), inc=True)
                    pend_red.append(lambda e, dt=dt, n4=n4, mbuf=mbuf: e.tensor_reduce(
                        hloc[:, dt, n4:n4 + 4], mbuf[:, 0:4, :], AX.X, ALU.add))
                    wS_last[dt] = wq[n4 + 3]
            for _ in range(3):
                p.op(V, lambda e: e.memset(zc[:], 0.0), inc=True)
            p.op(V, pend_red.pop(0), inc=True)
            for dt in range(DTt):
                p.op(V, lambda e, dt=dt: e.tensor_tensor(
                    carry3[:, dt, :], carry3[:, dt, :], P3[:, dt, :], ALU.mult),
                    inc=True, waits=(bias_done,) if dt == 0 else ())
            for dt in range(DTt):
                p.op(V, lambda e, dt=dt: e.tensor_tensor(
                    carry3[:, dt, :], carry3[:, dt, :], hloc[:, dt, :], ALU.add),
                    inc=True)
        p.barrier((T, V, S, G))

        if own:
            zset = inpT["f" if pos == 7 else "r"]
            z_war = {0: p.now(T), 1: p.now(T)}
            for g in range(0, DTt, 4):
                quad = (g, g + 1, g + 2, g + 3)
                buf = (g // 4) % 2
                zl = p.dma(Y, "slb", lambda e, buf=buf, g=g, zset=zset: e.dma_start(
                    out=slab_q[:, buf],
                    in_=zset[:, :, DI + g * 128:DI + g * 128 + 512]),
                    waits=(z_war[buf],))
                zmm = {}
                for dt in quad:
                    def mm_z(e, dt=dt, buf=buf):
                        ins = None
                        for k in range(KT):
                            ins = e.matmul(bank(XB[dt % 4]),
                                           slab_q[:, buf, k, (dt % 4) * 128:(dt % 4 + 1) * 128],
                                           x0cT[:, k, :], start=(k == 0), stop=(k == KT - 1))
                        return ins
                    zmm[dt] = p.op(T, mm_z, inc=True, waits=(zl, p.now(V), p.now(S)))
                z_war[buf] = zmm[quad[-1]]
                zs = {}
                for dt in quad:
                    zs[dt] = p.op(S, lambda e, dt=dt: e.activation(
                        z_bf[:, dt % 4, :], bank(XB[dt % 4]), AF.Silu), inc=True, waits=(zmm[dt],))
                for dt in quad:
                    p.op(V, lambda e, dt=dt: e.tensor_scalar(
                        tmp_c[:, dt % 4, :], x_silu[:, dt, :], D_sb[:, dt:dt + 1], None, ALU.mult),
                        waits=(zs[dt],))
                for dt in quad:
                    p.op(V, lambda e, dt=dt: e.tensor_tensor(
                        tmp_c[:, dt % 4, :], y_acc[:, dt, :], tmp_c[:, dt % 4, :], ALU.add))
                for dt in quad:
                    p.op(V, lambda e, dt=dt: e.tensor_tensor(
                        yq[:, dt, :], tmp_c[:, dt % 4, :], z_bf[:, dt % 4, :], ALU.mult), inc=True)
            p.barrier((T, V, S))

            outT_src = outT["f" if pos == 7 else "r"]
            dmm2 = {}
            for dt in range(DTt):
                lw = (dmm2[dt - 2],) if dt - 2 in dmm2 else (p.now(V),)
                otw = p.dma(Y, f"outT{dt % 2}", lambda e, dt=dt, outT_src=outT_src: e.dma_start(
                    out=outT_t[:, dt % 2, :], in_=outT_src[dt * 128:(dt + 1) * 128, :]),
                    waits=lw)

                def mm_out(e, dt=dt):
                    st = (dt == 0)
                    sp = (dt == DTt - 1)
                    ins = None
                    for m in range(MT):
                        for hh in range(2):
                            ins = e.matmul(bank(m * 2 + hh), yq[:, dt, m * 128:(m + 1) * 128],
                                           outT_t[:, dt % 2, hh * 512:(hh + 1) * 512],
                                           start=st, stop=sp)
                    return ins
                dmm2[dt] = p.op(T, mm_out, inc=True, waits=(otw,))
            out_T = p.now(T)
            for m in range(MT):
                evm = p.op(S, lambda e, m=m: e.copy(out=attn_sb[:], in_=psum[:, m * 1024:(m + 1) * 1024]),
                           inc=True, waits=(out_T, p.dnow("st")))
                p.dma(G, "st", lambda e, m=m, dirx=dirx, b=b: e.dma_start(
                    out=scr[dirx, b, m * 128:(m + 1) * 128, :], in_=attn_sb[:]),
                    waits=(evm,))
        else:
            sfm = masks_sb[:, 7 + pos:8 + pos]
            nsf = masks_sb[:, 14 + pos:15 + pos]
            srm = masks_sb[:, 21 + pos:22 + pos]
            nsr = masks_sb[:, 28 + pos:29 + pos]
            p.op(V, lambda e: e.tensor_scalar(cF[:], cF[:], nsf, None, ALU.mult),
                 inc=True, drain=True)
            p.op(V, lambda e: e.scalar_tensor_tensor(cF[:], carry[:], sfm, cF[:],
                                                     ALU.mult, ALU.add), inc=True,
                 drain=True)
            p.op(V, lambda e: e.tensor_scalar(cR[:], cR[:], nsr, None, ALU.mult),
                 inc=True, drain=True)
            p.op(V, lambda e: e.scalar_tensor_tensor(cR[:], carry[:], srm, cR[:],
                                                     ALU.mult, ALU.add), inc=True,
                 drain=True)
            p.op(V, lambda e: e.tensor_scalar(hF_fl[:], hF_fl[:], nsf, None, ALU.mult),
                 inc=True, drain=True)
            p.op(V, lambda e: e.scalar_tensor_tensor(hF_fl[:], halo_fl[:], sfm, hF_fl[:],
                                                     ALU.mult, ALU.add), inc=True,
                 drain=True)
            p.op(V, lambda e: e.tensor_scalar(hR_fl[:], hR_fl[:], nsr, None, ALU.mult),
                 inc=True, drain=True)
            p.op(V, lambda e: e.scalar_tensor_tensor(hR_fl[:], halo_fl[:], srm, hR_fl[:],
                                                     ALU.mult, ALU.add), inc=True,
                 drain=True)
        p.barrier((T, V, S, G, Y))
    p.wait(G, p.dnow("st"))
    p.op(G, lambda e: e.nop())
    return scr



"""Phase-B builders: x1=LN(x0+pf+pr); L0_ffn; L1_attn (GLA block); L1_ffn; final LN.

Token layout per core: 2048 tokens = 4 batches x 512 seq positions, b-major
(row = b*512 + l_local). 16 token-tiles of 128. GLA "time" = batch axis (4
steps); sub s (s=0..3) covers tiles {s, 4+s, 8+s, 12+s} = same l-range, b=0..3.
Canonical activation layout: (token-part 128, feature-free) f32 residual.
Matmul inputs are transposed on demand via TensorE into xT (feature-part,
token-free) bf16.

PSUM rule: one accumulation group per bank (first_mm clears the whole bank).
Same-engine dependent V/S ops carry drain=True (v1; modest op counts here).
"""

F32 = mybir.dt.float32
BF16 = mybir.dt.bfloat16
AF = mybir.ActivationFunctionType
ALU = mybir.AluOpType
AX = mybir.AxisListType
BF = ml_dtypes.bfloat16

T, V, S, G, Y = "tensor", "vector", "scalar", "gpsimd", "sync"

D = 1024
TT = 16
KT = 8
N = 2048
NH = 16
DK = 64
DHID = 4096
E0, R0 = 6, 16
E1, F1, FS = 8, 512, 1024
GLA_TEMP = 16.0
DBG_SUB = 0


class Ctx:
    pass


def phase2_casts(nc, p, tens):
    dram = lambda n, s, d=BF16: nc.dram_tensor(n, s, d, kind="Internal").ap()
    w = {}

    def cast(nm, src, shape):
        dd = dram(f"cst_{nm}", shape)
        p.dma(G, "cast2", lambda e, dd=dd, src=src: e.dma_start(out=dd, in_=src))
        w[nm] = dd
        return dd

    tn = tens
    specs = [("baseW", "L0_ffn_base_W", D, D), ("wq", "L1_attn_wq", D, D),
             ("wk", "L1_attn_wk", D, D), ("wv", "L1_attn_wv", D, D),
             ("wog", "L1_attn_wog", D, D), ("wo", "L1_attn_wo", D, D),
             ("wgate", "L1_attn_w_gate", DHID, D),
             ("wup", "L1_attn_w_up", DHID, D),
             ("wdown", "L1_attn_w_down", D, DHID)]
    w["trans_specs"] = []
    for nm, key, rows, cols in specs:
        raw = cast(nm + "_raw", tn[key], [rows, cols])
        out = dram(nm + "T", [cols, rows])
        w["trans_specs"].append((nm, raw, out, rows, cols))
        w[nm + "T"] = out
    cast("f1gate", tn["L1_ffn_w_gate"], [E1, D, F1])
    cast("f1up", tn["L1_ffn_w_up"], [E1, D, F1])
    cast("f1down", tn["L1_ffn_w_down"], [E1, F1, D])
    cast("wsgate", tn["L1_ffn_ws_gate"], [D, FS])
    cast("wsup", tn["L1_ffn_ws_up"], [D, FS])
    cast("wsdown", tn["L1_ffn_ws_down"], [FS, D])
    cast("sA_raw", tn["L0_ffn_shared_A"], [R0, D])
    cast("wg1_raw", tn["L1_attn_wg1"], [16, D])
    w["rW_raw"] = dram("rW_raw", [16, D])
    p.dma(G, "cast2", lambda e: e.dma_start(out=w["rW_raw"][0:E0, :],
                                           in_=tn["L0_ffn_router_W"]))
    w["wg2_pad"] = dram("wg2_pad", [D, 128])
    p.dma(G, "cast2", lambda e: e.dma_start(out=w["wg2_pad"][:, 0:16],
                                           in_=tn["L1_attn_wg2"]))
    w["sB_pad"] = dram("sB_pad", [D, 128])
    p.dma(G, "cast2", lambda e: e.dma_start(out=w["sB_pad"][:, 0:R0],
                                           in_=tn["L0_ffn_shared_B"]))
    w["rA_raw"] = []
    w["rB_pad"] = []
    for ei in range(E0):
        ra = dram(f"rA_raw{ei}", [R0, D])
        p.dma(G, "cast2", lambda e, ra=ra, ei=ei: e.dma_start(
            out=ra, in_=tn["L0_ffn_routed_A"][ei]))
        w["rA_raw"].append(ra)
        rb = dram(f"rB_pad{ei}", [D, 128])
        p.dma(G, "cast2", lambda e, rb=rb, ei=ei: e.dma_start(
            out=rb[:, 0:R0], in_=tn["L0_ffn_routed_B"][ei]))
        w["rB_pad"].append(rb)
    cast("r1_raw", tn["L1_ffn_router_w"], [D, 8])
    return w


def alloc_common(nc, p, stack, tens, psum, w2):
    c = Ctx()
    c.nc, c.p, c.stack, c.tens = nc, p, stack, tens
    sb = lambda n, s, d=F32: stack.enter_context(nc.sbuf_tensor(n, s, d))
    dram = lambda n, s, d=BF16: nc.dram_tensor(n, s, d, kind="Internal").ap()
    c.dram = dram

    c.resid = sb("resid", [128, TT, D])
    c.dacc = sb("dacc", [128, TT, D], BF16)
    c.xT = sb("xT", [128, KT, N], BF16)
    c.dump = sb("dump", [128, 4, D])
    c.cols = sb("cols", [128, 64])
    c.arena = sb("arena", [128, 24, 512], BF16)
    c.qkv = c.arena[:, 0:12, :]
    c.qb = c.qkv[:, 0:4, :]
    c.kb = c.qkv[:, 4:8, :]
    c.vb = c.qkv[:, 8:12, :]
    c.ogT = c.arena[:, 0:8, :]
    c.og = c.arena[:, 12:20, :].rearrange("p (a b) c -> p a (b c)", b=2)
    c.slab_g = c.arena[:, 0:8, :]
    c.slab_u = c.arena[:, 8:16, :]
    c.slab_d = c.arena[:, 16:24, :].rearrange("p (a b) c -> p a (b c)", b=2)
    c.Pb = sb("Pb", [128, 4, 512], BF16)
    c.Pib = sb("Pib", [128, 4, 512], BF16)
    c.oac = None
    c.sc = sb("sc", [128, 12, 8])
    c.u1 = sb("u1", [128, 4, 512], BF16)
    c.gh_f = c.u1[:, 0:2, :]
    c.uh_f = c.u1[:, 2:4, :]
    c.ghT = c.u1[0:16].rearrange("p a b -> p (a b)")
    c.hT = sb("hT", [128, 512], BF16)
    c.hTs = sb("hTs", [16, 128], BF16)
    c.hTr = sb("hTr", [96, 128], BF16)
    c.w8 = sb("w8", [128, TT, 8])
    c.wbufA = sb("wbufA", [128, 2, 1024], BF16)
    c.wg1T = sb("wg1T", [128, KT, 16], BF16)
    c.wg2T = sb("wg2T", [128, D], BF16)
    c.sharedAT = sb("sharedAT", [128, KT, 16], BF16)
    c.sharedBT = sb("sharedBT", [128, D], BF16)
    c.routedAT = sb("routedAT", [128, KT, E0 * R0], BF16)
    c.routedBcat = sb("routedBcat", [96, D], BF16)
    c.routerWT = sb("routerWT", [128, KT, 16], BF16)
    c.router1 = sb("router1", [128, KT, 8], BF16)
    c.bg1c = sb("bg1c", [16, 1])
    c.rbc = sb("rbc", [128, 8])
    c.bcast = {}
    for nm in ("base_b", "bg2", "bog", "scale1", "scale2"):
        c.bcast[nm] = sb(f"bc_{nm}", [128, D], BF16)
    c.ones1 = sb("ones1", [1, 128])
    c.rowbuf = None
    c.identF = sb("identF_sb", [128, 128])
    c.epsc = sb("epsc", [128, 1])
    c.identB = sb("identB_sb", [128, 128], BF16)
    c.identJ = sb("identJ_sb", [128, 128], BF16)
    c.psum = psum
    c.bank = lambda b, par=128: c.psum[0:par, b * 512:(b + 1) * 512]
    _psB = c.psum.bitcast(BF16)
    c.bankB = lambda b, par=128: _psB[0:par, b * 1024:(b + 1) * 1024]
    c.nstat = 0
    c.rowbuf = c.dump[0:1, 0, :]
    c.oac = c.dump[:, 0:2, :].rearrange("p a (b f) -> p (a b) f", f=512)

    p.dma(Y, "wld", lambda e: e.dma_start(out=c.identF[:], in_=tens["identF"]))
    p.dma(Y, "wld", lambda e: e.dma_start(out=c.identB[:], in_=tens["identB"]))
    p.dma(Y, "wld", lambda e: e.dma_start(out=c.identJ[:], in_=tens["identJ"]))

    tn = tens
    c.trans_specs = w2["trans_specs"]
    c.baseWT = w2["baseWT"]
    c.wqT = w2["wqT"]
    c.wkT = w2["wkT"]
    c.wvT = w2["wvT"]
    c.wogT = w2["wogT"]
    c.woT = w2["woT"]
    c.wgateT = w2["wgateT"]
    c.wupT = w2["wupT"]
    c.wdownT = w2["wdownT"]
    c.f1gate = w2["f1gate"]
    c.f1up = w2["f1up"]
    c.f1down = w2["f1down"]
    c.wsgate = w2["wsgate"]
    c.wsup = w2["wsup"]
    c.wsdown = w2["wsdown"]
    c.sA_raw = w2["sA_raw"]
    c.wg1_raw = w2["wg1_raw"]
    c.rW_raw = w2["rW_raw"]
    c.wg2_pad = w2["wg2_pad"]
    c.sB_pad = w2["sB_pad"]
    c.rA_raw = w2["rA_raw"]
    c.rB_pad = w2["rB_pad"]
    c.r1_raw = w2["r1_raw"]
    cast_done = p.dnow("cast2")

    stg = c.wbufA[:, 0, :]
    zmt2 = p.op(V, lambda e: e.memset(stg[:, 0:112], 0.0), inc=True, drain=True)
    p.dma(G, "stg", lambda e: e.dma_start(
        out=c.rW_raw[E0:16, :], in_=stg[0:10, 0:1024]), waits=(zmt2, cast_done))
    for kk in range(KT):
        p.dma(G, "stg", lambda e, kk=kk: e.dma_start(
            out=c.wg2_pad[kk * 128:(kk + 1) * 128, 16:128],
            in_=stg[0:128, 0:112]), waits=(zmt2, cast_done))
        p.dma(G, "stg", lambda e, kk=kk: e.dma_start(
            out=c.sB_pad[kk * 128:(kk + 1) * 128, R0:128],
            in_=stg[0:128, 0:112]), waits=(zmt2, cast_done))
        for ei in range(E0):
            p.dma(G, "stg", lambda e, kk=kk, ei=ei: e.dma_start(
                out=c.rB_pad[ei][kk * 128:(kk + 1) * 128, R0:128],
                in_=stg[0:128, 0:112]), waits=(zmt2, cast_done))
    for nm, raw, out, rows, cols in c.trans_specs:
        for kk in range(cols // 128):
            for r0 in range(0, rows, 1024):
                rc = min(1024, rows - r0)
                tpt = p.dma(Y, "tps", lambda e, raw=raw, kk=kk, r0=r0, rc=rc: e.dma_start(
                    out=stg[:, 0:rc], in_=raw[r0:r0 + rc, kk * 128:(kk + 1) * 128],
                    transpose=True), waits=(cast_done, p.dnow("stg")))
                p.dma(G, "stg", lambda e, out=out, kk=kk, r0=r0, rc=rc: e.dma_start(
                    out=out[kk * 128:(kk + 1) * 128, r0:r0 + rc], in_=stg[:, 0:rc]),
                    waits=(tpt,))
    for kk in range(KT):
        p.dma(Y, "tp", lambda e, kk=kk: e.dma_start(
            out=c.sharedAT[:, kk, :], in_=c.sA_raw[:, kk * 128:(kk + 1) * 128],
            transpose=True), waits=(cast_done,))
        p.dma(Y, "tp", lambda e, kk=kk: e.dma_start(
            out=c.wg1T[:, kk, :], in_=c.wg1_raw[:, kk * 128:(kk + 1) * 128],
            transpose=True), waits=(cast_done,))
        p.dma(Y, "tp", lambda e, kk=kk: e.dma_start(
            out=c.routerWT[:, kk, :], in_=c.rW_raw[:, kk * 128:(kk + 1) * 128],
            transpose=True), waits=(cast_done,))
        for ei in range(E0):
            p.dma(Y, "tp", lambda e, kk=kk, ei=ei: e.dma_start(
                out=c.routedAT[:, kk, ei * R0:(ei + 1) * R0],
                in_=c.rA_raw[ei][:, kk * 128:(kk + 1) * 128], transpose=True),
                waits=(cast_done,))
        p.dma(Y, "tp", lambda e, kk=kk: e.dma_start(
            out=c.wg2T[:, kk * 128:(kk + 1) * 128],
            in_=c.wg2_pad[kk * 128:(kk + 1) * 128, :], transpose=True),
            waits=(cast_done,))
        p.dma(Y, "tp", lambda e, kk=kk: e.dma_start(
            out=c.sharedBT[:, kk * 128:(kk + 1) * 128],
            in_=c.sB_pad[kk * 128:(kk + 1) * 128, :], transpose=True),
            waits=(cast_done,))
        p.dma(Y, "wld", lambda e, kk=kk: e.dma_start(
            out=c.router1[:, kk, :], in_=c.r1_raw[kk * 128:(kk + 1) * 128, :]),
            waits=(cast_done,))
    rb_stage = c.dacc[:, 0, :]
    for ei in range(E0):
        for kk in range(KT):
            tpt = p.dma(Y, "tpr", lambda e, ei=ei, kk=kk: e.dma_start(
                out=rb_stage[:, 0:128], in_=c.rB_pad[ei][kk * 128:(kk + 1) * 128, :],
                transpose=True), waits=(cast_done, p.dnow("rbmv")))
            p.dma(G, "rbmv", lambda e, ei=ei, kk=kk: e.dma_start(
                out=c.routedBcat[ei * R0:(ei + 1) * R0, kk * 128:(kk + 1) * 128],
                in_=rb_stage[0:R0, 0:128]), waits=(tpt,))

    mset = p.op(V, lambda e: e.memset(c.ones1[:], 1.0), inc=True)
    p.op(V, lambda e: e.memset(c.epsc[:], 1e-5), inc=True, drain=True)
    for nm, src in (("base_b", tn["L0_ffn_base_b"]), ("bg2", tn["L1_attn_bg2"]),
                    ("bog", tn["L1_attn_bog"]), ("scale1", tn["L1_attn_scale1"]),
                    ("scale2", tn["L1_attn_scale2"])):
        bl = p.dma(Y, "rbl", lambda e, src=src: e.dma_start(
            out=c.rowbuf[:], in_=src.unsqueeze(0)), waits=(p.now(T),))

        def bc_mm(e):
            e.matmul(c.bank(0), c.ones1[:], c.rowbuf[:, 0:512], start=True, stop=True)
            return e.matmul(c.bank(1), c.ones1[:], c.rowbuf[:, 512:1024], start=True,
                            stop=True)
        mm = p.op(T, bc_mm, inc=True, waits=(bl, mset, p.now(S)))
        p.op(S, lambda e, nm=nm: e.copy(out=c.bcast[nm][:, 0:512], in_=c.bank(0)),
             waits=(mm,))
        p.op(S, lambda e, nm=nm: e.copy(out=c.bcast[nm][:, 512:1024], in_=c.bank(1)),
             inc=True)
    p.dma(Y, "wld", lambda e: e.dma_start(out=c.bg1c[:],
                                          in_=tn["L1_attn_bg1"].unsqueeze(1)))
    rb_ld = p.dma(Y, "rbl", lambda e: e.dma_start(
        out=c.rowbuf[:, 0:E0], in_=tn["L0_ffn_router_b"].unsqueeze(0)),
        waits=(p.now(T),))
    mm = p.op(T, lambda e: e.matmul(c.psum[0:128, 0:8], c.ones1[:], c.rowbuf[:, 0:8],
                                    start=True, stop=True), inc=True,
              waits=(rb_ld, p.now(S)))
    p.op(S, lambda e: e.copy(out=c.rbc[:], in_=c.psum[0:128, 0:8]), inc=True,
         waits=(mm,))
    p.barrier((T, V, S))
    p.wait(T, p.dnow("tp"), p.dnow("wld"), p.dnow("stg"), p.dnow("rbmv"), p.dnow("tps"), p.dnow("tpr"))
    p.wait(V, p.dnow("tp"), p.dnow("wld"))
    p.wait(S, p.dnow("tp"), p.dnow("wld"), p.dnow("rbmv"))
    p.wait(G, p.dnow("tp"))
    return c



def ln_apply(c, v_ap, out_ap, kind="ln", scale_bc=None, eps=1e-5, rms_eps=1e-6, vwait=()):
    p = c.p
    i = c.nstat % 4
    c.nstat += 1
    sq = c.cols[:, i * 8:i * 8 + 1]
    sm = c.cols[:, i * 8 + 1:i * 8 + 2]
    v1 = c.cols[:, i * 8 + 2:i * 8 + 3]
    v2 = c.cols[:, i * 8 + 3:i * 8 + 4]
    v3 = c.cols[:, i * 8 + 4:i * 8 + 5]
    d0 = c.dump[:, 2, :]
    d1 = c.dump[:, 3, :]
    s1 = p.op(S, lambda e: e.activation(d0, v_ap, AF.Square, accum_out=sq), inc=True,
               waits=tuple(vwait) + (p.now(V),))
    if kind == "ln":
        s2 = p.op(S, lambda e: e.activation(d1, v_ap, AF.Copy, accum_out=sm), inc=True)
        p.op(V, lambda e: e.tensor_scalar(sm, sm, 1.0 / D, None, ALU.mult), drain=True,
             waits=(s2,))
        p.op(V, lambda e: e.tensor_tensor(v1, sm, sm, ALU.mult), drain=True)
        vv2 = p.op(V, lambda e: e.scalar_tensor_tensor(v2, sq, 1.0 / D, v1, ALU.mult,
                                                       ALU.subtract), drain=True,
                   inc=True)
        s3 = p.op(S, lambda e: e.activation(v2, v2, AF.Sqrt, bias=c.epsc[:]), inc=True,
                  waits=(vv2,))
        p.op(V, lambda e: e.reciprocal(v3, v2), drain=True, waits=(s3,))
        r2 = p.op(V, lambda e: e.scalar_tensor_tensor(v1, sm, -1.0, v3, ALU.mult,
                                                      ALU.mult), drain=True, inc=True)
        return p.op(V, lambda e: e.tensor_scalar(out_ap, v_ap, v3, v1, ALU.mult,
                                                 ALU.add), inc=True, drain=True,
                    waits=(r2,))
    else:
        s3 = p.op(S, lambda e: e.activation(v2, sq, AF.Sqrt), inc=True,
                  waits=(s1, p.now(V)))
        p.op(V, lambda e: e.tensor_scalar(v2, v2, 1.0 / np.sqrt(D), rms_eps, ALU.mult,
                                          ALU.add), drain=True, waits=(s3,))
        r1 = p.op(V, lambda e: e.reciprocal(v3, v2), drain=True, inc=True)
        sa = p.op(V, lambda e: e.tensor_scalar(d1, v_ap, v3, None, ALU.mult), inc=True,
                  drain=True, waits=(r1,))
        return p.op(V, lambda e: e.tensor_tensor(out_ap, d1, scale_bc, ALU.mult),
                    inc=True, waits=(sa,), drain=True)


def transpose_act(c, src_fn, dst):
    p = c.p
    prev = {}
    for tt in range(TT):
        for k in range(KT):
            bnk = (tt * KT + k) % 4
            w = (prev[bnk],) if bnk in prev else ()
            mm = p.op(T, lambda e, tt=tt, k=k, bnk=bnk: e.transpose(
                c.bank(bnk)[:, 0:128], src_fn(tt)[:, k * 128:(k + 1) * 128],
                c.identF[:]), inc=True, waits=w)
            prev[bnk] = p.op(S, lambda e, tt=tt, k=k, bnk=bnk: e.copy(
                out=dst[:, k, tt * 128:(tt + 1) * 128], in_=c.bank(bnk)[:, 0:128]),
                inc=True, waits=(mm,))
    p.barrier((T, S, V))


def softmax_topk(c, psum_ap, ncand, keepk, wout_ap, renorm, bias_bc=None, waits=()):
    p = c.p
    i = c.nstat % 4
    c.nstat += 1
    mx = c.cols[:, i * 8 + 5:i * 8 + 6]
    sm = c.cols[:, i * 8 + 6:i * 8 + 7]
    s2 = c.cols[:, i * 8 + 7:i * 8 + 8]
    lg = c.dump[:, 2, 0:8]
    cnt = c.dump[:, 3, 0:8]
    if bias_bc is not None:
        p.op(V, lambda e: e.tensor_tensor(lg[:, 0:ncand], psum_ap,
                                          bias_bc[:, 0:ncand], ALU.add), drain=True,
             waits=tuple(waits))
    else:
        p.op(V, lambda e: e.tensor_copy(lg[:, 0:ncand], psum_ap), drain=True,
             waits=tuple(waits))
    p.op(V, lambda e: e.tensor_reduce(mx, lg[:, 0:ncand], AX.X, ALU.max, negate=True),
         drain=True, inc=True)
    sx = p.op(S, lambda e: e.activation(lg[:, 0:ncand], lg[:, 0:ncand], AF.Exp,
                                        bias=mx, accum_out=sm), inc=True,
              waits=(p.now(V),))
    p.op(V, lambda e: e.reciprocal(sm, sm), drain=True, waits=(sx,))
    p.op(V, lambda e: e.tensor_scalar(lg[:, 0:ncand], lg[:, 0:ncand], sm, None,
                                      ALU.mult), drain=True)
    p.op(V, lambda e: e.memset(cnt[:, 0:ncand], 0.0), drain=True)
    for j in range(ncand):
        p.op(V, lambda e, j=j: e.scalar_tensor_tensor(
            cnt[:, 0:ncand], lg[:, 0:ncand], lg[:, j:j + 1], cnt[:, 0:ncand],
            ALU.is_lt, ALU.add), drain=True)
    p.op(V, lambda e: e.tensor_scalar(cnt[:, 0:ncand], cnt[:, 0:ncand],
                                      float(keepk) - 0.5, None, ALU.is_lt), drain=True)
    if ncand < 8:
        p.op(V, lambda e: e.memset(wout_ap[:, ncand:8], 0.0), drain=True)
    last = p.op(V, lambda e: e.tensor_tensor(wout_ap[:, 0:ncand], lg[:, 0:ncand],
                                             cnt[:, 0:ncand], ALU.mult), inc=True,
                drain=True)
    if renorm:
        p.op(V, lambda e: e.tensor_reduce(s2, wout_ap[:, 0:ncand], AX.X, ALU.add),
             drain=True)
        p.op(V, lambda e: e.tensor_scalar(s2, s2, 1e-6, None, ALU.add), drain=True)
        p.op(V, lambda e: e.reciprocal(s2, s2), drain=True)
        last = p.op(V, lambda e: e.tensor_scalar(wout_ap[:, 0:ncand],
                                                 wout_ap[:, 0:ncand], s2, None,
                                                 ALU.mult), inc=True, drain=True)
    return last



def step1_fused(c, x0_ap, scr):
    p = c.p
    for tt in range(TT):
        b, q = tt // 4, tt % 4
        lw = (p.now(V), p.now(S), p.now(T))
        if tt == 0:
            lw = lw + (p.dnow("rbmv"), p.dnow("tpr"))
        l3 = p.dma(Y, "ld1", lambda e, tt=tt: e.dma_start(
            out=c.dump[:, 2, :], in_=x0_ap[tt * 128:(tt + 1) * 128, :]), waits=lw)
        pf_sb = c.dacc[:, tt, :]
        l1 = p.dma(Y, "ld1", lambda e, pf_sb=pf_sb, b=b, q=q: e.dma_start(
            out=pf_sb, in_=scr[0, b, q * 128:(q + 1) * 128, :]))
        prb = c.Pb[:, 0:2, :].rearrange("p a b -> p (a b)")
        l2 = p.dma(Y, "ld1", lambda e, prb=prb, b=b, q=q: e.dma_start(
            out=prb, in_=scr[1, b, (3 - q) * 128:(4 - q) * 128, :]))

        def mm_j(e, prb=prb):
            e.matmul(c.bank(0), c.identJ[:], prb[:, 0:512], start=True, stop=True)
            return e.matmul(c.bank(1), c.identJ[:], prb[:, 512:1024], start=True,
                            stop=True)
        mmj = p.op(T, mm_j, inc=True, waits=(l2, p.now(V), p.now(S)))
        v = c.dump[:, 0, :]
        p.op(V, lambda e, pf_sb=pf_sb: e.tensor_tensor(
            v, c.dump[:, 2, :], pf_sb, ALU.add), waits=(l1, l3), drain=True)
        p.op(V, lambda e: e.tensor_tensor(v[:, 0:512], v[:, 0:512], c.bank(0),
                                          ALU.add), waits=(mmj,), drain=True)
        vv = p.op(V, lambda e: e.tensor_tensor(v[:, 512:1024], v[:, 512:1024],
                                               c.bank(1), ALU.add),
                  inc=True, drain=True)
        ln_apply(c, v, c.resid[:, tt, :], kind="ln", vwait=(vv,))
    p.barrier((T, V, S))



def step2(c):
    p = c.p
    transpose_act(c, lambda tt: c.resid[:, tt, :], c.xT)
    for tt in range(TT):
        ts = tt * 128

        def mm_small(e, ts=ts):
            ins = None
            for k in range(KT):
                st, sp = (k == 0), (k == KT - 1)
                e.matmul(c.bank(4)[:, 0:16], c.xT[:, k, ts:ts + 128],
                         c.sharedAT[:, k, :], start=st, stop=sp)
                e.matmul(c.bank(6)[:, 0:96], c.xT[:, k, ts:ts + 128],
                         c.routedAT[:, k, :], start=st, stop=sp)
                ins = e.matmul(c.bank(5)[:, 0:E0], c.xT[:, k, ts:ts + 128],
                               c.routerWT[:, k, 0:E0], start=st, stop=sp)
            return ins
        mm = p.op(T, mm_small, inc=True, waits=(p.now(S), p.now(V)))
        softmax_topk(c, c.bank(5)[:, 0:E0], E0, 2, c.w8[:, tt, :], True, bias_bc=c.rbc,
                     waits=(mm,))
        hsb = c.gh_f.rearrange("p a b -> p (a b)")[:, 0:112]
        ev1 = p.op(S, lambda e, hsb=hsb: e.copy(out=hsb[:, 0:16], in_=c.bank(4)[:, 0:16]),
                   waits=(mm, p.now(V)))
        ev2 = p.op(S, lambda e, hsb=hsb: e.copy(out=hsb[:, 16:112],
                                                in_=c.bank(6)[:, 0:96]), inc=True)
        for ei in range(E0):
            hv = p.op(V, lambda e, ei=ei, hsb=hsb, tt=tt: e.tensor_scalar(
                hsb[:, 16 + ei * R0:16 + (ei + 1) * R0],
                hsb[:, 16 + ei * R0:16 + (ei + 1) * R0],
                c.w8[:, tt, ei:ei + 1], None, ALU.mult), drain=True, waits=(ev2,),
                inc=(ei == E0 - 1))
        mm2 = p.op(T, lambda e, hsb=hsb: e.transpose(c.bankB(7)[0:16, 0:128],
                                                     hsb[:, 0:16], c.identB[:]),
                   waits=(hv, p.now(S)))
        mm2b = p.op(T, lambda e, hsb=hsb: e.transpose(c.bankB(6)[0:96, 0:128],
                                                      hsb[:, 16:112], c.identB[:]),
                    inc=True)
        p.op(S, lambda e: e.copy(out=c.hTs[:], in_=c.bankB(7)[0:16, 0:128]),
             waits=(mm2b,))
        evT = p.op(S, lambda e: e.copy(out=c.hTr[:], in_=c.bankB(6)[0:96, 0:128]),
                   inc=True)

        def mm_main(e, ts=ts):
            ins = None
            for half in range(2):
                f0 = half * 512
                for k in range(KT):
                    e.matmul(c.bank(half), c.xT[:, k, ts:ts + 128],
                             c.wbufA[:, (half * KT + k) % 2, 0:512],
                             start=(k == 0), stop=False)
                e.matmul(c.bank(half), c.hT[0:16, 0:128], c.sharedBT[0:16, f0:f0 + 512],
                         start=False, stop=False)
                ins = e.matmul(c.bank(half), c.hT[16:112, 0:128],
                               c.routedBcat[:, f0:f0 + 512], start=False, stop=True)
            return ins
        slot_consumer = {0: p.now(T), 1: p.now(T)}
        for half in range(2):
            f0 = half * 512
            for k in range(KT):
                slot = (half * KT + k) % 2
                wld = p.dma(Y, f"ws{slot}", lambda e, k=k, f0=f0, slot=slot: e.dma_start(
                    out=c.wbufA[:, slot, 0:512],
                    in_=c.baseWT[k * 128:(k + 1) * 128, f0:f0 + 512]),
                    waits=(slot_consumer[slot],))
                slot_consumer[slot] = p.op(T, lambda e, k=k, half=half, ts=ts, slot=slot: e.matmul(
                    c.bank(half), c.xT[:, k, ts:ts + 128], c.wbufA[:, slot, 0:512],
                    start=(k == 0), stop=False), inc=True,
                    waits=(wld,) + ((evT,) if (half == 0 and k == 0) else ()))
            p.op(T, lambda e, half=half, f0=f0: e.matmul(
                c.bank(half), c.hTs[:], c.sharedBT[0:16, f0:f0 + 512],
                start=False, stop=False))
            p.op(T, lambda e, half=half, f0=f0: e.matmul(
                c.bank(half), c.hTr[:], c.routedBcat[:, f0:f0 + 512],
                start=False, stop=True), inc=True)
        mm3 = p.now(T)
        v = c.dump[:, 0, :]
        p.op(V, lambda e: e.tensor_tensor(v[:, 0:512], c.bank(0),
                                          c.bcast["base_b"][:, 0:512], ALU.add),
             waits=(mm3,))
        p.op(V, lambda e: e.tensor_tensor(v[:, 512:1024], c.bank(1),
                                          c.bcast["base_b"][:, 512:1024], ALU.add),
             drain=True)
        vv = p.op(V, lambda e, tt=tt: e.tensor_tensor(v, v, c.resid[:, tt, :], ALU.add),
                  inc=True, drain=True)
        ln_apply(c, v, c.resid[:, tt, :], kind="ln", vwait=(vv,))
    p.barrier((T, V, S))



def step3(c, upto3='all'):
    p = c.p
    if upto3 == 'rms1':
        for tt in range(TT):
            ln_apply(c, c.resid[:, tt, :], c.dacc[:, tt, :], kind="rms",
                     scale_bc=c.bcast["scale1"][:])
        p.barrier((T, V, S))
        return
    for tt in range(TT):
        ln_apply(c, c.resid[:, tt, :], c.dump[:, 1, :], kind="rms",
                 scale_bc=c.bcast["scale1"][:])
        for k in range(KT):
            bnk = k % 4
            mm = p.op(T, lambda e, k=k, bnk=bnk: e.transpose(
                c.bank(bnk)[:, 0:128], c.dump[:, 1, k * 128:(k + 1) * 128],
                c.identF[:]), inc=True, waits=(p.now(V), p.now(S)))
            p.op(S, lambda e, k=k, bnk=bnk, tt=tt: e.copy(
                out=c.xT[:, k, tt * 128:(tt + 1) * 128], in_=c.bank(bnk)[:, 0:128]),
                inc=True, waits=(mm,))
        p.barrier((T, V, S))
    for fc in range(4):
        f0 = fc * 512

        def mm_gh(e, f0=f0):
            ins = None
            for k in range(KT):
                ins = e.matmul(c.bank(4, 16), c.wg1T[:, k, :], c.xT[:, k, f0:f0 + 512],
                               start=(k == 0), stop=(k == KT - 1))
            return ins
        mm = p.op(T, mm_gh, inc=True, waits=(p.now(S),))
        p.op(S, lambda e, f0=f0: e.activation(c.ghT[:, f0:f0 + 512], c.bank(4, 16),
                                              AF.Silu, bias=c.bg1c[:]), inc=True,
             waits=(mm,))
    p.barrier((T, V, S))
    if upto3 == 'xt':
        p.op(V, lambda e: e.memset(c.dacc[:], 0.0), inc=True, drain=True)
        for k in range(KT):
            p.op(V, lambda e, k=k: e.tensor_copy(c.dacc[:, k, :],
                                                 c.xT[:, k, 0:1024]), inc=True,
                 drain=True)
        p.barrier((T, V, S))
        return
    p.op(V, lambda e: e.memset(c.dacc[:], 0.0), inc=True, drain=True)
    pairs = [(t, s) for t in range(4) for s in range(t + 1)]
    for sub in range(4):
        tiles = [sub + 4 * b for b in range(4)]
        for hh in range(2):
            h0 = hh * 512
            for bi in range(4):
                ts = tiles[bi] * 128
                mm = p.op(T, lambda e, bi=bi, ts=ts, h0=h0: e.matmul(
                    c.bank(bi), c.ghT[:, ts:ts + 128], c.wg2T[0:16, h0:h0 + 512],
                    start=True, stop=True), inc=True, waits=(p.now(S), p.now(V)))
                zf = c.dump[:, bi % 2, 0:512]
                zv = p.op(V, lambda e, bi=bi, zf=zf, h0=h0: e.tensor_tensor(
                    zf, c.bank(bi), c.bcast["bg2"][:, h0:h0 + 512], ALU.add),
                    inc=True, drain=True, waits=(mm,))
                p.op(S, lambda e, bi=bi, zf=zf: e.activation(
                    c.Pb[:, bi, :], zf, AF.Sigmoid, scale=1.0 / GLA_TEMP), inc=True,
                    waits=(zv,))
                se = p.op(S, lambda e, bi=bi, zf=zf: e.activation(
                    c.dump[:, 2 + bi % 2, 0:512], zf, AF.Exp, scale=-1.0 / GLA_TEMP),
                    inc=True)
                p.op(V, lambda e, bi=bi: e.tensor_scalar(
                    c.Pib[:, bi, :], c.dump[:, 2 + bi % 2, 0:512], 1.0, None, ALU.add),
                    inc=True, drain=True, waits=(se,))
            for bi in range(1, 4):
                p.op(V, lambda e, bi=bi: e.tensor_tensor(
                    c.Pb[:, bi, :], c.Pb[:, bi, :], c.Pb[:, bi - 1, :], ALU.mult),
                    drain=True)
            for bi in range(1, 4):
                p.op(V, lambda e, bi=bi: e.tensor_tensor(
                    c.Pib[:, bi, :], c.Pib[:, bi, :], c.Pib[:, bi - 1, :], ALU.mult),
                    drain=True, inc=(bi == 3))
            if upto3 == 'al1' and sub == DBG_SUB and hh == 0:
                p.barrier((T, V, S))
                return
            slot_consumer = {0: p.now(T), 1: p.now(T)}
            for wd, kind in ((c.wqT, "q"), (c.wkT, "k"), (c.wvT, "v"), (c.wogT, "og")):
                for k in range(KT):
                    slot = k % 2
                    wld = p.dma(Y, f"ws{slot}", lambda e, wd=wd, k=k, h0=h0,
                                slot=slot: e.dma_start(
                        out=c.wbufA[:, slot, 0:512],
                        in_=wd[k * 128:(k + 1) * 128, h0:h0 + 512]),
                        waits=(slot_consumer[slot],))

                    def mm_w(e, k=k, tiles=tiles):
                        ins = None
                        for bi in range(4):
                            ts = tiles[bi] * 128
                            ins = e.matmul(c.bank(4 + bi), c.xT[:, k, ts:ts + 128],
                                           c.wbufA[:, k % 2, 0:512], start=(k == 0),
                                           stop=(k == KT - 1))
                        return ins
                    last_mm = p.op(T, mm_w, inc=True,
                                   waits=(wld,) + ((p.now(S), p.now(V)) if k == 0 else ()))
                    slot_consumer[slot] = last_mm
                sil = {}
                for bi in range(4):
                    if kind == "q":
                        p.op(S, lambda e, bi=bi: e.copy(out=c.qb[:, bi, :],
                                                        in_=c.bank(4 + bi)), inc=True,
                             waits=(last_mm,))
                    elif kind == "k":
                        p.op(S, lambda e, bi=bi: e.copy(out=c.kb[:, bi, :],
                                                        in_=c.bank(4 + bi)), inc=True,
                             waits=(last_mm,))
                    elif kind == "v":
                        p.op(S, lambda e, bi=bi: e.copy(out=c.vb[:, bi, :],
                                                        in_=c.bank(4 + bi)), inc=True,
                             waits=(last_mm,))
                    else:
                        ov = p.op(V, lambda e, bi=bi, h0=h0: e.tensor_tensor(
                            c.dump[:, bi % 2, 0:512], c.bank(4 + bi),
                            c.bcast["bog"][:, h0:h0 + 512], ALU.add), inc=True,
                            drain=True,
                            waits=(last_mm,) + ((sil[bi - 2],) if bi - 2 in sil else ()))
                        sil[bi] = p.op(S, lambda e, bi=bi, h0=h0: e.activation(
                            c.og[:, bi, h0:h0 + 512], c.dump[:, bi % 2, 0:512],
                            AF.Silu), inc=True, waits=(ov,))
            p.barrier((T, V, S))
            if upto3 in ('q', 'k', 'v', 'og', 'alpha') and sub == DBG_SUB and hh == 0:
                src = {'q': c.qb, 'k': c.kb, 'v': c.vb, 'og': c.og, 'alpha': c.Pb}[upto3]
                wid = 1024 if upto3 == 'og' else 512
                for bi in range(4):
                    p.op(V, lambda e, bi=bi, src=src, wid=wid: e.tensor_copy(
                        c.dacc[:, bi, 0:wid], src[:, bi, 0:wid]), inc=True, drain=True)
                p.barrier((T, V, S))
                return
            for bi in range(4):
                pi = pairs.index((bi, bi))
                tm = c.dump[:, 2 + bi % 2, 0:512] if bi < 2 else c.dump[:, 2 + bi % 2, 512:1024]
                p.op(V, lambda e, bi=bi, tm=tm: e.tensor_tensor(
                    tm, c.qb[:, bi, :], c.kb[:, bi, :], ALU.mult), drain=True)
                p.op(V, lambda e, pi=pi, tm=tm: e.tensor_reduce(
                    c.sc[:, pi, :], tm.rearrange("p (h k) -> p h k", k=DK), AX.X,
                    ALU.add), drain=True, inc=True)
            for bi in range(4):
                p.op(V, lambda e, bi=bi: e.tensor_tensor(
                    c.qb[:, bi, :], c.qb[:, bi, :], c.Pb[:, bi, :], ALU.mult))
            for bi in range(4):
                p.op(V, lambda e, bi=bi: e.tensor_tensor(
                    c.kb[:, bi, :], c.kb[:, bi, :], c.Pib[:, bi, :], ALU.mult),
                    inc=(bi == 3))
            for pi, (t_, s_) in enumerate(pairs):
                if t_ == s_:
                    continue
                tm = c.dump[:, 2 + pi % 2, 0:512] if pi % 4 < 2 else c.dump[:, 2 + pi % 2, 512:1024]
                p.op(V, lambda e, t_=t_, s_=s_, tm=tm: e.tensor_tensor(
                    tm, c.qb[:, t_, :], c.kb[:, s_, :], ALU.mult), drain=True)
                p.op(V, lambda e, pi=pi, tm=tm: e.tensor_reduce(
                    c.sc[:, pi, :], tm.rearrange("p (h k) -> p h k", k=DK), AX.X,
                    ALU.add), drain=True, inc=True)
            p.op(V, lambda e: e.memset(c.oac[:], 0.0), drain=True)
            for pi, (t_, s_) in enumerate(pairs):
                tm = c.dump[:, 2 + pi % 2, 0:512] if pi % 4 < 2 else c.dump[:, 2 + pi % 2, 512:1024]
                scbc = c.sc[:, pi, :].unsqueeze(2).to_broadcast([128, 8, DK])
                p.op(V, lambda e, s_=s_, tm=tm, scbc=scbc: e.tensor_tensor(
                    tm.rearrange("p (h k) -> p h k", k=DK), scbc,
                    c.vb[:, s_, :].rearrange("p (h k) -> p h k", k=DK), ALU.mult),
                    drain=True)
                p.op(V, lambda e, t_=t_, tm=tm: e.tensor_tensor(
                    c.oac[:, t_, :], c.oac[:, t_, :], tm, ALU.add), drain=True)
            if upto3 == 'o' and sub == DBG_SUB and hh == 0:
                for bi in range(4):
                    p.op(V, lambda e, bi=bi: e.tensor_copy(
                        c.dacc[:, bi, 0:512], c.oac[:, bi, :]), inc=True, drain=True)
                p.barrier((T, V, S))
                return
            for bi in range(4):
                i8 = c.sc[:, 10, :]
                r8 = c.sc[:, 11, :]
                t8 = c.cols[:, 40:48]
                ov = c.oac[:, bi, :]
                o3 = ov.rearrange("p (h k) -> p h k", k=DK)
                p.op(V, lambda e, o3=o3, i8=i8: e.tensor_reduce(i8, o3, AX.X, ALU.add),
                     drain=True, inc=True)
                sq = c.dump[:, 2, 0:512]
                sqo = p.op(S, lambda e, ov=ov, sq=sq: e.activation(sq, ov, AF.Square),
                           inc=True, waits=(p.now(V),))
                p.op(V, lambda e, sq=sq, r8=r8: e.tensor_reduce(
                    r8, sq.rearrange("p (h k) -> p h k", k=DK), AX.X, ALU.add),
                    drain=True, waits=(sqo,))
                p.op(V, lambda e, i8=i8: e.tensor_scalar(i8, i8, 1.0 / DK, None,
                                                         ALU.mult), drain=True)
                p.op(V, lambda e, i8=i8, t8=t8: e.tensor_tensor(t8, i8, i8, ALU.mult),
                     drain=True)
                p.op(V, lambda e, r8=r8, t8=t8: e.scalar_tensor_tensor(
                    r8, r8, 1.0 / DK, t8, ALU.mult, ALU.subtract), drain=True)
                sr = p.op(S, lambda e, r8=r8: e.activation(r8, r8, AF.Sqrt, bias=c.epsc[:]),
                          inc=True, waits=(p.now(V),))
                p.op(V, lambda e, r8=r8: e.reciprocal(r8, r8), drain=True, waits=(sr,))
                p.op(V, lambda e, o3=o3, i8=i8: e.tensor_tensor(
                    o3, o3, i8.unsqueeze(2).to_broadcast([128, 8, DK]), ALU.subtract),
                    drain=True)
                p.op(V, lambda e, o3=o3, r8=r8: e.tensor_tensor(
                    o3, o3, r8.unsqueeze(2).to_broadcast([128, 8, DK]), ALU.mult),
                    drain=True)
                p.op(V, lambda e, bi=bi, ov=ov, h0=h0: e.tensor_tensor(
                    c.og[:, bi, h0:h0 + 512], ov, c.og[:, bi, h0:h0 + 512], ALU.mult),
                    inc=True, drain=True)
            p.barrier((V, S, T))
            if upto3 == 'oln' and sub == DBG_SUB and hh == 0:
                for bi in range(4):
                    p.op(V, lambda e, bi=bi: e.tensor_copy(
                        c.dacc[:, bi, 0:512], c.oac[:, bi, :]), inc=True, drain=True)
                p.barrier((T, V, S))
                return
        if upto3 == 'og2' and sub == DBG_SUB:
            for bi in range(4):
                p.op(V, lambda e, bi=bi: e.tensor_copy(
                    c.dacc[:, bi, :], c.og[:, bi, :]), inc=True, drain=True)
            p.barrier((T, V, S))
            return
        if upto3 == 'hh1' and sub == DBG_SUB:
            p.barrier((T, V, S))
            return
        for bi in range(4):
            for k in range(KT):
                bnk = k % 4
                mm = p.op(T, lambda e, bi=bi, k=k, bnk=bnk: e.transpose(
                    c.bankB(bnk)[:, 0:128], c.og[:, bi, k * 128:(k + 1) * 128],
                    c.identB[:]), inc=True, waits=(p.now(S), p.now(V)))
                p.op(S, lambda e, bi=bi, k=k, bnk=bnk: e.copy(
                    out=c.ogT[:, k, bi * 128:(bi + 1) * 128],
                    in_=c.bankB(bnk)[:, 0:128]), inc=True, waits=(mm,))
        p.barrier((T, S, V))
        slot_consumer = {0: p.now(T), 1: p.now(T)}
        for bi in range(4):
            tt = tiles[bi]
            for k in range(KT):
                slot = k % 2
                wld = p.dma(Y, f"ws{slot}", lambda e, k=k, slot=slot: e.dma_start(
                    out=c.wbufA[:, slot, :], in_=c.woT[k * 128:(k + 1) * 128, :]),
                    waits=(slot_consumer[slot],))

                def mm_wo(e, bi=bi, k=k, slot=slot):
                    e.matmul(c.bank(4), c.ogT[:, k, bi * 128:(bi + 1) * 128],
                             c.wbufA[:, slot, 0:512], start=(k == 0),
                             stop=(k == KT - 1))
                    return e.matmul(c.bank(5), c.ogT[:, k, bi * 128:(bi + 1) * 128],
                                    c.wbufA[:, slot, 512:1024], start=(k == 0),
                                    stop=(k == KT - 1))
                slot_consumer[slot] = p.op(T, mm_wo, inc=True,
                     waits=(wld,) + ((p.now(V), p.now(S)) if k == 0 else ()))
            ev = p.op(V, lambda e, tt=tt: e.tensor_copy(c.dacc[:, tt, 0:512],
                                                        c.bank(4)), drain=True,
                      waits=(p.now(T),))
            p.op(V, lambda e, tt=tt: e.tensor_copy(c.dacc[:, tt, 512:1024],
                                                   c.bank(5)), inc=True, drain=True)
        if upto3 == 'wo1' and sub == DBG_SUB:
            p.barrier((T, V, S))
            return
    p.barrier((T, V, S))
    if upto3 == 'gla':
        return
    for tt in range(TT):
        v = c.dump[:, 1, :]
        vv = p.op(V, lambda e, tt=tt: e.tensor_tensor(v, c.resid[:, tt, :],
                                                      c.dacc[:, tt, :], ALU.add),
                  inc=True, drain=True, waits=(p.now(S),))
        ln_apply(c, v, c.dump[:, 0, :], kind="rms", scale_bc=c.bcast["scale2"][:],
                 vwait=(vv,))
        for k in range(KT):
            bnk = k % 4
            mm = p.op(T, lambda e, k=k, bnk=bnk: e.transpose(
                c.bank(bnk)[:, 0:128], c.dump[:, 0, k * 128:(k + 1) * 128],
                c.identF[:]), inc=True, waits=(p.now(V), p.now(S)))
            p.op(S, lambda e, k=k, bnk=bnk, tt=tt: e.copy(
                out=c.xT[:, k, tt * 128:(tt + 1) * 128], in_=c.bank(bnk)[:, 0:128]),
                inc=True, waits=(mm,))
        p.barrier((T, V, S))
    if upto3 == 'rms2':
        return
    ffn_slab(c, [("g", c.wgateT, c.wupT, c.wdownT, None, DHID // 512)], c.dacc, step=3)
    p.barrier((T, V, S))
    for tt in range(TT):
        v = c.dump[:, 1, :]
        vv = p.op(V, lambda e, tt=tt: e.scalar_tensor_tensor(
            v, c.resid[:, tt, :], 2.0, c.dacc[:, tt, :], ALU.mult, ALU.add),
            inc=True, drain=True, waits=(p.now(S),))
        ln_apply(c, v, c.resid[:, tt, :], kind="ln", vwait=(vv,))
    p.barrier((T, V, S))



def step4(c, out_ap):
    p = c.p
    transpose_act(c, lambda tt: c.resid[:, tt, :], c.xT)
    for tt in range(TT):
        ts = tt * 128

        def mm_r(e, ts=ts):
            ins = None
            for k in range(KT):
                ins = e.matmul(c.bank(6)[:, 0:8], c.xT[:, k, ts:ts + 128],
                               c.router1[:, k, :], start=(k == 0), stop=(k == KT - 1))
            return ins
        mm = p.op(T, mm_r, inc=True, waits=(p.now(S), p.now(V)))
        p.wait(V, mm)
        softmax_topk(c, c.bank(6)[:, 0:E1], E1, 6, c.w8[:, tt, :], False)
    p.barrier((T, V, S))
    p.op(V, lambda e: e.memset(c.dacc[:], 0.0), inc=True, drain=True)
    specs = []
    for ei in range(E1):
        specs.append(("exp", ei, c.f1gate[ei], c.f1up[ei], c.f1down[ei]))
    specs.append(("shr", 0, c.wsgate[:, 0:512], c.wsup[:, 0:512], c.wsdown[0:512, :]))
    specs.append(("shr", 1, c.wsgate[:, 512:1024], c.wsup[:, 512:1024],
                  c.wsdown[512:1024, :]))
    ffn_slab(c, [(kind, wg, wu, wd, ei, 1) for kind, ei, wg, wu, wd in specs],
             c.dacc, step=4)
    p.barrier((T, V, S))
    for tt in range(TT):
        v = c.dump[:, 1, :]
        vv = p.op(V, lambda e, tt=tt: e.tensor_tensor(v, c.resid[:, tt, :],
                                                      c.dacc[:, tt, :], ALU.add),
                  inc=True, drain=True, waits=(p.now(S),))
        lo = ln_apply(c, v, c.dump[:, 0, :], kind="ln", vwait=(vv,))
        p.dma(G, "st", lambda e, tt=tt: e.dma_start(
            out=out_ap[tt * 128:(tt + 1) * 128, :], in_=c.dump[:, 0, :]),
            waits=(lo,))
        p.wait(V, p.dnow("st"))
        p.wait(S, p.dnow("st"))
    p.wait(G, p.dnow("st"))
    p.op(G, lambda e: e.nop())


def ffn_slab(c, units, dacc, step):
    p = c.p
    slab_last_T = p.now(T)
    zS = ("e", S, 0)
    zV = ("e", V, 0)
    zT = ("e", T, 0)
    sgsu_tok = {}
    hv_tok = {}
    cp_tok = {}
    dn_tok = {}
    dacc_tok = {}
    ti = -1
    for kind, wg, wu, wd, ei, nfc in units:
        for fc in range(nfc):
            f0 = fc * 512
            lds = []
            for k in range(KT):
                if step == 3:
                    ga = wg[k * 128:(k + 1) * 128, f0:f0 + 512]
                    ua = wu[k * 128:(k + 1) * 128, f0:f0 + 512]
                else:
                    ga = wg[k * 128:(k + 1) * 128, 0:512]
                    ua = wu[k * 128:(k + 1) * 128, 0:512]
                lds.append(p.dma(Y, "wsl", lambda e, k=k, ga=ga: e.dma_start(
                    out=c.slab_g[:, k, :], in_=ga), waits=(slab_last_T,)))
                lds.append(p.dma(G, "wsl2", lambda e, k=k, ua=ua: e.dma_start(
                    out=c.slab_u[:, k, :], in_=ua), waits=(slab_last_T,)))
            for k4 in range(4):
                if step == 3:
                    da = wd[f0 + k4 * 128:f0 + (k4 + 1) * 128, :]
                else:
                    da = wd[k4 * 128:(k4 + 1) * 128, :]
                lds.append(p.dma(Y, "wsl", lambda e, k4=k4, da=da: e.dma_start(
                    out=c.slab_d[:, k4, :], in_=da), waits=(slab_last_T,)))
            slab_ok = (p.dnow("wsl"), p.dnow("wsl2"))
            for tt in range(TT):
                ti += 1
                ts = tt * 128
                q = ti % 2
                gl = c.u1[:, 2 * q, :]
                ul = c.u1[:, 2 * q + 1, :]

                def mm_gu(e, ts=ts):
                    ins = None
                    for k in range(KT):
                        e.matmul(c.bank(0), c.xT[:, k, ts:ts + 128],
                                 c.slab_g[:, k, :], start=(k == 0), stop=(k == KT - 1))
                        ins = e.matmul(c.bank(1), c.xT[:, k, ts:ts + 128],
                                       c.slab_u[:, k, :], start=(k == 0),
                                       stop=(k == KT - 1))
                    return ins
                gu = p.op(T, mm_gu, inc=True,
                          waits=slab_ok + (sgsu_tok.get(ti - 1, zS),))
                sg = p.op(S, lambda e, gl=gl: e.activation(gl, c.bank(0),
                                                           AF.Silu), inc=True,
                          waits=(gu, hv_tok.get(ti - 2, zV)))
                su = p.op(S, lambda e, ul=ul: e.copy(out=ul, in_=c.bank(1)),
                          inc=True)
                sgsu_tok[ti] = su
                hv = p.op(V, lambda e, gl=gl, ul=ul: e.tensor_tensor(
                    ul, gl, ul, ALU.mult),
                    inc=True, drain=True, waits=(su,))
                if kind == "exp":
                    hv = p.op(V, lambda e, ei=ei, tt=tt, ul=ul: e.tensor_scalar(
                        ul, ul, c.w8[:, tt, ei:ei + 1],
                        None, ALU.mult), inc=True, drain=True)
                hv_tok[ti] = hv
                last_cp = None
                for k4 in range(4):
                    bnk = 2 + k4 % 2
                    mm = p.op(T, lambda e, k4=k4, bnk=bnk, ul=ul: e.transpose(
                        c.bankB(bnk)[:, 0:128], ul[:, k4 * 128:(k4 + 1) * 128],
                        c.identB[:]), inc=True,
                        waits=(hv, cp_tok.get(k4 % 2, zS)))
                    last_cp = p.op(S, lambda e, k4=k4, bnk=bnk: e.copy(
                        out=c.hT[:, k4 * 128:(k4 + 1) * 128],
                        in_=c.bankB(bnk)[:, 0:128]), inc=True,
                        waits=(mm, dn_tok.get(ti - 1, zT)) if k4 == 0 else (mm,))
                    cp_tok[k4 % 2] = last_cp

                def mm_dn(e):
                    ins = None
                    for k4 in range(4):
                        e.matmul(c.bank(4), c.hT[:, k4 * 128:(k4 + 1) * 128],
                                 c.slab_d[:, k4, 0:512], start=(k4 == 0),
                                 stop=(k4 == 3))
                        ins = e.matmul(c.bank(5), c.hT[:, k4 * 128:(k4 + 1) * 128],
                                       c.slab_d[:, k4, 512:1024], start=(k4 == 0),
                                       stop=(k4 == 3))
                    return ins
                dn = p.op(T, mm_dn, inc=True,
                          waits=(last_cp, dacc_tok.get(ti - 1, zV)))
                dn_tok[ti] = dn
                slab_last_T = dn
                p.op(V, lambda e, tt=tt: e.tensor_tensor(
                    dacc[:, tt, 0:512], dacc[:, tt, 0:512], c.bank(4), ALU.add),
                    drain=True, waits=(dn,))
                dacc_tok[ti] = p.op(V, lambda e, tt=tt: e.tensor_tensor(
                    dacc[:, tt, 512:1024], dacc[:, tt, 512:1024], c.bank(5), ALU.add),
                    inc=True, drain=True)



WNAMES = [
    "L0_ffn_base_W", "L0_ffn_base_b", "L0_ffn_shared_A", "L0_ffn_shared_B",
    "L0_ffn_routed_A", "L0_ffn_routed_B", "L0_ffn_router_W", "L0_ffn_router_b",
    "L1_attn_scale1", "L1_attn_scale2", "L1_attn_wq", "L1_attn_wk", "L1_attn_wv",
    "L1_attn_wg1", "L1_attn_bg1", "L1_attn_wg2", "L1_attn_bg2", "L1_attn_wog",
    "L1_attn_bog", "L1_attn_wo", "L1_attn_w_gate", "L1_attn_w_up", "L1_attn_w_down",
    "L1_ffn_router_w", "L1_ffn_w_gate", "L1_ffn_w_up", "L1_ffn_w_down",
    "L1_ffn_ws_gate", "L1_ffn_ws_up", "L1_ffn_ws_down",
]

_WSHAPES = {
    "L0_ffn_base_W": (1024, 1024), "L0_ffn_base_b": (1024,),
    "L0_ffn_shared_A": (16, 1024), "L0_ffn_shared_B": (1024, 16),
    "L0_ffn_routed_A": (6, 16, 1024), "L0_ffn_routed_B": (6, 1024, 16),
    "L0_ffn_router_W": (6, 1024), "L0_ffn_router_b": (6,),
    "L1_attn_scale1": (1024,), "L1_attn_scale2": (1024,),
    "L1_attn_wq": (1024, 1024), "L1_attn_wk": (1024, 1024),
    "L1_attn_wv": (1024, 1024), "L1_attn_wg1": (16, 1024),
    "L1_attn_bg1": (16,), "L1_attn_wg2": (1024, 16), "L1_attn_bg2": (1024,),
    "L1_attn_wog": (1024, 1024), "L1_attn_bog": (1024,),
    "L1_attn_wo": (1024, 1024), "L1_attn_w_gate": (4096, 1024),
    "L1_attn_w_up": (4096, 1024), "L1_attn_w_down": (1024, 4096),
    "L1_ffn_router_w": (1024, 8), "L1_ffn_w_gate": (8, 1024, 512),
    "L1_ffn_w_up": (8, 1024, 512), "L1_ffn_w_down": (8, 512, 1024),
    "L1_ffn_ws_gate": (1024, 1024), "L1_ffn_ws_up": (1024, 1024),
    "L1_ffn_ws_down": (1024, 1024),
}


def _build_fused():
    nc = bass.Bass("TRN2", target_bir_lowering=False)
    tens = {}
    ein = lambda nm, shp, dt=F32: nc.dram_tensor(nm, list(shp), dt,
                                                 kind="ExternalInput").ap()
    tens["x0sched"] = ein("x0sched", [9, 4, 512, 1024])
    for s in "fr":
        tens[f"in_proj_{s}"] = ein(f"in_proj_{s}", [4096, 1024])
        tens[f"out_proj_{s}"] = ein(f"out_proj_{s}", [1024, 2048])
        tens[f"x_proj_{s}"] = ein(f"x_proj_{s}", [96, 2048])
        tens[f"dt_w_{s}"] = ein(f"dt_w_{s}", [2048, 64])
        tens[f"smallsF_{s}"] = ein(f"smallsF_{s}", [128, 368])
    tens["sel16"] = ein("sel16", [16, 2048], BF16)
    tens["masks"] = ein("masks", [128, 35])
    for nm in WNAMES:
        tens[nm] = ein(nm, _WSHAPES[nm])
    tens["x0s"] = ein("x0s", [2048, 1024])
    tens["identF"] = ein("identF", [128, 128])
    tens["identB"] = ein("identB", [128, 128], BF16)
    tens["identJ"] = ein("identJ", [128, 128], BF16)
    out = nc.dram_tensor("out", [2048, 1024], F32, kind="ExternalOutput").ap()

    p = Prog(inc_all=True)
    stack = ExitStack()
    with stack:
        psum = stack.enter_context(nc.psum_tensor("psum", [128, 4096], F32))
        w2 = phase2_casts(nc, p, tens)
        st1 = ExitStack()
        scr = build_mamba_fused(nc, p, st1, tens, psum)
        p.barrier(tuple(ENGINES))
        chans = list(p.dcnt)
        for eng in ENGINES:
            p.wait(eng, *[p.dnow(ch) for ch in chans])
        st1.close()
        c = alloc_common(nc, p, stack, tens, psum, w2)
        step1_fused(c, tens["x0s"], scr)
        step2(c)
        step3(c)
        step4(c, out)
        p.build(nc, stack)
    return nc


def kernel(**inputs):
    inputs = {k: np.asarray(v, np.float32) for k, v in inputs.items()}
    x_full = inputs["L0_attn_hidden_states"]
    flip = x_full[:, ::-1, :]

    nc = _build_fused()

    sel = np.zeros((16, 2048), BF)
    for n in range(16):
        sel[n, n * 128:(n + 1) * 128] = 1
    identF = np.eye(128, dtype=np.float32)
    identB = np.eye(128, dtype=BF)
    identJ = np.eye(128, dtype=np.float32)[::-1].astype(BF)
    smallsF = {}
    for s in "fr":
        A = np.asarray(inputs[f"L0_attn_A_log_{s}"]).reshape(16, 128, 16)
        A = A.transpose(1, 0, 2).reshape(128, 256)
        dtb = inputs[f"L0_attn_dt_b_{s}"].reshape(16, 128).T
        convb = inputs[f"L0_attn_conv_b_{s}"].reshape(16, 128).T
        Dp = inputs[f"L0_attn_D_{s}"].reshape(16, 128).T
        convw = inputs[f"L0_attn_conv_w_{s}"].reshape(2048, 4)
        convw = convw.reshape(16, 128, 4).transpose(1, 0, 2).reshape(128, 64)
        smallsF[s] = np.ascontiguousarray(
            np.concatenate([A, dtb, convb, Dp, convw], axis=1), np.float32)

    in_maps = []
    for cid in range(8):
        keep = np.ones(7, np.float32)
        keep[0] = 0.0
        if cid < 7:
            keep[cid] = 0.0
        sf = np.zeros(7, np.float32)
        if cid >= 1:
            sf[cid - 1] = 1.0
        sr = np.zeros(7, np.float32)
        if cid < 7:
            sr[6] = 1.0
        mrow = np.concatenate([keep, sf, 1.0 - sf, sr, 1.0 - sr])
        masks = np.ascontiguousarray(np.tile(mrow, (128, 1)), np.float32)

        x0sched = np.zeros((9, 4, 512, 1024), np.float32)
        for b in range(4):
            for pos in range(7):
                if pos < cid:
                    x0sched[pos, b] = x_full[b, 512 * pos:512 * (pos + 1)]
                else:
                    k = pos - cid
                    x0sched[pos, b] = flip[b, 512 * k:512 * (k + 1)]
            x0sched[7, b] = x_full[b, 512 * cid:512 * (cid + 1)]
            x0sched[8, b] = flip[b, 512 * (7 - cid):512 * (8 - cid)]
        x0s2 = np.ascontiguousarray(np.concatenate(
            [x_full[b, 512 * cid:512 * (cid + 1)] for b in range(4)], 0))

        im = {nm: inputs[nm] for nm in WNAMES}
        for s in "fr":
            im[f"in_proj_{s}"] = inputs[f"L0_attn_in_proj_{s}"]
            im[f"out_proj_{s}"] = inputs[f"L0_attn_out_proj_{s}"]
            im[f"x_proj_{s}"] = inputs[f"L0_attn_x_proj_{s}"]
            im[f"dt_w_{s}"] = inputs[f"L0_attn_dt_w_{s}"]
            im[f"smallsF_{s}"] = smallsF[s]
        im["x0sched"] = np.ascontiguousarray(x0sched)
        im["sel16"] = sel
        im["masks"] = masks
        im["x0s"] = x0s2
        im["identF"] = identF
        im["identB"] = identB
        im["identJ"] = np.ascontiguousarray(identJ)
        in_maps.append(im)

    res = run_bass_kernel_spmd(nc, in_maps, core_ids=list(range(8)))

    out = np.zeros((4, 4096, 1024), np.float32)
    for cid in range(8):
        o = np.asarray(res.results[cid]["out"]).reshape(4, 512, 1024)
        out[:, 512 * cid:512 * (cid + 1), :] = o
    return out
